# version 11
# baseline (speedup 1.0000x reference)
import numpy as np
import concourse.bass as bass
import concourse.mybir as mybir
from concourse.bass_utils import run_bass_kernel_spmd
from contextlib import ExitStack

F32 = mybir.dt.float32
BF16 = mybir.dt.bfloat16
I32 = mybir.dt.int32
AF = mybir.ActivationFunctionType
ALU = mybir.AluOpType
AX = mybir.AxisListType

NCORES = 8
B_LOC = 4
S = 2048
D = 1024
H = 4096
T = B_LOC * S
NT = T // 128
DC = D // 128
HC = H // 128
C1 = 12582912.0
NEG = -1.0e30
R127 = float(np.float32(1.0) / np.float32(127.0))

SCRATCH = {
    "xnT_s":  ([D, T], BF16),
    "xqT_s":  ([D, T], BF16),
    "qT_s":   ([D, T], BF16),
    "kT_s":   ([D, T], BF16),
    "v_s":    ([T, D], BF16),
    "rsc_s":  ([T, 1], F32),
    "hq_s":   ([H, T], BF16),
    "g_s":    ([T, D], F32),
    "x1_s":   ([T, D], F32),
    "xn1_s":  ([T, D], BF16),
    "xn1T_s": ([D, T], BF16),
    "q1T_s":  ([D, T], BF16),
    "k1T_s":  ([D, T], BF16),
    "v1_s":   ([T, D], BF16),
    "y_s":    ([T, D], BF16),
    "gc_s":   ([32, D], F32),
    "w1qT_s": ([D, H], BF16),
    "w2qT_s": ([H, D], BF16),
    "weT_s":  ([D, D], BF16),
}

REAL_IN = {
    "x":      ([T, D], F32),
    "WqT":    ([D, D], F32),
    "WkT":    ([D, D], F32),
    "WvT":    ([D, D], F32),
    "bq_c":   ([128, DC], F32),
    "bk_c":   ([128, DC], F32),
    "bvRep":  ([128, D], F32),
    "w1T":    ([D, H], F32),
    "w2T":    ([H, D], F32),
    "convw":  ([128, HC * 3], F32),
    "convb":  ([128, HC], F32),
    "Wql":    ([D, D], F32),
    "Wkl":    ([D, D], F32),
    "Wvl":    ([D, D], F32),
    "Wol":    ([D, D], F32),
    "Wqg":    ([D, D], F32),
    "Wkg":    ([D, D], F32),
    "Wvg":    ([D, D], F32),
    "Wog":    ([D, D], F32),
    "weightT": ([D, D], F32),
    "alpha":  ([1, 1], F32),
    "bias1Rep": ([128, D], F32),
    "iden":   ([128, 128], F32),
    "cmask":  ([128, 128], F32),
    "gmask":  ([32, 32], F32),
    "onesr":  ([1, 128], F32),
    "rowsel": ([128, 32 * 32], F32),
}
REAL_OUT = {"y_out": ([T, D], F32)}

PHASE_IO = {
    "P0":  (set(), {"w1qT_s", "w2qT_s", "weT_s"}),
    "PA":  (set(), {"xnT_s", "xqT_s"}),
    "PC":  ({"xnT_s"}, {"qT_s", "kT_s", "v_s"}),
    "PB1": ({"xqT_s", "w1qT_s"}, {"hq_s", "rsc_s"}),
    "PB2": ({"hq_s", "rsc_s", "w2qT_s"}, {"g_s"}),
    "PD":  ({"qT_s", "kT_s", "v_s", "g_s"}, {"x1_s", "xn1_s", "xn1T_s"}),
    "PE1": ({"xn1T_s"}, {"q1T_s", "k1T_s", "v1_s"}),
    "PF1": ({"q1T_s", "k1T_s", "v1_s", "xn1_s"}, {"y_s", "gc_s"}),
    "PG1": ({"xn1T_s", "y_s", "x1_s", "gc_s", "weT_s"}, set()),
}
PHASE_ORDER = ["P0", "PA", "PC", "PB1", "PB2", "PD", "PE1", "PF1", "PG1"]


class K:

    PERSIST = ("cld", "cldg", "cinit", "pc_wl", "pe_wl", "pf_wl")

    def __init__(self, nc, dram):
        self.nc = nc
        self.dram = dram
        self.es = ExitStack()
        self._semh = {}
        self._base = {}
        self.cnt = {}
        self._phys = {}
        self._pool = []
        self._nalloc = 0
        self.g = {}
        self.store_sems = set()
        self.entry = {}

    def phase_begin(self):
        for name in list(self._semh):
            if name in self.store_sems or name in self.PERSIST:
                continue
            h = self._semh.pop(name)
            self._pool.append(h)
            self._base.pop(name, None)

    def sem(self, name):
        if name not in self._semh:
            if name in self.store_sems or name in self.PERSIST or not self._pool:
                h = self.es.enter_context(self.nc.semaphore(
                    f"s{self._nalloc}_{name}"))
                self._nalloc += 1
                self._phys[id(h)] = 0
            else:
                h = self._pool.pop()
            self._semh[name] = h
            self._base[name] = self._phys[id(h)]
            self.cnt[name] = 0
        return self._semh[name]

    def wg(self, eng, name, val):
        h = self.sem(name)
        eng.wait_ge(h, self._base[name] + val)

    def inc(self, instr, name, n=1):
        h = self.sem(name)
        instr.then_inc(h, n)
        self.cnt[name] += n
        self._phys[id(h)] += n
        return self.cnt[name]

    def inc2(self, instr, name, n=1):
        self.inc(instr, name, n)
        return instr

    def stinc(self, instr, name):
        if name not in self.store_sems:
            assert name not in self._semh, f"store sem {name} leased late"
            self.store_sems.add(name)
        return self.inc(instr, name, 16)

    def c(self, name):
        self.sem(name)
        return self.cnt[name]

    def declare_stores(self, *names):
        for nm in names:
            for par in (0, 1):
                n2 = f"{nm}{par}"
                self.store_sems.add(n2)
                self.sem(n2)



def _host_prep(inp):
    f = np.float32
    A = np.ascontiguousarray
    sh = {}
    sh["WqT"] = A(np.asarray(inp["L0_attn_Wq"], f).T)
    sh["WkT"] = A(np.asarray(inp["L0_attn_Wk"], f).T)
    sh["WvT"] = A(np.asarray(inp["L0_attn_Wv"], f).T)
    sh["bq_c"] = A(np.asarray(inp["L0_attn_bq"], f).reshape(DC, 128).T)
    sh["bk_c"] = A(np.asarray(inp["L0_attn_bk"], f).reshape(DC, 128).T)
    sh["bvRep"] = A(np.broadcast_to(np.asarray(inp["L0_attn_bv"], f)[None, :], (128, D)))
    sh["w1T"] = A(np.asarray(inp["L0_ffn_w1"], f).T)
    sh["w2T"] = A(np.asarray(inp["L0_ffn_w2"], f).T)
    cw = np.asarray(inp["L0_ffn_conv_w"], f).reshape(HC, 128, 3)
    sh["convw"] = A(cw.transpose(1, 0, 2).reshape(128, HC * 3))
    sh["convb"] = A(np.asarray(inp["L0_ffn_conv_b"], f).reshape(HC, 128).T)
    for nm, key in (("Wql", "L1_attn_Wq_l"), ("Wkl", "L1_attn_Wk_l"),
                    ("Wvl", "L1_attn_Wv_l"), ("Wol", "L1_attn_Wo_l"),
                    ("Wqg", "L1_attn_Wq_g"), ("Wkg", "L1_attn_Wk_g"),
                    ("Wvg", "L1_attn_Wv_g"), ("Wog", "L1_attn_Wo_g")):
        sh[nm] = A(np.asarray(inp[key], f))
    sh["weightT"] = A(np.asarray(inp["L1_ffn_weight"], f).T)
    sh["alpha"] = np.asarray(inp["L1_ffn_alpha"], f).reshape(1, 1)
    sh["bias1Rep"] = A(np.broadcast_to(np.asarray(inp["L1_ffn_bias"], f)[None, :], (128, D)))
    sh["iden"] = np.eye(128, dtype=f)
    sh["cmask"] = np.triu(np.full((128, 128), NEG, f), 1)
    gm = np.full((32, 32), NEG, f)
    for b in range(4):
        gm[b * 8:(b + 1) * 8, b * 8:(b + 1) * 8] = 0.0
    sh["gmask"] = gm
    sh["onesr"] = np.ones((1, 128), f)
    rs = np.zeros((128, 32 * 32), f)
    for r in range(32):
        rs[:, r * 32 + r] = 1.0
    sh["rowsel"] = A(rs)

    x_full = np.asarray(inp["L0_attn_x"], f)
    maps = []
    for c in range(NCORES):
        m = dict(sh)
        m["x"] = A(x_full[c * B_LOC:(c + 1) * B_LOC].reshape(T, D))
        maps.append(m)
    return maps



def emit_PINIT(k):
    nc, d, g = k.nc, k.dram, k.g
    mk = lambda n, s, t: k.es.enter_context(nc.sbuf_tensor(n, s, t))
    g["iden32"] = mk("g_iden32", [128, 128], F32)
    g["idenb"] = mk("g_idenb", [128, 128], BF16)
    g["cmaskb"] = mk("g_cmaskb", [128, 128], BF16)
    g["gmaskb"] = mk("g_gmaskb", [32, 32], BF16)
    g["onesr32"] = mk("g_onesr32", [1, 128], F32)
    g["onesrb"] = mk("g_onesrb", [1, 128], BF16)
    g["onescol"] = mk("g_onescol", [128, 1], F32)
    g["onescolb"] = mk("g_onescolb", [128, 1], BF16)
    with nc.Block() as blk:
        @blk.sync
        def _(sync):
            k.inc(sync.dma_start(out=g["iden32"][:], in_=d["iden"][:]), "cld", 16)
            k.inc(sync.dma_start(out=g["onesr32"][:], in_=d["onesr"][:]), "cld", 16)

        @blk.gpsimd
        def _(gp):
            k.inc(gp.dma_start(out=g["idenb"][:], in_=d["iden"][:]), "cldg", 16)
            k.inc(gp.dma_start(out=g["cmaskb"][:], in_=d["cmask"][:]), "cldg", 16)
            k.inc(gp.dma_start(out=g["gmaskb"][:], in_=d["gmask"][:]), "cldg", 16)
            k.inc(gp.dma_start(out=g["onesrb"][:], in_=d["onesr"][:]), "cldg", 16)

        @blk.vector
        def _(v):
            v.memset(g["onescol"][:], 1.0)
            k.inc(v.memset(g["onescolb"][:], 1.0), "cinit", 1)


def ldw(k, eng, name, j, n=16):
    k.wg(eng, f"{name}{j % 2}", n * (j // 2 + 1))


def ldi(k, eng, name, j, out, in_, n=16):
    return k.inc(eng.dma_start(out=out, in_=in_), f"{name}{j % 2}", n)


def sti(k, eng, name, j, out, in_):
    for par in (0, 1):
        nm = f"{name}{par}"
        if nm not in k.store_sems:
            k.store_sems.add(nm)
        k.sem(nm)
    return k.inc(eng.dma_start(out=out, in_=in_), f"{name}{j % 2}", 16)


def stw(k, eng, name, j, n=16):
    if j >= 2:
        k.wg(eng, f"{name}{j % 2}", n * (j // 2))


def _entry(k, eng):
    for nm, v in sorted(k.entry.items()):
        if v:
            eng.wait_ge(k.sem(nm), v)



def emit_P0(k):
    nc, d, g = k.nc, k.dram, k.g
    k.declare_stores("p0_st")
    with ExitStack() as es:
        sb = lambda n, s, t: es.enter_context(nc.sbuf_tensor(n, s, t))
        wb = [sb("p0_w0", [128, H], F32), sb("p0_w1", [128, H], F32)]
        wq_sb = [sb("p0_wq0", [128, H], BF16), sb("p0_wq1", [128, H], BF16)]
        t1 = sb("p0_t1", [128, H], F32)
        t2 = sb("p0_t2", [128, H], F32)
        ki = sb("p0_ki", [128, H], I32)
        db = sb("p0_db", [128, H], BF16)
        ai = sb("p0_ai", [128, H], I32)
        col = sb("p0_col", [128, 2], F32)
        st1 = sb("p0_st1", [1, 4], F32)
        bcol = sb("p0_bcol", [128, 4], F32)
        ps = es.enter_context(nc.psum_tensor("p0_ps", [128, 512], F32))
        blk = es.enter_context(nc.Block())

        cfgs = [("w1T", "w1qT_s", DC, H, "bit"),
                ("w2T", "w2qT_s", HC, D, "bit"),
                ("weightT", "weT_s", DC, D, "tern")]
        ld_before = []
        tot = 0
        for _w, _o, nch, _c, _kd in cfgs:
            ld_before.append(tot)
            tot += 2 * nch
        q_before = [0, DC, DC + HC]

        @blk.sync
        def _(sync):
            _entry(k, sync)
            k.inc(sync.dma_start(out=st1[0:1, 3:4], in_=d["alpha"][:]), "p0a", 16)
            for idx, (wname, _o, nch, colw, _kd) in enumerate(cfgs):
                for ps_ in range(2):
                    for c in range(nch):
                        gi = ld_before[idx] + ps_ * nch + c
                        if gi >= 2:
                            k.wg(sync, "p0r", gi - 1)
                        ldi(k, sync, "p0ld", gi, wb[gi % 2][:, 0:colw],
                            d[wname][c * 128:(c + 1) * 128, :])

        @blk.vector
        def _(v):
            _entry(k, v)
            for idx, (wname, _o, nch, colw, kind) in enumerate(cfgs):
                for c in range(nch):
                    gi = ld_before[idx] + c
                    ldw(k, v, "p0ld", gi)
                    if c == 0:
                        e = k.inc(v.tensor_reduce(col[:, 0:1], wb[gi % 2][:, 0:colw],
                                                  AX.X, ALU.add,
                                                  apply_absolute_value=True), "p0r")
                    else:
                        e = k.inc(v.tensor_reduce(col[:, 1:2], wb[gi % 2][:, 0:colw],
                                                  AX.X, ALU.add,
                                                  apply_absolute_value=True), "p0r")
                        k.wg(v, "p0r", e)
                        e = k.inc(v.tensor_tensor(col[:, 0:1], col[:, 0:1],
                                                  col[:, 1:2], ALU.add), "p0e")
                        k.wg(v, "p0e", e)
                k.inc(v.tensor_copy(col[:, 1:2], col[:, 0:1]), "p0v")
                k.wg(v, "p0m", 2 * idx + 1)
                e = k.inc(v.tensor_scalar(st1[0:1, 0:1], ps[0:1, 0:1],
                                          1.0 / (nch * 128 * colw), None,
                                          ALU.mult), "p0e")
                k.wg(v, "p0e", e)
                if kind == "bit":
                    e = k.inc(v.tensor_scalar(st1[0:1, 1:2], st1[0:1, 0:1],
                                              1e-5, None, ALU.max), "p0e")
                    k.wg(v, "p0e", e)
                    e = k.inc(v.reciprocal(st1[0:1, 2:3], st1[0:1, 1:2]), "p0e")
                    k.wg(v, "p0e", e)
                    e = k.inc(v.tensor_scalar(st1[0:1, 2:3], st1[0:1, 2:3],
                                              -1.0, None, ALU.mult), "p0e")
                    k.wg(v, "p0e", e)
                else:
                    k.wg(v, "p0a", 16)
                    e = k.inc(v.tensor_scalar(st1[0:1, 1:2], st1[0:1, 0:1],
                                              0.7, None, ALU.mult), "p0e")
                    k.wg(v, "p0e", e)
                k.inc(v.tensor_copy(st1[0:1, 0:1], st1[0:1, 0:1]), "p0v")
                k.wg(v, "p0m", 2 * idx + 2)
                if kind == "bit":
                    e = k.inc(v.tensor_copy(bcol[:, 0:3], ps[:, 0:3]), "p0e")
                else:
                    e = k.inc(v.tensor_copy(bcol[:, 1:4], ps[:, 1:4]), "p0e")
                k.wg(v, "p0e", e)
                if kind == "tern":
                    e = k.inc(v.tensor_scalar(bcol[:, 2:3], bcol[:, 2:3], -1.0,
                                              None, ALU.mult), "p0e")
                    k.wg(v, "p0e", e)
                for c in range(nch):
                    gi = ld_before[idx] + nch + c
                    ldw(k, v, "p0ld", gi)
                    gq = q_before[idx] + c
                    stw(k, v, "p0_st", gq)
                    wc = wb[gi % 2][:, 0:colw]
                    oc = wq_sb[gq % 2][:, 0:colw]
                    cw = colw
                    if k.cnt.get("p0q", 0):
                        k.wg(v, "p0q", k.cnt["p0q"])
                    def _e(ins):
                        ee = k.inc(ins, "p0e")
                        k.wg(v, "p0e", ee)
                        return ins

                    if kind == "bit":
                        k.inc(v.tensor_scalar(t1[:, 0:cw], wc, bcol[:, 2:3],
                                              None, ALU.mult), "p0r")
                        k.wg(v, "p0r", k.cnt["p0r"])
                        _e(v.tensor_scalar(t2[:, 0:cw], t1[:, 0:cw], C1, -C1,
                                           ALU.add, ALU.add))
                        _e(v.tensor_copy(ki[:, 0:cw], t2[:, 0:cw]))
                        _e(v.scalar_tensor_tensor(db[:, 0:cw], ki[:, 0:cw],
                                                  -1.0, t2[:, 0:cw], ALU.mult,
                                                  ALU.add))
                        _e(v.scalar_tensor_tensor(ai[:, 0:cw], db[:, 0:cw], 0.5,
                                                  ki[:, 0:cw], ALU.is_gt,
                                                  ALU.add))
                        _e(v.scalar_tensor_tensor(t1[:, 0:cw], db[:, 0:cw],
                                                  -0.5, ai[:, 0:cw], ALU.is_lt,
                                                  ALU.subtract))
                        _e(v.tensor_scalar(t2[:, 0:cw], t1[:, 0:cw], -1.0, 1.0,
                                           ALU.max, ALU.min))
                        ins = v.tensor_scalar(oc, t2[:, 0:cw], bcol[:, 1:2],
                                              None, ALU.mult)
                    else:
                        v.tensor_scalar(t1[:, 0:cw], wc, bcol[:, 1:2], None, ALU.is_gt)
                        k.inc(v.tensor_scalar(t2[:, 0:cw], wc, bcol[:, 2:3],
                                              None, ALU.is_lt), "p0r")
                        k.wg(v, "p0r", k.cnt["p0r"])
                        _e(v.tensor_tensor(t1[:, 0:cw], t1[:, 0:cw], t2[:, 0:cw],
                                           ALU.subtract))
                        ins = v.tensor_scalar(oc, t1[:, 0:cw], bcol[:, 3:4],
                                              None, ALU.mult)
                    k.inc(ins, "p0q")

        @blk.tensor
        def _(te):
            _entry(k, te)
            for idx, (wname, _o, nch, colw, kind) in enumerate(cfgs):
                k.wg(te, "p0v", 2 * idx + 1)
                k.inc(te.matmul(ps[0:1, 0:1], g["onescol"][:, 0:1], col[:, 0:1],
                                start=True, stop=True), "p0m")
                k.wg(te, "p0v", 2 * idx + 2)
                if kind == "bit":
                    te.matmul(ps[:, 0:1], g["onesr32"][0:1, :], st1[0:1, 0:1],
                              start=True, stop=True)
                    te.matmul(ps[:, 1:2], g["onesr32"][0:1, :], st1[0:1, 1:2],
                              start=True, stop=True)
                    k.inc(te.matmul(ps[:, 2:3], g["onesr32"][0:1, :],
                                    st1[0:1, 2:3], start=True, stop=True), "p0m")
                else:
                    te.matmul(ps[:, 1:2], g["onesr32"][0:1, :], st1[0:1, 1:2],
                              start=True, stop=True)
                    te.matmul(ps[:, 2:3], g["onesr32"][0:1, :], st1[0:1, 1:2],
                              start=True, stop=True)
                    k.inc(te.matmul(ps[:, 3:4], g["onesr32"][0:1, :],
                                    st1[0:1, 3:4], start=True, stop=True), "p0m")

        @blk.gpsimd
        def _(gp):
            _entry(k, gp)
            nq = 0
            for idx, (wname, oname, nch, colw, kind) in enumerate(cfgs):
                for c in range(nch):
                    nq += 1
                    k.wg(gp, "p0q", nq)
                    gq = q_before[idx] + c
                    sti(k, gp, "p0_st", gq, d[oname][c * 128:(c + 1) * 128, :],
                        wq_sb[gq % 2][:, 0:colw])



def emit_PA(k):
    nc, d, g = k.nc, k.dram, k.g
    k.declare_stores("pa_st")
    with ExitStack() as es:
        sb = lambda n, s, t: es.enter_context(nc.sbuf_tensor(n, s, t))
        xb = [sb("pa_x0", [128, D], F32), sb("pa_x1", [128, D], F32)]
        xn = [sb("pa_xn0", [128, D], F32), sb("pa_xn1", [128, D], F32)]
        sq = sb("pa_sq", [128, D], F32)
        stt = [sb("pa_st0", [128, 16], F32), sb("pa_st1", [128, 16], F32)]
        qt = sb("pa_qt", [128, D], F32)
        r1 = sb("pa_r1", [128, D], F32)
        k0 = sb("pa_k0", [128, D], I32)
        dd = sb("pa_dd", [128, D], BF16)
        aa = sb("pa_aa", [128, D], I32)
        fin = sb("pa_fin", [128, D], BF16)
        xq = [sb("pa_xq0", [128, D], BF16), sb("pa_xq1", [128, D], BF16)]
        xnT = [sb("pa_xnT0", [128, D], BF16), sb("pa_xnT1", [128, D], BF16)]
        xqT = [sb("pa_xqT0", [128, D], BF16), sb("pa_xqT1", [128, D], BF16)]
        psA = [es.enter_context(nc.psum_tensor("pa_psA0", [128, D], F32)),
               es.enter_context(nc.psum_tensor("pa_psA1", [128, D], F32))]
        psB = [es.enter_context(nc.psum_tensor("pa_psB0", [128, D], BF16)),
               es.enter_context(nc.psum_tensor("pa_psB1", [128, D], BF16))]
        blk = es.enter_context(nc.Block())
        xnT_d = d["xnT_s"].rearrange("(c p) t -> p c t", p=128)
        xqT_d = d["xqT_s"].rearrange("(c p) t -> p c t", p=128)

        @blk.sync
        def _(sync):
            _entry(k, sync)
            for i in range(NT):
                if i >= 2:
                    k.wg(sync, "pa_xn", i - 1)
                ldi(k, sync, "pa_ld", i, xb[i % 2][:],
                    d["x"][i * 128:(i + 1) * 128, :])

        @blk.vector
        def _(v):
            _entry(k, v)
            for i in range(NT):
                p = i % 2
                st_ = stt[p]
                ldw(k, v, "pa_ld", i)
                if i >= 2:
                    k.wg(v, "pa_xn", i - 1)
                e = k.inc(v.tensor_reduce(st_[:, 0:1], xb[p][:], AX.X, ALU.add),
                          "pa_ln")
                k.wg(v, "pa_ln", e)
                e = k.inc(v.tensor_scalar(st_[:, 2:3], st_[:, 0:1], 1.0 / D,
                                          None, ALU.mult), "pa_ln")
                k.wg(v, "pa_ln", e)
                e = k.inc(v.tensor_tensor(st_[:, 3:4], st_[:, 2:3], st_[:, 2:3],
                                          ALU.mult), "pa_ln")
                k.wg(v, "pa_sq", 2 * i + 1)
                k.wg(v, "pa_ln", e)
                e = k.inc(v.scalar_tensor_tensor(st_[:, 4:5], st_[:, 1:2],
                                                 1.0 / D, st_[:, 3:4], ALU.mult,
                                                 ALU.subtract), "pa_ln")
                k.wg(v, "pa_ln", e)
                e = k.inc(v.tensor_scalar(st_[:, 5:6], st_[:, 4:5], 1e-5, None,
                                          ALU.add), "pa_ln")
                k.wg(v, "pa_sq", 2 * i + 2)
                e = k.inc(v.reciprocal(st_[:, 7:8], st_[:, 6:7]), "pa_ln")
                k.wg(v, "pa_ln", e)
                e = k.inc(v.scalar_tensor_tensor(st_[:, 8:9], st_[:, 2:3], -1.0,
                                                 st_[:, 7:8], ALU.mult,
                                                 ALU.mult), "pa_ln")
                k.wg(v, "pa_xn", i + 1)
                e = k.inc(v.tensor_reduce(st_[:, 9:10], xn[p][:], AX.X, ALU.max,
                                          apply_absolute_value=True), "pa_ch2")
                k.wg(v, "pa_ch2", e)
                e = k.inc(v.tensor_scalar(st_[:, 10:11], st_[:, 9:10], 1e-5,
                                          None, ALU.max), "pa_ch2")
                k.wg(v, "pa_ch2", e)
                e = k.inc(v.reciprocal(st_[:, 13:14], st_[:, 10:11]), "pa_ch2")
                k.wg(v, "pa_ch2", e)
                e = k.inc(v.tensor_scalar(st_[:, 11:12], st_[:, 13:14], -127.0,
                                          None, ALU.mult), "pa_ch2")
                k.wg(v, "pa_ch2", e)
                e = k.inc(v.tensor_scalar(st_[:, 12:13], st_[:, 10:11], R127,
                                          None, ALU.mult), "pa_ch2")
                k.wg(v, "pa_ch2", e)
                e = k.inc(v.tensor_scalar(qt[:], xn[p][:], st_[:, 11:12],
                                          None, ALU.mult), "pa_ve")
                k.wg(v, "pa_ve", e)
                e = k.inc(v.tensor_scalar(r1[:], qt[:], C1, -C1, ALU.add,
                                          ALU.add), "pa_ve")
                k.wg(v, "pa_ve", e)
                e = k.inc(v.tensor_copy(k0[:], r1[:]), "pa_ve")
                k.wg(v, "pa_ve", e)
                e = k.inc(v.scalar_tensor_tensor(dd[:], k0[:], -1.0, r1[:],
                                                 ALU.mult, ALU.add), "pa_ve")
                k.wg(v, "pa_ve", e)
                e = k.inc(v.scalar_tensor_tensor(aa[:], dd[:], 0.5, k0[:],
                                                 ALU.is_gt, ALU.add), "pa_ve")
                k.wg(v, "pa_ve", e)
                e = k.inc(v.scalar_tensor_tensor(fin[:], dd[:], -0.5, aa[:],
                                                 ALU.is_lt, ALU.subtract),
                          "pa_ve")
                k.wg(v, "pa_ve", e)
                if i >= 2:
                    k.wg(v, "pa_mm", 2 * i - 2)
                k.inc(v.tensor_scalar(xq[p][:], fin[:], st_[:, 12:13], None,
                                      ALU.mult), "pa_ch")

        @blk.scalar
        def _(sc):
            _entry(k, sc)

            def evac(j):
                q = j % 2
                k.wg(sc, "pa_mm", 2 * j + 1)
                stw(k, sc, "pa_st", j, n=32)
                k.inc(sc.activation(xnT[q][:], psA[q][:], AF.Identity), "pa_ev")
                k.wg(sc, "pa_mm", 2 * j + 2)
                k.inc(sc.activation(xqT[q][:], psB[q][:], AF.Identity), "pa_ev")

            for i in range(NT):
                p = i % 2
                st_ = stt[p]
                ldw(k, sc, "pa_ld", i)
                if i >= 2:
                    k.wg(sc, "pa_xn", i - 1)
                k.inc(sc.activation(sq[:], xb[p][:], AF.Square,
                                    accum_out=st_[:, 1:2]), "pa_sq")
                k.wg(sc, "pa_ln", 7 * i + 5)
                k.inc(sc.activation(st_[:, 6:7], st_[:, 5:6], AF.Sqrt), "pa_sq")
                k.wg(sc, "pa_ln", 7 * (i + 1))
                if i >= 2:
                    k.wg(sc, "pa_mm", 2 * i - 2)
                k.inc(sc.activation(xn[p][:], xb[p][:], AF.Identity,
                                    bias=st_[:, 8:9], scale=st_[:, 7:8]),
                      "pa_xn")
                if i >= 1:
                    evac(i - 1)
            evac(NT - 1)

        @blk.tensor
        def _(te):
            _entry(k, te)
            for i in range(NT):
                p = i % 2
                k.wg(te, "pa_xn", i + 1)
                if i >= 2:
                    k.wg(te, "pa_ev", 2 * i - 2)
                for c in range(DC):
                    ins = te.transpose(psA[p][:, c * 128:(c + 1) * 128],
                                       xn[p][:, c * 128:(c + 1) * 128],
                                       g["iden32"][:])
                k.inc(ins, "pa_mm")
                k.wg(te, "pa_ch", i + 1)
                for c in range(DC):
                    ins = te.transpose(psB[p][:, c * 128:(c + 1) * 128],
                                       xq[p][:, c * 128:(c + 1) * 128],
                                       g["idenb"][:])
                k.inc(ins, "pa_mm")

        @blk.gpsimd
        def _(gp):
            _entry(k, gp)
            for i in range(NT):
                p = i % 2
                k.wg(gp, "pa_ev", 2 * i + 1)
                sti(k, gp, "pa_st", i, xnT_d[:, :, i * 128:(i + 1) * 128],
                    xnT[p].ap().rearrange("p (c t) -> p c t", c=DC))
                k.wg(gp, "pa_ev", 2 * i + 2)
                sti(k, gp, "pa_st", i, xqT_d[:, :, i * 128:(i + 1) * 128],
                    xqT[p].ap().rearrange("p (c t) -> p c t", c=DC))



def emit_PC(k):
    import os
    _parts = os.environ.get("PC_PARTS", "qkv")
    nc, d, g = k.nc, k.dram, k.g
    k.declare_stores("pc_st")
    NS = T // 512
    with ExitStack() as es:
        sb = lambda n, s, t: es.enter_context(nc.sbuf_tensor(n, s, t))
        wq = sb("pc_wq", [128, DC * D], BF16)
        wk = sb("pc_wk", [128, DC * D], BF16)
        wv = sb("pc_wv", [128, DC * D], BF16)
        bq = sb("pc_bq", [128, DC], F32)
        bk = sb("pc_bk", [128, DC], F32)
        bv = sb("pc_bv", [128, D], F32)
        xs = [sb("pc_x0", [128, DC * 512], BF16), sb("pc_x1", [128, DC * 512], BF16)]
        qs = [sb("pc_q0", [128, DC * 512], BF16), sb("pc_q1", [128, DC * 512], BF16)]
        ks_ = [sb("pc_k0", [128, DC * 512], BF16), sb("pc_k1", [128, DC * 512], BF16)]
        vs = [sb("pc_v0", [128, 4 * D], BF16), sb("pc_v1", [128, 4 * D], BF16)]
        psq = es.enter_context(nc.psum_tensor("pc_psq", [128, 1024], F32))
        psk = es.enter_context(nc.psum_tensor("pc_psk", [128, 1024], F32))
        psv = es.enter_context(nc.psum_tensor("pc_psv", [128, 2048], F32))
        blk = es.enter_context(nc.Block())
        xnT_d = d["xnT_s"].rearrange("(c p) t -> p c t", p=128)
        qT_d = d["qT_s"].rearrange("(c p) t -> p c t", p=128)
        kT_d = d["kT_s"].rearrange("(c p) t -> p c t", p=128)

        @blk.sync
        def _(sync):
            _entry(k, sync)
            k.inc(sync.dma_start(out=bq[:], in_=d["bq_c"][:]), "pc_wlb", 16)
            k.inc(sync.dma_start(out=bk[:], in_=d["bk_c"][:]), "pc_wlb", 16)
            k.inc(sync.dma_start(out=bv[:], in_=d["bvRep"][:]), "pc_wlb", 16)
            for s in range(NS):
                if s >= 2 and "v" in _parts:
                    k.wg(sync, "pc_mv", 4 * (s - 1))
                elif s >= 2:
                    k.wg(sync, "pc_mq", DC * (s - 1))
                ldi(k, sync, "pc_ld", s,
                    xs[s % 2].ap().rearrange("p (c t) -> p c t", c=DC),
                    xnT_d[:, :, s * 512:(s + 1) * 512])

        @blk.gpsimd
        def _(gp):
            _entry(k, gp)
            for wsb, wnm in ((wq, "WqT"), (wk, "WkT"), (wv, "WvT")):
                k.inc(gp.dma_start(
                    out=wsb.ap().rearrange("p (c e) -> p c e", c=DC),
                    in_=d[wnm].rearrange("(c p) e -> p c e", p=128)), "pc_wl", 16)
            for s in range(NS):
                p = s % 2
                if "q" in _parts:
                    k.wg(gp, "pc_eq", DC * (s + 1))
                    sti(k, gp, "pc_st", s, qT_d[:, :, s * 512:(s + 1) * 512],
                        qs[p].ap().rearrange("p (c t) -> p c t", c=DC))
                if "k" in _parts:
                    k.wg(gp, "pc_ek", DC * (s + 1))
                    sti(k, gp, "pc_st", s, kT_d[:, :, s * 512:(s + 1) * 512],
                        ks_[p].ap().rearrange("p (c t) -> p c t", c=DC))
                if "v" in _parts:
                    k.wg(gp, "pc_ev", 4 * (s + 1))
                    sti(k, gp, "pc_st", s,
                        d["v_s"][s * 512:(s + 1) * 512, :].rearrange(
                            "(tt p) e -> p tt e", p=128),
                        vs[p].ap().rearrange("p (tt e) -> p tt e", tt=4))

        @blk.tensor
        def _(te):
            _entry(k, te)
            k.wg(te, "pc_wl", 16 * 3)
            k.wg(te, "pc_wlb", 16 * 3)
            for s in range(NS):
                p = s % 2
                ldw(k, te, "pc_ld", s)
                for wsb, pst, msem, esem in ((wq, psq, "pc_mq", "pc_eq"),
                                             (wk, psk, "pc_mk", "pc_ek"))[
                                                 :("q" in _parts) + ("k" in _parts)]:
                    for ec in range(DC):
                        gq = s * DC + ec
                        bank = ec % 2
                        if gq >= 2:
                            k.wg(te, esem, gq - 1)
                        for dc2 in range(DC):
                            ins = te.matmul(pst[:, bank * 512:(bank + 1) * 512],
                                            wsb[:, dc2 * D + ec * 128:
                                                dc2 * D + (ec + 1) * 128],
                                            xs[p][:, dc2 * 512:(dc2 + 1) * 512],
                                            start=(dc2 == 0),
                                            stop=(dc2 == DC - 1))
                        k.inc(ins, msem)
                for tt in range(4 if "v" in _parts else 0):
                    gv = 4 * s + tt
                    bank = tt % 2
                    if gv >= 2:
                        k.wg(te, "pc_ev", gv - 1)
                    for dc2 in range(DC):
                        for half in range(2):
                            ins = te.matmul(
                                psv[:, bank * D + half * 512:
                                    bank * D + (half + 1) * 512],
                                xs[p][:, dc2 * 512 + tt * 128:
                                      dc2 * 512 + (tt + 1) * 128],
                                wv[:, dc2 * D + half * 512:
                                   dc2 * D + (half + 1) * 512],
                                start=(dc2 == 0), stop=(dc2 == DC - 1))
                    k.inc(ins, "pc_mv")

        @blk.scalar
        def _(sc):
            _entry(k, sc)
            for s in range(NS):
                p = s % 2
                stw(k, sc, "pc_st", s,
                    n=16 * (("q" in _parts) + ("k" in _parts) + ("v" in _parts)))
                for pst, bcol_, out, msem, esem in (
                        (psq, bq, qs, "pc_mq", "pc_eq"),
                        (psk, bk, ks_, "pc_mk", "pc_ek"))[
                            :("q" in _parts) + ("k" in _parts)]:
                    for ec in range(DC):
                        bank = ec % 2
                        k.wg(sc, msem, s * DC + ec + 1)
                        k.inc(sc.activation(out[p][:, ec * 512:(ec + 1) * 512],
                                            pst[:, bank * 512:(bank + 1) * 512],
                                            AF.Identity,
                                            bias=bcol_[:, ec:ec + 1]), esem)

        @blk.vector
        def _(v):
            _entry(k, v)
            for s in range(NS):
                p = s % 2
                stw(k, v, "pc_st", s,
                    n=16 * (("q" in _parts) + ("k" in _parts) + ("v" in _parts)))
                for tt in range(4 if "v" in _parts else 0):
                    bank = tt % 2
                    k.wg(v, "pc_mv", 4 * s + tt + 1)
                    k.inc(v.tensor_tensor(vs[p][:, tt * D:(tt + 1) * D],
                                          psv[:, bank * D:(bank + 1) * D],
                                          bv[:], ALU.add), "pc_ev")



def _pb1_slabs():
    out = []
    for seq in range(B_LOC):
        s0 = seq * S
        a = s0
        while a < s0 + S:
            n_out = min(510, s0 + S - a)
            zl = (a == s0)
            a0 = a if zl else a - 1
            a1 = min(a + n_out + 1, s0 + S)
            zr = (a1 == s0 + S) and (a + n_out == s0 + S)
            n_in = a1 - a0
            ofs = 1 if zl else 0
            out.append((seq, a, n_out, a0, n_in, ofs, zl, zr))
            a += n_out
    return out


def emit_PB1(k):
    nc, d, g = k.nc, k.dram, k.g
    k.declare_stores("pb_hqst", "pb_rscst")
    slabs = _pb1_slabs()
    NS = len(slabs)
    G = NS * HC
    chl = [(s, hc) + slabs[s] for s in range(NS) for hc in range(HC)]
    with ExitStack() as es:
        sb = lambda n, s, t: es.enter_context(nc.sbuf_tensor(n, s, t))
        w1 = sb("pb_w1", [128, DC * H], BF16)
        cwv = sb("pb_cw", [128, HC * 3], F32)
        cbv = sb("pb_cb", [128, HC], F32)
        xqs = [sb("pb_xq0", [128, DC * 512], BF16),
               sb("pb_xq1", [128, DC * 512], BF16)]
        cacc = [sb(f"pb_ca{i}", [128, 512], F32) for i in range(3)]
        t2m = [sb(f"pb_t2{i}", [128, 512], F32) for i in range(2)]
        sil = sb("pb_sil", [128, HC * 512], F32)
        asb = [sb("pb_ab0", [128, 512], BF16), sb("pb_ab1", [128, 512], BF16)]
        mx = sb("pb_mx", [128, 512], BF16)
        mt4 = sb("pb_mt4", [128, 8], F32)
        rsc4 = [sb("pb_rs0", [128, 4], F32), sb("pb_rs1", [128, 4], F32)]
        ng4 = sb("pb_ng4", [128, 4], F32)
        srow = sb("pb_srow", [1, 512], F32)
        srep = sb("pb_srep", [128, 512], F32)
        qtile = [sb("pb_qt0", [128, 512], F32), sb("pb_qt1", [128, 512], F32)]
        k0 = [sb("pb_k00", [128, 512], I32), sb("pb_k01", [128, 512], I32)]
        k0b = [sb("pb_kb0", [128, 512], BF16), sb("pb_kb1", [128, 512], BF16)]
        ddt = sb("pb_dd", [128, 512], BF16)
        aat = sb("pb_aa", [128, 512], BF16)
        hqs = sb("pb_hq", [128, HC * 512], BF16)
        psH = es.enter_context(nc.psum_tensor("pb_psH", [128, 2048], F32))
        psX = es.enter_context(nc.psum_tensor("pb_psX", [128, 512], BF16))
        psR = es.enter_context(nc.psum_tensor("pb_psR", [128, 512], F32))
        blk = es.enter_context(nc.Block())
        xqT_d = d["xqT_s"].rearrange("(c p) t -> p c t", p=128)
        hq_d = d["hq_s"].rearrange("(c p) t -> p c t", p=128)

        def hq_target(p, qr):
            return 16 * ((p // 2) * 4 + qr + 1)

        def rsc_target(p):
            n = 0
            for p2 in range(p % 2, p + 1, 2):
                n += (slabs[p2][2] + 127) // 128
            return 16 * n

        @blk.sync
        def _(sync):
            _entry(k, sync)
            k.inc(sync.dma_start(
                out=w1.ap().rearrange("p (c h) -> p c h", c=DC),
                in_=d["w1qT_s"].rearrange("(c p) h -> p c h", p=128)),
                "pb_wl", 16)
            k.inc(sync.dma_start(out=cwv[:], in_=d["convw"][:]), "pb_wl", 16)
            k.inc(sync.dma_start(out=cbv[:], in_=d["convb"][:]), "pb_wl", 16)

            def stores(p):
                a, n_out = slabs[p][1], slabs[p][2]
                for qr in range(4):
                    k.wg(sync, "pb_qc", 32 * p + 8 * (qr + 1))
                    k.inc(sync.dma_start(
                        out=hq_d[:, qr * 8:qr * 8 + 8, a:a + n_out],
                        in_=bass.AP(hqs, qr * 8 * 512,
                                    [[HC * 512, 128], [512, 8], [1, n_out]])),
                        f"pb_hqst{p % 2}", 16)
                    assert k.cnt[f"pb_hqst{p % 2}"] == hq_target(p, qr)
                k.wg(sync, "pb_rs", p + 1)
                ngrp = (n_out + 127) // 128
                for gi in range(ngrp):
                    w = min(128, n_out - gi * 128)
                    k.inc(sync.dma_start(
                        out=d["rsc_s"][a + gi * 128:a + gi * 128 + w, 0:1],
                        in_=rsc4[p % 2][0:w, gi:gi + 1]),
                        f"pb_rscst{p % 2}", 16)
                assert k.cnt[f"pb_rscst{p % 2}"] == rsc_target(p)

            for sl, (seq, a, n_out, a0, n_in, ofs, zl, zr) in enumerate(slabs):
                if sl >= 2:
                    k.wg(sync, "pb_mm", 32 * (sl - 1))
                ldi(k, sync, "pb_ld", sl,
                    bass.AP(xqs[sl % 2], 0,
                            [[DC * 512, 128], [512, DC], [1, n_in]]),
                    xqT_d[:, :, a0:a0 + n_in])
                if sl >= 2:
                    stores(sl - 2)
            stores(NS - 2)
            stores(NS - 1)

        @blk.tensor
        def _(te):
            _entry(k, te)
            k.wg(te, "pb_wl", 48)
            for s in range(NS + 1):

                def mchunks(rng):
                    seq, a, n_out, a0, n_in, ofs, zl, zr = slabs[s]
                    for hc in rng:
                        ghc = s * HC + hc
                        bank = hc % 4
                        if ghc >= 4:
                            k.wg(te, "pb_cv", ghc - 3)
                        for dc2 in range(DC):
                            ins = te.matmul(
                                psH[:, bank * 512 + ofs:bank * 512 + ofs + n_in],
                                w1[:, dc2 * H + hc * 128:dc2 * H + (hc + 1) * 128],
                                xqs[s % 2][:, dc2 * 512:dc2 * 512 + n_in],
                                start=(dc2 == 0), stop=(dc2 == DC - 1))
                        k.inc(ins, "pb_mm")

                if s < NS:
                    ldw(k, te, "pb_ld", s)
                    mchunks(range(0, 4))
                if s >= 1:
                    p = s - 1
                    n_out = slabs[p][2]
                    ngrp = (n_out + 127) // 128
                    k.wg(te, "pb_mx", 32 * (p + 1))
                    if p >= 1:
                        k.wg(te, "pb_s1", p)
                    for gi in range(ngrp):
                        w = min(128, n_out - gi * 128)
                        ins = te.transpose(psX[0:w, gi * 128:gi * 128 + 128],
                                           mx[:, gi * 128:gi * 128 + w],
                                           g["idenb"][:])
                    k.inc(ins, "pb_tx")
                    k.wg(te, "pb_ng", p + 1)
                    if p >= 1:
                        k.wg(te, "pb_s3", p)
                    for gi in range(ngrp):
                        ins = te.transpose(psR[0:1, gi * 128:(gi + 1) * 128],
                                           ng4[:, gi:gi + 1], g["iden32"][:])
                    k.inc(ins, "pb_tr")
                    k.wg(te, "pb_s2", p + 1)
                    k.inc(te.matmul(psR[:, 0:n_out], g["onesr32"][0:1, :],
                                    srow[0:1, 0:n_out], start=True, stop=True),
                          "pb_rp")
                if s < NS:
                    mchunks(range(4, HC))

        @blk.scalar
        def _(sc):
            _entry(k, sc)
            for gg in range(G + 33):
                if gg < G:
                    s, hc, seq, a, n_out, a0, n_in, ofs, zl, zr = chl[gg]
                    bank = hc % 4
                    k.wg(sc, "pb_mm", gg + 1)
                    if zl:
                        e = k.inc(sc.memzero(psH[:, bank * 512:bank * 512 + 1]),
                                  "pb_ae")
                        k.wg(sc, "pb_ae", e)
                    if zr:
                        e = k.inc(sc.memzero(
                            psH[:, bank * 512 + ofs + n_in:
                                bank * 512 + ofs + n_in + 1]), "pb_ae")
                        k.wg(sc, "pb_ae", e)
                    if gg >= 2:
                        k.wg(sc, "pb_p2", gg - 1)
                    if gg >= 3:
                        k.wg(sc, "pb_cv", gg - 2)
                    k.inc(sc.activation(cacc[gg % 3][:, 0:n_out],
                                        psH[:, bank * 512 + 1:bank * 512 + 1 + n_out],
                                        AF.Identity,
                                        scale=cwv[:, hc * 3 + 1:hc * 3 + 2]),
                          "pb_t1")
                    k.inc(sc.activation(t2m[gg % 2][:, 0:n_out],
                                        psH[:, bank * 512:bank * 512 + n_out],
                                        AF.Identity,
                                        scale=cwv[:, hc * 3:hc * 3 + 1]),
                          "pb_t2")
                if 2 <= gg < G + 2:
                    s2, hc2, seq, a, n_out, a0, n_in, ofs, zl, zr = chl[gg - 2]
                    k.wg(sc, "pb_cv", gg - 1)
                    if gg - 33 >= 1:
                        k.wg(sc, "pb_qf", gg - 33)
                    k.inc(sc.activation(sil[:, hc2 * 512:hc2 * 512 + n_out],
                                        cacc[(gg - 2) % 3][:, 0:n_out], AF.Silu,
                                        bias=cbv[:, hc2:hc2 + 1]), "pb_sl")
                    k.wg(sc, "pb_sl", k.cnt["pb_sl"])
                    if gg >= 4:
                        k.wg(sc, "pb_mx", gg - 3)
                    k.inc(sc.activation(asb[hc2 % 2][:, 0:n_out],
                                        sil[:, hc2 * 512:hc2 * 512 + n_out],
                                        AF.Abs), "pb_ab")
                if 33 <= gg:
                    f = gg - 33
                    fp, fhc = f // HC, f % HC
                    n_outf = slabs[fp][2]
                    k.wg(sc, "pb_k0", f + 1)
                    if f >= 2:
                        k.wg(sc, "pb_qc", f - 1)
                    k.inc(sc.activation(k0b[f % 2][:, 0:n_outf],
                                        k0[fhc % 2][:, 0:n_outf],
                                        AF.Identity), "pb_kb")

        @blk.gpsimd
        def _(gp):
            _entry(k, gp)
            for s in range(NS + 1):
                for hc in range(HC):
                    if s < NS:
                        ghc = s * HC + hc
                        k.wg(gp, "pb_t1", ghc + 1)
                        k.wg(gp, "pb_t2", ghc + 1)
                        n_out = slabs[s][2]
                        ghc = s * HC + hc
                        k.inc(gp.tensor_tensor(cacc[ghc % 3][:, 0:n_out],
                                               cacc[ghc % 3][:, 0:n_out],
                                               t2m[ghc % 2][:, 0:n_out],
                                               ALU.add), "pb_p2")
                    if s >= 1:
                        p = s - 1
                        n_out = slabs[p][2]
                        if hc == 0:
                            k.wg(gp, "pb_s3", s)
                        k.wg(gp, "pb_sl", 32 * p + hc + 1)
                        if 32 * p + hc - 1 >= 1:
                            k.wg(gp, "pb_qc", 32 * p + hc - 1)
                        k.inc(gp.tensor_tensor(qtile[hc % 2][:, 0:n_out],
                                               sil[:, hc * 512:hc * 512 + n_out],
                                               srep[:, 0:n_out], ALU.mult),
                              "pb_qf")
                        k.wg(gp, "pb_qf", k.cnt["pb_qf"])
                        k.inc(gp.tensor_copy(k0[hc % 2][:, 0:n_out],
                                             qtile[hc % 2][:, 0:n_out]), "pb_k0")

        @blk.vector
        def _(v):
            _entry(k, v)
            v.memset(mt4[:], 0.0)
            e = k.inc(v.memset(ng4[:], 0.0), "pb_e")
            k.wg(v, "pb_e", e)

            def repair(p, hc):
                n_out = slabs[p][2]
                if hc == 0 and p >= 1:
                    k.wg(v, f"pb_hqst{(p - 1) % 2}", hq_target(p - 1, 3))
                f = 32 * p + hc
                k.wg(v, "pb_kb", f + 1)
                k.wg(v, "pb_qc", k.cnt["pb_qc"])
                e = k.inc(v.scalar_tensor_tensor(ddt[:, 0:n_out],
                                                 k0b[f % 2][:, 0:n_out], -1.0,
                                                 qtile[hc % 2][:, 0:n_out],
                                                 ALU.mult, ALU.add), "pb_e")
                k.wg(v, "pb_e", e)
                e = k.inc(v.scalar_tensor_tensor(aat[:, 0:n_out],
                                                 ddt[:, 0:n_out], 0.5,
                                                 k0b[f % 2][:, 0:n_out],
                                                 ALU.is_gt, ALU.add), "pb_e")
                k.wg(v, "pb_e", e)
                k.inc(v.scalar_tensor_tensor(
                    hqs[:, hc * 512:hc * 512 + n_out], ddt[:, 0:n_out],
                    -0.5, aat[:, 0:n_out], ALU.is_lt, ALU.subtract), "pb_qc")

            def mxop(j):
                pj, hcj = j // HC, j % HC
                n_outj = slabs[pj][2]
                k.wg(v, "pb_ab", j + 1)
                if hcj == 0:
                    k.wg(v, "pb_tx", pj)
                    ins = v.tensor_copy(mx[:, 0:n_outj], asb[0][:, 0:n_outj])
                else:
                    ins = v.tensor_tensor(mx[:, 0:n_outj],
                                          asb[hcj % 2][:, 0:n_outj],
                                          mx[:, 0:n_outj], ALU.max)
                k.inc(ins, "pb_mx")
                k.wg(v, "pb_mx", k.cnt["pb_mx"])

            for s in range(NS + 1):
                if s >= 1:
                    mxop(s * HC - 2)
                    mxop(s * HC - 1)
                    p = s - 1
                    n_out = slabs[p][2]
                    ngrp = (n_out + 127) // 128
                    k.wg(v, "pb_tx", s)
                    for gi in range(ngrp):
                        w = min(128, n_out - gi * 128)
                        ins = v.tensor_reduce(mt4[0:w, gi:gi + 1],
                                              psX[0:w, gi * 128:(gi + 1) * 128],
                                              AX.X, ALU.max)
                        if gi == ngrp - 1:
                            k.inc(ins, "pb_s1")
                            k.wg(v, "pb_s1", p + 1)
                        else:
                            e = k.inc(ins, "pb_e")
                            k.wg(v, "pb_e", e)
                    e = k.inc(v.tensor_scalar(mt4[:, 4:4 + ngrp], mt4[:, 0:ngrp],
                                              1e-5, None, ALU.max), "pb_e")
                    k.wg(v, "pb_e", e)
                    if p >= 2:
                        v.wait_ge(k.sem(f"pb_rscst{(p - 2) % 2}"),
                                  rsc_target(p - 2))
                    e = k.inc(v.tensor_scalar(rsc4[p % 2][:, 0:ngrp],
                                              mt4[:, 4:4 + ngrp],
                                              R127, None, ALU.mult), "pb_rs")
                    k.wg(v, "pb_rs", e)
                    if p >= 1:
                        k.wg(v, "pb_tr", p)
                    e = k.inc(v.reciprocal(ng4[:, 0:ngrp], mt4[:, 4:4 + ngrp]),
                              "pb_e")
                    k.wg(v, "pb_e", e)
                    k.inc(v.tensor_scalar(ng4[:, 0:ngrp], ng4[:, 0:ngrp],
                                          -127.0, None, ALU.mult), "pb_ng")
                    k.wg(v, "pb_ng", p + 1)
                    k.wg(v, "pb_tr", p + 1)
                    k.inc(v.tensor_copy(srow[0:1, 0:ngrp * 128],
                                        psR[0:1, 0:ngrp * 128]), "pb_s2")
                    k.wg(v, "pb_s2", p + 1)
                    k.wg(v, "pb_rp", p + 1)
                    if p >= 1:
                        k.wg(v, "pb_qf", 32 * p)
                    k.inc(v.tensor_copy(srep[:, 0:n_out], psR[:, 0:n_out]),
                          "pb_s3")
                    k.wg(v, "pb_s3", k.cnt["pb_s3"])
                if s < NS:
                    n_out = slabs[s][2]
                    for hc in range(HC):
                        ghc = s * HC + hc
                        bank = hc % 4
                        k.wg(v, "pb_p2", ghc + 1)
                        k.inc(v.scalar_tensor_tensor(
                            cacc[ghc % 3][:, 0:n_out],
                            psH[:, bank * 512 + 2:bank * 512 + 2 + n_out],
                            cwv[:, hc * 3 + 2:hc * 3 + 3],
                            cacc[ghc % 3][:, 0:n_out], ALU.mult, ALU.add),
                            "pb_cv")
                        if hc >= 2:
                            mxop(ghc - 2)
                        if s >= 1 and hc >= 1:
                            repair(s - 1, hc - 1)
                    if s >= 1:
                        repair(s - 1, HC - 1)
                else:
                    for hc in range(HC):
                        repair(s - 1, hc)



def emit_PB2(k):
    nc, d, g = k.nc, k.dram, k.g
    k.declare_stores("p2_st")
    with ExitStack() as es:
        sb = lambda n, s, t: es.enter_context(nc.sbuf_tensor(n, s, t))
        w2 = sb("p2_w2", [128, HC * D], BF16)
        hqt = [sb("p2_h0", [128, HC * 128], BF16), sb("p2_h1", [128, HC * 128], BF16)]
        rsc = [sb("p2_r0", [128, 1], F32), sb("p2_r1", [128, 1], F32)]
        gt = [sb("p2_g0", [128, D], F32), sb("p2_g1", [128, D], F32)]
        ps = es.enter_context(nc.psum_tensor("p2_ps", [128, 2 * D], F32))
        blk = es.enter_context(nc.Block())
        hq_d = d["hq_s"].rearrange("(c p) t -> p c t", p=128)

        @blk.sync
        def _(sync):
            _entry(k, sync)
            k.inc(sync.dma_start(
                out=w2.ap().rearrange("p (c e) -> p c e", c=HC),
                in_=d["w2qT_s"].rearrange("(c p) e -> p c e", p=128)),
                "p2_wl", 16)
            for i in range(NT):
                if i >= 2:
                    k.wg(sync, "p2_mm", i - 1)
                ldi(k, sync, "p2_ld", i,
                    hqt[i % 2].ap().rearrange("p (c t) -> p c t", c=HC),
                    hq_d[:, :, i * 128:(i + 1) * 128])
                if i >= 2:
                    k.wg(sync, "p2_ev", 2 * (i - 1))
                ldi(k, sync, "p2_ld", i, rsc[i % 2][:],
                    d["rsc_s"][i * 128:(i + 1) * 128, :])

        @blk.tensor
        def _(te):
            _entry(k, te)
            k.wg(te, "p2_wl", 16)
            for i in range(NT):
                p = i % 2
                bank = i % 2
                ldw(k, te, "p2_ld", i, n=32)
                if i >= 2:
                    k.wg(te, "p2_ev", 2 * (i - 1))
                for hc in range(HC):
                    for half in range(2):
                        ins = te.matmul(
                            ps[:, bank * D + half * 512:bank * D + (half + 1) * 512],
                            hqt[p][:, hc * 128:(hc + 1) * 128],
                            w2[:, hc * D + half * 512:hc * D + (half + 1) * 512],
                            start=(hc == 0), stop=(hc == HC - 1))
                k.inc(ins, "p2_mm")

        @blk.scalar
        def _(sc):
            _entry(k, sc)
            for i in range(NT):
                p = i % 2
                bank = i % 2
                k.wg(sc, "p2_mm", i + 1)
                ldw(k, sc, "p2_ld", i, n=32)
                stw(k, sc, "p2_st", i)
                for half in range(2):
                    ins = sc.activation(gt[p][:, half * 512:(half + 1) * 512],
                                        ps[:, bank * D + half * 512:
                                           bank * D + (half + 1) * 512],
                                        AF.Silu, scale=rsc[p][:, 0:1])
                k.inc(ins, "p2_ev", 2)

        @blk.gpsimd
        def _(gp):
            _entry(k, gp)
            for i in range(NT):
                p = i % 2
                k.wg(gp, "p2_ev", 2 * (i + 1))
                sti(k, gp, "p2_st", i, d["g_s"][i * 128:(i + 1) * 128, :],
                    gt[p][:])


def _nB(j):
    n = 0
    for t in range(j + 1):
        if (t % 16) + 1 > 8:
            n += 1
    return n


def emit_PD(k):
    nc, d, g = k.nc, k.dram, k.g
    k.declare_stores("pd_stx1", "pd_stn", "pd_stnT")
    with ExitStack() as es:
        sb = lambda n, s, t: es.enter_context(nc.sbuf_tensor(n, s, t))
        kTb = [sb("pd_kT0", [128, DC * S], BF16), sb("pd_kT1", [128, DC * S], BF16)]
        vb = [sb("pd_v0", [128, 16 * D], BF16), sb("pd_v1", [128, 16 * D], BF16)]
        qT = [sb("pd_q0", [128, DC * 128], BF16), sb("pd_q1", [128, DC * 128], BF16)]
        gtl = [sb(f"pd_g{i}", [128, D], F32) for i in range(3)]
        xtl = [sb(f"pd_xx{i}", [128, D], F32) for i in range(3)]
        ptile = [sb("pd_p0", [128, S], BF16), sb("pd_p1", [128, S], BF16)]
        pT = [sb("pd_pT0", [128, S], BF16), sb("pd_pT1", [128, S], BF16)]
        stt = [sb("pd_st0", [128, 16], F32), sb("pd_st1", [128, 16], F32)]
        t_sb = sb("pd_t", [128, D], F32)
        x1sb = [sb(f"pd_x1{i}", [128, D], F32) for i in range(3)]
        sq = sb("pd_sq", [128, D], F32)
        epsc = sb("pd_eps", [128, 1], F32)
        xn1 = [sb("pd_n0", [128, D], BF16), sb("pd_n1", [128, D], BF16)]
        xn1T = [sb("pd_nT0", [128, D], BF16), sb("pd_nT1", [128, D], BF16)]
        psS = es.enter_context(nc.psum_tensor("pd_psS", [128, 2048], F32))
        psT = es.enter_context(nc.psum_tensor("pd_psT", [128, 1024], BF16))
        psN = es.enter_context(nc.psum_tensor("pd_psN", [128, 1024], BF16))
        psC = es.enter_context(nc.psum_tensor("pd_psC", [128, D], F32))
        blk = es.enter_context(nc.Block())
        qT_d = d["qT_s"].rearrange("(c p) t -> p c t", p=128)
        kT_d = d["kT_s"].rearrange("(c p) t -> p c t", p=128)
        xn1T_d = d["xn1T_s"].rearrange("(c p) t -> p c t", p=128)

        def nkb_of(j):
            return (j % 16) + 1

        @blk.sync
        def _(sync):
            _entry(k, sync)
            for b in range(B_LOC):
                if b >= 2:
                    k.wg(sync, "pd_cx", 16 * (b - 1))
                ldi(k, sync, "pd_ldk", b,
                    kTb[b % 2].ap().rearrange("p (c t) -> p c t", c=DC),
                    kT_d[:, :, b * S:(b + 1) * S])
                ldi(k, sync, "pd_ldk", b,
                    vb[b % 2].ap().rearrange("p (kc e) -> p kc e", kc=16),
                    d["v_s"][b * S:(b + 1) * S, :].rearrange(
                        "(kc p) e -> p kc e", p=128))
                for qi in range(16):
                    i = b * 16 + qi
                    if i >= 2:
                        k.wg(sync, "pd_mm", i - 1)
                    ldi(k, sync, "pd_ldq", i,
                        qT[i % 2].ap().rearrange("p (c t) -> p c t", c=DC),
                        qT_d[:, :, i * 128:(i + 1) * 128])
                    if i >= 3:
                        k.wg(sync, "pd_x1p", i - 2)
                    k.inc(sync.dma_start(out=gtl[i % 3][:],
                                         in_=d["g_s"][i * 128:(i + 1) * 128, :]),
                          f"pd_ldg{i % 3}", 16)
                    k.inc(sync.dma_start(out=xtl[i % 3][:],
                                         in_=d["x"][i * 128:(i + 1) * 128, :]),
                          f"pd_ldg{i % 3}", 16)

        @blk.tensor
        def _(te):
            _entry(k, te)
            for i in range(NT + 6):
                if i < NT:
                    b, qi = i // 16, i % 16
                    pb, p = b % 2, i % 2
                    jd = qi // 4
                    wd = (qi % 4 + 1) * 128
                    if qi == 0:
                        ldw(k, te, "pd_ldk", b, n=32)
                    ldw(k, te, "pd_ldq", i)
                    if i >= 1:
                        k.wg(te, "pd_exp", i)
                        k.wg(te, "pd_mmax", i)
                    for j in range(jd + 1):
                        wj = 512 if j < jd else wd
                        for dc2 in range(DC):
                            ins = te.matmul(
                                psS[:, j * 512:j * 512 + wj],
                                qT[p][:, dc2 * 128:(dc2 + 1) * 128],
                                kTb[pb][:, dc2 * S + j * 512:dc2 * S + j * 512 + wj],
                                start=(dc2 == 0),
                                stop=(dc2 == DC - 1 and j != jd))
                        if j == jd:
                            ins = te.matmul(psS[:, j * 512 + wd - 128:j * 512 + wd],
                                            g["idenb"][:], g["cmaskb"][:],
                                            start=False, stop=True)
                    k.inc(ins, "pd_mm")
                if 1 <= i <= NT:
                    j = i - 1
                    cA = min(nkb_of(j), 8)
                    k.wg(te, "pd_exp", j + 1)
                    if j >= 1:
                        k.wg(te, "pd_pvB", _nB(j - 1))
                        k.wg(te, "pd_pvA", j)
                    for kc in range(cA):
                        ins = te.transpose(psT[:, kc * 128:(kc + 1) * 128],
                                           ptile[j % 2][:, kc * 128:(kc + 1) * 128],
                                           g["idenb"][:])
                    k.inc(ins, "pd_tpA")
                if 5 <= i <= NT + 4:
                    j = i - 5
                    k.wg(te, "pd_xn1", j + 1)
                    if j >= 1:
                        k.wg(te, "pd_ntev", j)
                    for c in range(DC):
                        ins = te.transpose(psN[:, c * 128:(c + 1) * 128],
                                           xn1[j % 2][:, c * 128:(c + 1) * 128],
                                           g["idenb"][:])
                    k.inc(ins, "pd_tn")
                if 1 <= i <= NT:
                    j = i - 1
                    if nkb_of(j) > 8:
                        k.wg(te, "pd_pvA", j + 1)
                        for kc in range(8, nkb_of(j)):
                            ins = te.transpose(
                                psT[:, (kc - 8) * 128:(kc - 7) * 128],
                                ptile[j % 2][:, kc * 128:(kc + 1) * 128],
                                g["idenb"][:])
                        k.inc(ins, "pd_tpB")
                if 2 <= i <= NT + 1:
                    m = i - 2
                    bm = m // 16
                    nkb = nkb_of(m)
                    k.wg(te, "pd_pvA", m + 1)
                    k.wg(te, "pd_pvB", _nB(m))
                    if m >= 1:
                        k.wg(te, "pd_gm", m)
                    for kc in range(nkb):
                        for half in range(2):
                            ins = te.matmul(
                                psC[:, half * 512:(half + 1) * 512],
                                pT[m % 2][:, kc * 128:(kc + 1) * 128],
                                vb[bm % 2][:, kc * D + half * 512:
                                           kc * D + (half + 1) * 512],
                                start=(kc == 0), stop=(kc == nkb - 1))
                    k.inc(ins, "pd_cx")

        @blk.vector
        def _(v):
            _entry(k, v)
            e = k.inc(v.memset(epsc[:], 1e-5), "pd_einit")
            for i in range(NT + 6):
                if i < NT:
                    qi = i % 16
                    nk = 512 * (qi // 4) + (qi % 4 + 1) * 128
                    st_ = stt[i % 2]
                    k.wg(v, "pd_mm", i + 1)
                    if i >= 2:
                        k.wg(v, "pd_exp", i - 1)
                    e = k.inc(v.tensor_reduce(st_[:, 0:1], psS[:, 0:nk], AX.X,
                                              ALU.max), "pd_e")
                    k.wg(v, "pd_e", e)
                    k.inc(v.tensor_scalar(st_[:, 1:2], st_[:, 0:1], -1.0 / 32.0,
                                          None, ALU.mult), "pd_mmax")
                if 2 <= i <= NT + 1:
                    m = i - 2
                    if m >= 2:
                        k.wg(v, "pd_gm", m - 1)
                    e = k.inc(v.reciprocal(stt[m % 2][:, 3:4],
                                           stt[m % 2][:, 2:3]), "pd_rs")
                    k.wg(v, "pd_rs", e)
                if 1 <= i <= NT:
                    j = i - 1
                    cA = min(nkb_of(j), 8)
                    k.wg(v, "pd_tpA", j + 1)
                    if j >= 2:
                        k.wg(v, "pd_cx", j - 1)
                    k.inc(v.tensor_copy(pT[j % 2][:, 0:cA * 128],
                                        psT[:, 0:cA * 128]), "pd_pvA")
                if 1 <= i <= NT:
                    j = i - 1
                    if nkb_of(j) > 8:
                        k.wg(v, "pd_tpB", _nB(j))
                        k.inc(v.tensor_copy(
                            pT[j % 2][:, 1024:nkb_of(j) * 128],
                            psT[:, 0:(nkb_of(j) - 8) * 128]), "pd_pvB")
                if 3 <= i <= NT + 2:
                    m = i - 3
                    k.wg(v, "pd_cx", m + 1)
                    k.wg(v, f"pd_ldg{m % 3}", 32 * (m // 3 + 1))
                    if m >= 1:
                        k.wg(v, "pd_x1p", m)
                    k.inc(v.scalar_tensor_tensor(t_sb[:], psC[:],
                                                 stt[m % 2][:, 3:4],
                                                 gtl[m % 3][:],
                                                 ALU.mult, ALU.mult), "pd_gm")
                if 5 <= i <= NT + 4:
                    j = i - 5
                    k.wg(v, "pd_tn", j + 1)
                    stw(k, v, "pd_stnT", j)
                    k.inc(v.tensor_copy(xn1T[j % 2][:], psN[:]), "pd_ntev")

        @blk.gpsimd
        def _(gp):
            _entry(k, gp)
            for i in range(NT + 6):
                if 4 <= i <= NT + 3:
                    m = i - 4
                    k.wg(gp, "pd_gm", m + 1)
                    k.wg(gp, f"pd_ldg{m % 3}", 32 * (m // 3 + 1))
                    stw(k, gp, "pd_stx1", m)
                    if m >= 3:
                        k.wg(gp, "pd_xn1", m - 2)
                    k.inc(gp.tensor_tensor(x1sb[m % 3][:], t_sb[:],
                                           xtl[m % 3][:], ALU.add), "pd_x1p")

        @blk.scalar
        def _(sc):
            _entry(k, sc)

            def ae(ins):
                e = k.inc(ins, "pd_ae")
                k.wg(sc, "pd_ae", e)

            for i in range(NT + 6):
                if i < NT:
                    qi = i % 16
                    nk = 512 * (qi // 4) + (qi % 4 + 1) * 128
                    st_ = stt[i % 2]
                    k.wg(sc, "pd_mmax", i + 1)
                    if i >= 2:
                        k.wg(sc, "pd_tpA", i - 1)
                        k.wg(sc, "pd_tpB", _nB(i - 2))
                        k.wg(sc, "pd_rs", i - 1)
                    k.inc(sc.activation(ptile[i % 2][:, 0:nk], psS[:, 0:nk],
                                        AF.Exp, bias=st_[:, 1:2],
                                        scale=1.0 / 32.0,
                                        accum_out=st_[:, 2:3]), "pd_exp")
                if 4 <= i <= NT + 3:
                    j = i - 4
                    st_ = stt[j % 2]
                    k.wg(sc, "pd_x1p", j + 1)
                    if j == 0:
                        k.wg(sc, "pd_einit", 1)
                    ae(sc.activation(sq[:], x1sb[j % 3][:], AF.Identity,
                                     accum_out=st_[:, 4:5]))
                    ae(sc.activation(sq[:], x1sb[j % 3][:], AF.Square,
                                     accum_out=st_[:, 5:6]))
                    sti(k, sc, "pd_stx1", j, d["x1_s"][j * 128:(j + 1) * 128, :],
                        x1sb[j % 3][:])
                    ae(sc.activation(st_[:, 6:7], st_[:, 4:5], AF.Identity,
                                     scale=1.0 / D))
                    ae(sc.activation(st_[:, 7:8], st_[:, 6:7], AF.Square))
                    ae(sc.activation(st_[:, 8:9], st_[:, 7:8], AF.Identity,
                                     scale=-1.0))
                    ae(sc.activation(st_[:, 9:10], st_[:, 5:6], AF.Identity,
                                     scale=1.0 / D, bias=st_[:, 8:9]))
                    ae(sc.activation(st_[:, 10:11], st_[:, 9:10], AF.Ln,
                                     bias=epsc[:, 0:1]))
                    ae(sc.activation(st_[:, 13:14], st_[:, 10:11], AF.Exp,
                                     scale=-0.5))
                    ae(sc.activation(st_[:, 11:12], st_[:, 6:7], AF.Identity,
                                     scale=st_[:, 13:14]))
                    ae(sc.activation(st_[:, 12:13], st_[:, 11:12], AF.Identity,
                                     scale=-1.0))
                    if j >= 2:
                        k.wg(sc, "pd_tn", j - 1)
                    stw(k, sc, "pd_stn", j)
                    k.inc(sc.activation(xn1[j % 2][:], x1sb[j % 3][:],
                                        AF.Identity, bias=st_[:, 12:13],
                                        scale=st_[:, 13:14]), "pd_xn1")
                    k.wg(sc, "pd_xn1", j + 1)
                    sti(k, sc, "pd_stn", j, d["xn1_s"][j * 128:(j + 1) * 128, :],
                        xn1[j % 2][:])
                if 5 <= i <= NT + 4:
                    j = i - 5
                    k.wg(sc, "pd_ntev", j + 1)
                    sti(k, sc, "pd_stnT", j,
                        xn1T_d[:, :, j * 128:(j + 1) * 128],
                        xn1T[j % 2].ap().rearrange("p (c t) -> p c t", c=DC))



def emit_PE1(k):
    nc, d, g = k.nc, k.dram, k.g
    k.declare_stores("pe_st")
    NS = T // 512
    with ExitStack() as es:
        sb = lambda n, s, t: es.enter_context(nc.sbuf_tensor(n, s, t))
        wq = sb("pe_wq", [128, DC * D], BF16)
        wk = sb("pe_wk", [128, DC * D], BF16)
        wv = sb("pe_wv", [128, DC * D], BF16)
        xs = [sb("pe_x0", [128, DC * 512], BF16), sb("pe_x1", [128, DC * 512], BF16)]
        qs = [sb("pe_q0", [128, DC * 512], BF16), sb("pe_q1", [128, DC * 512], BF16)]
        ks_ = [sb("pe_k0", [128, DC * 512], BF16), sb("pe_k1", [128, DC * 512], BF16)]
        vs = [sb("pe_v0", [128, 4 * D], BF16), sb("pe_v1", [128, 4 * D], BF16)]
        psq = es.enter_context(nc.psum_tensor("pe_psq", [128, 1024], F32))
        psk = es.enter_context(nc.psum_tensor("pe_psk", [128, 1024], F32))
        psv = es.enter_context(nc.psum_tensor("pe_psv", [128, 2048], F32))
        blk = es.enter_context(nc.Block())
        xn1T_d = d["xn1T_s"].rearrange("(c p) t -> p c t", p=128)
        q1T_d = d["q1T_s"].rearrange("(c p) t -> p c t", p=128)
        k1T_d = d["k1T_s"].rearrange("(c p) t -> p c t", p=128)

        @blk.sync
        def _(sync):
            _entry(k, sync)
            for s in range(NS):
                if s >= 2:
                    k.wg(sync, "pe_mv", 4 * (s - 1))
                ldi(k, sync, "pe_ld", s,
                    xs[s % 2].ap().rearrange("p (c t) -> p c t", c=DC),
                    xn1T_d[:, :, s * 512:(s + 1) * 512])

        @blk.gpsimd
        def _(gp):
            _entry(k, gp)
            for wsb, wnm in ((wq, "Wql"), (wk, "Wkl"), (wv, "Wvl")):
                k.inc(gp.dma_start(
                    out=wsb.ap().rearrange("p (c e) -> p c e", c=DC),
                    in_=d[wnm].rearrange("(c p) e -> p c e", p=128)), "pe_wl", 16)
            for s in range(NS):
                p = s % 2
                k.wg(gp, "pe_eq", DC * (s + 1))
                sti(k, gp, "pe_st", s, q1T_d[:, :, s * 512:(s + 1) * 512],
                    qs[p].ap().rearrange("p (c t) -> p c t", c=DC))
                k.wg(gp, "pe_ek", DC * (s + 1))
                sti(k, gp, "pe_st", s, k1T_d[:, :, s * 512:(s + 1) * 512],
                    ks_[p].ap().rearrange("p (c t) -> p c t", c=DC))
                k.wg(gp, "pe_ev", 4 * (s + 1))
                sti(k, gp, "pe_st", s,
                    d["v1_s"][s * 512:(s + 1) * 512, :].rearrange(
                        "(tt p) e -> p tt e", p=128),
                    vs[p].ap().rearrange("p (tt e) -> p tt e", tt=4))

        @blk.tensor
        def _(te):
            _entry(k, te)
            k.wg(te, "pe_wl", 16 * 3)
            for s in range(NS):
                p = s % 2
                ldw(k, te, "pe_ld", s)
                for wsb, pst, msem, esem in ((wq, psq, "pe_mq", "pe_eq"),
                                             (wk, psk, "pe_mk", "pe_ek")):
                    for ec in range(DC):
                        gq = s * DC + ec
                        bank = ec % 2
                        if gq >= 2:
                            k.wg(te, esem, gq - 1)
                        for dc2 in range(DC):
                            ins = te.matmul(pst[:, bank * 512:(bank + 1) * 512],
                                            wsb[:, dc2 * D + ec * 128:
                                                dc2 * D + (ec + 1) * 128],
                                            xs[p][:, dc2 * 512:(dc2 + 1) * 512],
                                            start=(dc2 == 0),
                                            stop=(dc2 == DC - 1))
                        k.inc(ins, msem)
                for tt in range(4):
                    gv = 4 * s + tt
                    bank = tt % 2
                    if gv >= 2:
                        k.wg(te, "pe_ev", gv - 1)
                    for dc2 in range(DC):
                        for half in range(2):
                            ins = te.matmul(
                                psv[:, bank * D + half * 512:
                                    bank * D + (half + 1) * 512],
                                xs[p][:, dc2 * 512 + tt * 128:
                                      dc2 * 512 + (tt + 1) * 128],
                                wv[:, dc2 * D + half * 512:
                                   dc2 * D + (half + 1) * 512],
                                start=(dc2 == 0), stop=(dc2 == DC - 1))
                    k.inc(ins, "pe_mv")

        @blk.scalar
        def _(sc):
            _entry(k, sc)
            for s in range(NS):
                p = s % 2
                stw(k, sc, "pe_st", s, n=48)
                for pst, out, msem, esem in ((psq, qs, "pe_mq", "pe_eq"),
                                             (psk, ks_, "pe_mk", "pe_ek")):
                    for ec in range(DC):
                        bank = ec % 2
                        k.wg(sc, msem, s * DC + ec + 1)
                        k.inc(sc.activation(out[p][:, ec * 512:(ec + 1) * 512],
                                            pst[:, bank * 512:(bank + 1) * 512],
                                            AF.Identity), esem)

        @blk.vector
        def _(v):
            _entry(k, v)
            for s in range(NS):
                p = s % 2
                stw(k, v, "pe_st", s, n=48)
                for tt in range(4):
                    bank = tt % 2
                    k.wg(v, "pe_mv", 4 * s + tt + 1)
                    k.inc(v.tensor_copy(vs[p][:, tt * D:(tt + 1) * D],
                                        psv[:, bank * D:(bank + 1) * D]),
                          "pe_ev")





def _b_of(hl):
    return (hl % 2) * 4 + hl // 2


def emit_PF1(k):
    nc, d, g = k.nc, k.dram, k.g
    k.declare_stores("pf_yst", "pf_gin", "pf_gcst")
    NBC = B_LOC * (S // 256)
    with ExitStack() as es:
        sb = lambda n, s, t: es.enter_context(nc.sbuf_tensor(n, s, t))
        wol = sb("pf_wol", [128, DC * D], BF16)
        wg = {nm: sb("pf_" + nm, [128, DC * D], BF16)
              for nm in ("Wqg", "Wkg", "Wvg", "Wog")}
        q1 = [sb("pf_q0", [128, DC * 256], BF16), sb("pf_q1", [128, DC * 256], BF16)]
        k1 = [sb("pf_k0", [128, DC * 256], BF16), sb("pf_k1", [128, DC * 256], BF16)]
        v1 = [sb("pf_v0", [128, 2 * D], BF16), sb("pf_v1", [128, 2 * D], BF16)]
        xn1 = [sb("pf_xn0", [128, 2 * D], BF16), sb("pf_xn1", [128, 2 * D], BF16)]
        p_sb = sb("pf_p", [128, 4096], BF16)
        pT_sb = sb("pf_pT", [128, 2 * 2048], BF16)
        ctx_sb = sb("pf_ctx", [128, D], BF16)
        ctxT_sb = sb("pf_ctxT", [128, D], BF16)
        y_sb = [sb("pf_y0", [128, D], BF16), sb("pf_y1", [128, D], BF16)]
        stt = sb("pf_stt", [128, 24], F32)
        grow = sb("pf_grow", [1, D], F32)
        gin = sb("pf_gin", [32, D], F32)
        ginT = sb("pf_ginT", [128, 8 * 32], BF16)
        qgT = sb("pf_qgT", [64, 16 * 32], BF16)
        kgT = sb("pf_kgT", [64, 16 * 32], BF16)
        vg = sb("pf_vg", [32, D], BF16)
        pg = sb("pf_pg", [32, 512], BF16)
        pgT = sb("pf_pgT", [32, 512], BF16)
        ctxg = sb("pf_ctxg", [32, D], BF16)
        ctxgT = sb("pf_ctxgT", [128, 8 * 32], BF16)
        gst = sb("pf_gst", [32, 24], F32)
        gc_sb = sb("pf_gc", [32, D], F32)
        psS = es.enter_context(nc.psum_tensor("pf_psS", [128, 2048], F32))
        psT = es.enter_context(nc.psum_tensor("pf_psT", [128, 2048], BF16))
        psC = es.enter_context(nc.psum_tensor("pf_psC", [128, 1024], F32))
        blk = es.enter_context(nc.Block())
        q1T_d = d["q1T_s"].rearrange("(c p) t -> p c t", p=128)
        k1T_d = d["k1T_s"].rearrange("(c p) t -> p c t", p=128)

        @blk.sync
        def _(sync):
            _entry(k, sync)
            for bc in range(NBC):
                t0 = bc * 256
                p = bc % 2
                if bc >= 2:
                    k.wg(sync, "pf_mm", 4 * (bc - 1))
                ldi(k, sync, "pf_ld", bc,
                    q1[p].ap().rearrange("p (c t) -> p c t", c=DC),
                    q1T_d[:, :, t0:t0 + 256])
                ldi(k, sync, "pf_ld", bc,
                    k1[p].ap().rearrange("p (c t) -> p c t", c=DC),
                    k1T_d[:, :, t0:t0 + 256])
                if bc >= 2:
                    k.wg(sync, "pf_cx", 2 * (bc - 1))
                ldi(k, sync, "pf_ld", bc,
                    v1[p].ap().rearrange("p (kc e) -> p kc e", kc=2),
                    d["v1_s"][t0:t0 + 256, :].rearrange(
                        "(kc p) e -> p kc e", p=128))
                if bc >= 2:
                    k.wg(sync, "pf_y", 2 * (bc - 1))
                ldi(k, sync, "pf_ld", bc,
                    xn1[p].ap().rearrange("p (tt e) -> p tt e", tt=2),
                    d["xn1_s"][t0:t0 + 256, :].rearrange(
                        "(tt p) e -> p tt e", p=128))
            k.wg(sync, "pf_gin0", 16 * (NBC // 2))
            k.wg(sync, "pf_gin1", 16 * (NBC // 2))
            k.inc(sync.dma_start(out=gin[:], in_=d["gc_s"][:]), "pf_gld", 16)

        @blk.tensor
        def _(te):
            _entry(k, te)
            k.wg(te, "pf_wl", 16 * 5)
            for bc in range(NBC):
                p = bc % 2
                ldw(k, te, "pf_ld", bc, n=64)
                for qs_ in range(2):
                    iq = 2 * bc + qs_
                    for r in range(2):
                        gr = 2 * iq + r
                        if gr >= 1:
                            k.wg(te, "pf_ex", gr)
                        if r == 0 and iq >= 1:
                            k.wg(te, "pf_gn", iq)
                        for hl in range(8):
                            j = (8 * r + hl) % 2
                            ec = (8 * r + hl) // 2
                            cS = (hl % 2) * 1024 + (hl // 2) * 256
                            ins = te.matmul(
                                psS[:, cS:cS + 256],
                                q1[p][64 * j:64 * (j + 1),
                                      ec * 256 + qs_ * 128:
                                      ec * 256 + (qs_ + 1) * 128],
                                k1[p][64 * j:64 * (j + 1),
                                      ec * 256:(ec + 1) * 256],
                                start=True, stop=True,
                                tile_position=(64 * j, 0))
                        k.inc(ins, "pf_mm")
                        k.wg(te, "pf_ex", gr + 1)
                        if gr >= 1:
                            k.wg(te, "pf_pev", gr)
                        if iq >= 1:
                            k.wg(te, "pf_ctev", iq)
                        for b in range(8):
                            for kc in range(2):
                                ins = te.transpose(
                                    psT[:, (b * 2 + kc) * 128:
                                        (b * 2 + kc + 1) * 128],
                                    p_sb[:, r * 2048 + b * 256 + kc * 128:
                                         r * 2048 + b * 256 + (kc + 1) * 128],
                                    g["idenb"][:])
                        k.inc(ins, "pf_tp")
                    k.wg(te, "pf_pev", 2 * iq + 2)
                    if iq >= 1:
                        k.wg(te, "pf_y", iq)
                    for h in range(16):
                        r, hl = h // 8, h % 8
                        b = _b_of(hl)
                        for kc in range(2):
                            ins = te.matmul(
                                psC[:, h * 64:(h + 1) * 64],
                                pT_sb[:, r * 2048 + (b * 2 + kc) * 128:
                                      r * 2048 + (b * 2 + kc + 1) * 128],
                                v1[p][:, kc * D + h * 64:kc * D + (h + 1) * 64],
                                start=(kc == 0), stop=(kc == 1))
                    k.inc(ins, "pf_cx")
                    k.wg(te, "pf_cev", iq + 1)
                    for c in range(DC):
                        ins = te.transpose(psT[:, c * 128:(c + 1) * 128],
                                           ctx_sb[:, c * 128:(c + 1) * 128],
                                           g["idenb"][:])
                    k.inc(ins, "pf_ctt")
                    k.wg(te, "pf_ctev", iq + 1)
                    k.wg(te, "pf_cev", iq + 1)
                    for ec in range(DC):
                        for half in range(2):
                            ins = te.matmul(
                                psC[:, half * 512:(half + 1) * 512],
                                ctxT_sb[:, ec * 128:(ec + 1) * 128],
                                wol[:, ec * D + half * 512:
                                    ec * D + (half + 1) * 512],
                                start=(ec == 0), stop=(ec == DC - 1))
                    k.inc(ins, "pf_wo")
                    k.wg(te, "pf_y", iq + 1)
                    k.wg(te, "pf_ex", 2 * iq + 2)
                    te.matmul(psS[0:1, 0:512], g["onescolb"][:, 0:1],
                              y_sb[qs_][:, 0:512], start=True, stop=True)
                    k.inc(te.matmul(psS[0:1, 512:1024], g["onescolb"][:, 0:1],
                                    y_sb[qs_][:, 512:1024], start=True,
                                    stop=True), "pf_sum")

            k.wg(te, "pf_gld", 16)
            k.wg(te, "pf_y", 2 * NBC)
            k.wg(te, "pf_gn", 2 * NBC)
            for c in range(DC):
                ins = te.transpose(psC[:, c * 32:(c + 1) * 32],
                                   gin[:, c * 128:(c + 1) * 128],
                                   g["iden32"][0:32, 0:32])
            k.inc(ins, "pf_g1")
            k.wg(te, "pf_g2", 1)
            for wnm, ofs in (("Wqg", 0), ("Wkg", 256)):
                for ec in range(DC):
                    for dc2 in range(DC):
                        ins = te.matmul(
                            psS[:, ofs + ec * 32:ofs + (ec + 1) * 32],
                            wg[wnm][:, dc2 * D + ec * 128:dc2 * D + (ec + 1) * 128],
                            ginT[:, dc2 * 32:(dc2 + 1) * 32],
                            start=(dc2 == 0), stop=(dc2 == DC - 1))
            k.inc(ins, "pf_g3")
            for dc2 in range(DC):
                for half in range(2):
                    ins = te.matmul(
                        psS[0:32, 512 + half * 512:1024 + half * 512],
                        ginT[:, dc2 * 32:(dc2 + 1) * 32],
                        wg["Wvg"][:, dc2 * D + half * 512:
                                  dc2 * D + (half + 1) * 512],
                        start=(dc2 == 0), stop=(dc2 == DC - 1))
            k.inc(ins, "pf_g4")
            k.wg(te, "pf_g5", 1)
            for h in range(16):
                te.matmul(psC[0:32, h * 32:(h + 1) * 32],
                          g["idenb"][0:32, 0:32], g["gmaskb"][:],
                          start=True, stop=False)
                ins = te.matmul(psC[0:32, h * 32:(h + 1) * 32],
                                qgT[:, h * 32:(h + 1) * 32],
                                kgT[:, h * 32:(h + 1) * 32],
                                start=False, stop=True)
            k.inc(ins, "pf_g6")
            k.wg(te, "pf_g7", 1)
            for h in range(16):
                ins = te.transpose(psT[0:32, h * 32:(h + 1) * 32],
                                   pg[:, h * 32:(h + 1) * 32],
                                   g["idenb"][0:32, 0:32])
            k.inc(ins, "pf_g8")
            k.wg(te, "pf_g9", 1)
            for h in range(16):
                ins = te.matmul(psC[0:32, h * 64:(h + 1) * 64],
                                pgT[:, h * 32:(h + 1) * 32],
                                vg[:, h * 64:(h + 1) * 64],
                                start=True, stop=True)
            k.inc(ins, "pf_g10")
            k.wg(te, "pf_g11", 1)
            for c in range(DC):
                ins = te.transpose(psT[:, 1024 + c * 32:1024 + (c + 1) * 32],
                                   ctxg[:, c * 128:(c + 1) * 128],
                                   g["idenb"][0:32, 0:32])
            k.inc(ins, "pf_g12")
            k.wg(te, "pf_g13", 1)
            for ec in range(DC):
                for half in range(2):
                    ins = te.matmul(
                        psS[0:32, half * 512:(half + 1) * 512],
                        ctxgT[:, ec * 32:(ec + 1) * 32],
                        wg["Wog"][:, ec * D + half * 512:
                                  ec * D + (half + 1) * 512],
                        start=(ec == 0), stop=(ec == DC - 1))
            k.inc(ins, "pf_g14")

        @blk.vector
        def _(v):
            _entry(k, v)
            for bc in range(NBC):
                p = bc % 2
                for qs_ in range(2):
                    iq = 2 * bc + qs_
                    for r in range(2):
                        gr = 2 * iq + r
                        k.wg(v, "pf_mm", gr + 1)
                        if gr >= 2:
                            k.wg(v, "pf_ex", gr - 1)
                        e = k.inc(v.tensor_reduce(stt[:, 2 * r:2 * r + 1],
                                                  psS[:, 0:2048], AX.X, ALU.max),
                                  "pf_e")
                        k.wg(v, "pf_e", e)
                        k.inc(v.tensor_scalar(stt[:, 2 * r + 1:2 * r + 2],
                                              stt[:, 2 * r:2 * r + 1],
                                              -0.125, None, ALU.mult), "pf_mx")
                        k.wg(v, "pf_ex", gr + 1)
                        k.inc(v.reciprocal(stt[:, 4 + 8 * r:12 + 8 * r],
                                           stt[:, 4 + 8 * r:12 + 8 * r]),
                              "pf_sm")
                    k.wg(v, "pf_cx", iq + 1)
                    k.wg(v, "pf_sm", 2 * iq + 2)
                    if iq >= 1:
                        k.wg(v, "pf_ctt", iq)
                    k.inc(v.tensor_tensor(
                        bass.AP(ctx_sb, 0, [[1024, 128], [64, 16], [1, 64]]),
                        bass.AP(psC, 0, [[1024, 128], [64, 16], [1, 64]]),
                        bass.AP(stt, 4, [[24, 128], [1, 16], [0, 64]]),
                        ALU.mult), "pf_cev")
                    k.wg(v, "pf_ctt", iq + 1)
                    if iq >= 1:
                        k.wg(v, "pf_wo", iq)
                    k.inc(v.tensor_copy(ctxT_sb[:], psT[:, 0:1024]), "pf_ctev")
                    k.wg(v, "pf_wo", iq + 1)
                    ldw(k, v, "pf_ld", bc, n=64)
                    stw(k, v, "pf_yst", iq)
                    if iq >= 2:
                        k.wg(v, "pf_sum", iq - 1)
                    k.inc(v.tensor_tensor(y_sb[qs_][:], psC[:, 0:1024],
                                          xn1[p][:, qs_ * D:(qs_ + 1) * D],
                                          ALU.add), "pf_y")
                    k.wg(v, "pf_sum", iq + 1)
                    if qs_ == 0:
                        stw(k, v, "pf_gin", bc + 1)
                        k.inc(v.tensor_scalar(grow[0:1, :], psS[0:1, 0:1024],
                                              1.0 / 256.0, None, ALU.mult),
                              "pf_gn")
                        k.wg(v, "pf_gn", k.cnt["pf_gn"])
                    else:
                        k.inc(v.scalar_tensor_tensor(grow[0:1, :],
                                                     psS[0:1, 0:1024],
                                                     1.0 / 256.0, grow[0:1, :],
                                                     ALU.mult, ALU.add),
                              "pf_gn")

            k.wg(v, "pf_g1", 1)
            k.inc(v.tensor_copy(ginT[:], psC[:, 0:256]), "pf_g2")
            k.wg(v, "pf_g3", 1)
            for ec in range(DC):
                v.tensor_copy(qgT[0:64, (2 * ec) * 32:(2 * ec + 1) * 32],
                              psS[0:64, ec * 32:(ec + 1) * 32])
                v.tensor_copy(qgT[0:64, (2 * ec + 1) * 32:(2 * ec + 2) * 32],
                              psS[64:128, ec * 32:(ec + 1) * 32])
                v.tensor_copy(kgT[0:64, (2 * ec) * 32:(2 * ec + 1) * 32],
                              psS[0:64, 256 + ec * 32:256 + (ec + 1) * 32])
                v.tensor_copy(kgT[0:64, (2 * ec + 1) * 32:(2 * ec + 2) * 32],
                              psS[64:128, 256 + ec * 32:256 + (ec + 1) * 32])
            k.wg(v, "pf_g4", 1)
            k.inc(v.tensor_copy(vg[:], psS[0:32, 512:1536]), "pf_g5")
            k.wg(v, "pf_g6", 1)
            e = k.inc(v.tensor_reduce(gst[:, 0:1], psC[0:32, 0:512], AX.X,
                                      ALU.max), "pf_e")
            k.wg(v, "pf_e", e)
            k.inc(v.tensor_scalar(gst[:, 1:2], gst[:, 0:1], -0.125, None,
                                  ALU.mult), "pf_gmx")
            k.wg(v, "pf_gex", 1)
            e = k.inc(v.tensor_reduce(
                bass.AP(gst, 4, [[24, 32], [1, 16], [1, 1]]),
                bass.AP(pg, 0, [[512, 32], [32, 16], [1, 32]]),
                AX.X, ALU.add), "pf_e")
            k.wg(v, "pf_e", e)
            e = k.inc(v.reciprocal(gst[:, 4:20], gst[:, 4:20]), "pf_e")
            k.wg(v, "pf_e", e)
            for h in range(16):
                ins = v.tensor_scalar(pg[:, h * 32:(h + 1) * 32],
                                      pg[:, h * 32:(h + 1) * 32],
                                      gst[:, 4 + h:5 + h], None, ALU.mult)
            k.inc(ins, "pf_g7")
            k.wg(v, "pf_g8", 1)
            k.inc(v.tensor_copy(pgT[:], psT[0:32, 0:512]), "pf_g9")
            k.wg(v, "pf_g10", 1)
            k.inc(v.tensor_copy(ctxg[:], psC[0:32, 0:1024]), "pf_g11")
            k.wg(v, "pf_g12", 1)
            k.inc(v.tensor_copy(ctxgT[:], psT[:, 1024:1280]), "pf_g13")
            k.wg(v, "pf_g14", 1)
            k.inc(v.tensor_tensor(gc_sb[:], psS[0:32, 0:1024], gin[:], ALU.add),
                  "pf_g15")

        @blk.scalar
        def _(sc):
            _entry(k, sc)
            for bc in range(NBC):
                for qs_ in range(2):
                    iq = 2 * bc + qs_
                    for r in range(2):
                        gr = 2 * iq + r
                        k.wg(sc, "pf_mx", gr + 1)
                        if gr >= 2:
                            k.wg(sc, "pf_tp", gr - 1)
                        if iq >= 1:
                            k.wg(sc, "pf_cev", iq)
                        for b in range(8):
                            hb = (b % 4) * 2 + b // 4
                            ins = sc.activation(
                                p_sb[:, r * 2048 + b * 256:
                                     r * 2048 + (b + 1) * 256],
                                psS[:, b * 256:(b + 1) * 256], AF.Exp,
                                bias=stt[:, 2 * r + 1:2 * r + 2],
                                scale=0.125,
                                accum_out=stt[:, 4 + 8 * r + hb:
                                              5 + 8 * r + hb])
                        k.inc(ins, "pf_ex")
                        k.wg(sc, "pf_tp", gr + 1)
                        if iq >= 1:
                            k.wg(sc, "pf_cx", iq)
                        k.inc(sc.activation(pT_sb[:, r * 2048:(r + 1) * 2048],
                                            psT[:], AF.Identity), "pf_pev")
            k.wg(sc, "pf_gmx", 1)
            k.inc(sc.activation(pg[:], psC[0:32, 0:512], AF.Exp,
                                bias=gst[:, 1:2], scale=0.125), "pf_gex")

        @blk.gpsimd
        def _(gp):
            _entry(k, gp)
            k.inc(gp.dma_start(
                out=wol.ap().rearrange("p (c e) -> p c e", c=DC),
                in_=d["Wol"].rearrange("(c p) e -> p c e", p=128)), "pf_wl", 16)
            for nm in ("Wqg", "Wkg", "Wvg", "Wog"):
                k.inc(gp.dma_start(
                    out=wg[nm].ap().rearrange("p (c e) -> p c e", c=DC),
                    in_=d[nm].rearrange("(c p) e -> p c e", p=128)), "pf_wl", 16)
            for bc in range(NBC):
                for qs_ in range(2):
                    iq = 2 * bc + qs_
                    k.wg(gp, "pf_y", iq + 1)
                    sti(k, gp, "pf_yst", iq, d["y_s"][iq * 128:(iq + 1) * 128, :],
                        y_sb[qs_][:])
                    if qs_ == 1:
                        k.wg(gp, "pf_gn", iq + 1)
                        sti(k, gp, "pf_gin", bc, d["gc_s"][bc:bc + 1, :],
                            grow[0:1, :])
            k.wg(gp, "pf_g15", 1)
            sti(k, gp, "pf_gcst", 0, d["gc_s"][:], gc_sb[:])



def emit_PG1(k):
    nc, d, g = k.nc, k.dram, k.g
    k.declare_stores("pg_st")
    with ExitStack() as es:
        sb = lambda n, s, t: es.enter_context(nc.sbuf_tensor(n, s, t))
        we = sb("pg_we", [128, DC * D], BF16)
        b1 = sb("pg_b1", [128, D], F32)
        xT = [sb("pg_xT0", [128, DC * 128], BF16), sb("pg_xT1", [128, DC * 128], BF16)]
        yt = [sb("pg_y0", [128, D], BF16), sb("pg_y1", [128, D], BF16)]
        x1t = [sb("pg_x10", [128, D], F32), sb("pg_x11", [128, D], F32)]
        grow = [sb("pg_gr0", [1, D], F32), sb("pg_gr1", [1, D], F32)]
        t1 = sb("pg_t1", [128, D], F32)
        g1 = sb("pg_g1", [128, D], F32)
        s1 = sb("pg_s1", [128, D], F32)
        t2 = sb("pg_t2", [128, D], F32)
        outb = [sb("pg_o0", [128, D], F32), sb("pg_o1", [128, D], F32)]
        psW = es.enter_context(nc.psum_tensor("pg_psW", [128, 2048], F32))
        psR = es.enter_context(nc.psum_tensor("pg_psR", [128, 1024], F32))
        blk = es.enter_context(nc.Block())
        xn1T_d = d["xn1T_s"].rearrange("(c p) t -> p c t", p=128)

        @blk.sync
        def _(sync):
            _entry(k, sync)
            k.inc(sync.dma_start(
                out=we.ap().rearrange("p (c e) -> p c e", c=DC),
                in_=d["weT_s"].rearrange("(c p) e -> p c e", p=128)), "pg_wl", 16)
            k.inc(sync.dma_start(out=b1[:], in_=d["bias1Rep"][:]), "pg_wl", 16)
            for i in range(NT):
                p = i % 2
                if i % 2 == 0:
                    ch = i // 2
                    if ch >= 2:
                        k.wg(sync, "pg_rep", ch - 1)
                    ldi(k, sync, "pg_ldr", ch, grow[ch % 2][:],
                        d["gc_s"][ch:ch + 1, :])
                if i >= 2:
                    k.wg(sync, "pg_mm", i - 1)
                ldi(k, sync, "pg_ldx", i,
                    xT[p].ap().rearrange("p (c t) -> p c t", c=DC),
                    xn1T_d[:, :, i * 128:(i + 1) * 128])
                if i >= 2:
                    k.wg(sync, "pg_s1", i - 1)
                ldi(k, sync, "pg_ldy", i, yt[p][:],
                    d["y_s"][i * 128:(i + 1) * 128, :])
                if i >= 2:
                    k.wg(sync, "pg_out", i - 1)
                ldi(k, sync, "pg_ldy", i, x1t[p][:],
                    d["x1_s"][i * 128:(i + 1) * 128, :])
            k.wg(sync, "pg_st0", 16 * (NT // 2))
            k.wg(sync, "pg_st1", 16 * (NT // 2))

        @blk.tensor
        def _(te):
            _entry(k, te)
            k.wg(te, "pg_wl", 32)
            for i in range(NT):
                p = i % 2
                if i % 2 == 0:
                    ch = i // 2
                    ldw(k, te, "pg_ldr", ch)
                    if ch >= 1:
                        k.wg(te, "pg_s1", 2 * ch)
                    te.matmul(psR[:, 0:512], g["onesr32"][0:1, :],
                              grow[ch % 2][0:1, 0:512], start=True, stop=True)
                    k.inc(te.matmul(psR[:, 512:1024], g["onesr32"][0:1, :],
                                    grow[ch % 2][0:1, 512:1024], start=True,
                                    stop=True), "pg_rep")
                ldw(k, te, "pg_ldx", i)
                if i >= 2:
                    k.wg(te, "pg_t1", i - 1)
                for dc2 in range(DC):
                    for half in range(2):
                        ins = te.matmul(
                            psW[:, p * 1024 + half * 512:
                                p * 1024 + (half + 1) * 512],
                            xT[p][:, dc2 * 128:(dc2 + 1) * 128],
                            we[:, dc2 * D + half * 512:dc2 * D + (half + 1) * 512],
                            start=(dc2 == 0), stop=(dc2 == DC - 1))
                k.inc(ins, "pg_mm")

        @blk.vector
        def _(v):
            _entry(k, v)
            for i in range(NT):
                p = i % 2
                ch = i // 2
                k.wg(v, "pg_mm", i + 1)
                if i >= 1:
                    k.wg(v, "pg_g1", i)
                k.inc(v.tensor_tensor(t1[:], psW[:, p * 1024:(p + 1) * 1024],
                                      b1[:], ALU.add), "pg_t1")
                k.wg(v, "pg_rep", ch + 1)
                ldw(k, v, "pg_ldy", i, n=32)
                if i >= 1:
                    k.wg(v, "pg_out", i)
                k.inc(v.tensor_tensor(s1[:], psR[:, 0:1024], yt[p][:], ALU.add),
                      "pg_s1")

        @blk.scalar
        def _(sc):
            _entry(k, sc)
            for i in range(NT):
                k.wg(sc, "pg_t1", i + 1)
                if i >= 1:
                    k.wg(sc, "pg_out", i)
                k.inc(sc.activation(g1[:], t1[:], AF.Silu), "pg_g1")

        @blk.gpsimd
        def _(gp):
            _entry(k, gp)
            for i in range(NT):
                p = i % 2
                k.wg(gp, "pg_g1", i + 1)
                k.wg(gp, "pg_s1", i + 1)
                e = k.inc(gp.tensor_tensor(t2[:], s1[:], g1[:], ALU.mult),
                          "pg_e")
                k.wg(gp, "pg_e", e)
                stw(k, gp, "pg_st", i)
                k.inc(gp.tensor_tensor(outb[p][:], t2[:], x1t[p][:], ALU.add),
                      "pg_out")
                k.wg(gp, "pg_out", i + 1)
                sti(k, gp, "pg_st", i, d["y_out"][i * 128:(i + 1) * 128, :],
                    outb[p][:])



EMITTERS = {"P0": emit_P0, "PA": emit_PA, "PC": emit_PC, "PB1": emit_PB1,
            "PB2": emit_PB2, "PD": emit_PD, "PE1": emit_PE1, "PF1": emit_PF1,
            "PG1": emit_PG1}


def _group_io(group):
    later = []
    seen = False
    for ph in PHASE_ORDER:
        if ph in group:
            seen = True
            continue
        if seen or ph not in group:
            later.append(ph)
    later = [ph for ph in PHASE_ORDER if ph not in group and
             PHASE_ORDER.index(ph) > max(PHASE_ORDER.index(p) for p in group)]
    later_cons = set()
    for ph in later:
        later_cons |= PHASE_IO[ph][0]
    cons = set()
    prod = set()
    for ph in group:
        cons |= PHASE_IO[ph][0]
        prod |= PHASE_IO[ph][1]
    ext_in = cons - prod
    ext_out = prod & later_cons
    internal = prod - ext_out
    return ext_in, ext_out, internal


def build_program(group, expose_all=False):
    nc = bass.Bass("TRN2", target_bir_lowering=False)
    dram = {}
    for nm, (shape, dt) in REAL_IN.items():
        dram[nm] = nc.dram_tensor(nm, shape, dt, kind="ExternalInput").ap()
    ext_in, ext_out, internal = _group_io(group)
    if expose_all:
        ext_out = ext_out | internal
        internal = set()
    for nm in sorted(ext_in):
        shape, dt = SCRATCH[nm]
        dram[nm] = nc.dram_tensor(nm, shape, dt, kind="ExternalInput").ap()
    for nm in sorted(ext_out):
        shape, dt = SCRATCH[nm]
        dram[nm] = nc.dram_tensor(nm, shape, dt, kind="ExternalOutput").ap()
    for nm in sorted(internal):
        shape, dt = SCRATCH[nm]
        dram[nm] = nc.dram_tensor(nm, shape, dt, kind="Internal").ap()
    if "PG1" in group:
        dram["y_out"] = nc.dram_tensor("y_out", [T, D], F32,
                                       kind="ExternalOutput").ap()
    k = K(nc, dram)
    emit_PINIT(k)
    for ph in group:
        k.entry = {nm: k.cnt[nm] for nm in
                   list(k.store_sems) + ["cld", "cldg", "cinit"]
                   if k.cnt.get(nm, 0)}
        k.phase_begin()
        EMITTERS[ph](k)
    with nc.Block() as blk:
        @blk.sync
        def _(sync):
            for nm in sorted(k.store_sems):
                sync.wait_ge(k.sem(nm), k.cnt[nm])
    k.es.close()
    return nc, sorted(ext_in), sorted(ext_out | ({"y_out"} if "PG1" in group
                                                 else set()))


import os as _os
if _os.environ.get("KM_UNFUSED"):
    PLAN = [("P0",), ("PA",), ("PC",), ("PB1",), ("PB2",), ("PD",), ("PE1",),
            ("PF1",), ("PG1",)]
else:
    PLAN = [tuple(PHASE_ORDER)]

_CACHE = {}


def kernel(**inputs):
    in_maps = _host_prep(inputs)
    pool = [dict(m) for m in in_maps]
    for group in PLAN:
        key = tuple(group)
        if key not in _CACHE:
            _CACHE[key] = build_program(list(group))
        nc, ext_in, ext_out = _CACHE[key]
        need = list(REAL_IN) + [nm for nm in ext_in if nm in SCRATCH]
        maps = [{nm: pool[c][nm] for nm in need} for c in range(NCORES)]
        res = run_bass_kernel_spmd(nc, maps, core_ids=list(range(NCORES)))
        for c in range(NCORES):
            for nm in ext_out:
                pool[c][nm] = np.asarray(res.results[c][nm])
    out = np.concatenate(
        [np.asarray(pool[c]["y_out"], dtype=np.float32).reshape(B_LOC, S, D)
         for c in range(NCORES)], axis=0)
    return out


# version 12
# speedup vs baseline: 1.0012x; 1.0012x over previous
import numpy as np
import concourse.bass as bass
import concourse.mybir as mybir
from concourse.bass_utils import run_bass_kernel_spmd
from contextlib import ExitStack

F32 = mybir.dt.float32
BF16 = mybir.dt.bfloat16
I32 = mybir.dt.int32
AF = mybir.ActivationFunctionType
ALU = mybir.AluOpType
AX = mybir.AxisListType

NCORES = 8
B_LOC = 4
S = 2048
D = 1024
H = 4096
T = B_LOC * S
NT = T // 128
DC = D // 128
HC = H // 128
C1 = 12582912.0
NEG = -1.0e30
R127 = float(np.float32(1.0) / np.float32(127.0))

SCRATCH = {
    "xnT_s":  ([D, T], BF16),
    "xqT_s":  ([D, T], BF16),
    "qT_s":   ([D, T], BF16),
    "kT_s":   ([D, T], BF16),
    "v_s":    ([T, D], BF16),
    "rsc_s":  ([T, 1], F32),
    "hq_s":   ([H, T], BF16),
    "g_s":    ([T, D], F32),
    "x1_s":   ([T, D], F32),
    "xn1_s":  ([T, D], BF16),
    "xn1T_s": ([D, T], BF16),
    "q1T_s":  ([D, T], BF16),
    "k1T_s":  ([D, T], BF16),
    "v1_s":   ([T, D], BF16),
    "y_s":    ([T, D], BF16),
    "gc_s":   ([32, D], F32),
    "w1qT_s": ([D, H], BF16),
    "w2qT_s": ([H, D], BF16),
    "weT_s":  ([D, D], BF16),
}

REAL_IN = {
    "x":      ([T, D], F32),
    "WqT":    ([D, D], F32),
    "WkT":    ([D, D], F32),
    "WvT":    ([D, D], F32),
    "bq_c":   ([128, DC], F32),
    "bk_c":   ([128, DC], F32),
    "bvRep":  ([128, D], F32),
    "w1T":    ([D, H], F32),
    "w2T":    ([H, D], F32),
    "convw":  ([128, HC * 3], F32),
    "convb":  ([128, HC], F32),
    "Wql":    ([D, D], F32),
    "Wkl":    ([D, D], F32),
    "Wvl":    ([D, D], F32),
    "Wol":    ([D, D], F32),
    "Wqg":    ([D, D], F32),
    "Wkg":    ([D, D], F32),
    "Wvg":    ([D, D], F32),
    "Wog":    ([D, D], F32),
    "weightT": ([D, D], F32),
    "alpha":  ([1, 1], F32),
    "bias1Rep": ([128, D], F32),
    "iden":   ([128, 128], F32),
    "cmask":  ([128, 128], F32),
    "gmask":  ([32, 32], F32),
    "onesr":  ([1, 128], F32),
    "rowsel": ([128, 32 * 32], F32),
}
REAL_OUT = {"y_out": ([T, D], F32)}

PHASE_IO = {
    "P0":  (set(), {"w1qT_s", "w2qT_s", "weT_s"}),
    "PA":  (set(), {"xnT_s", "xqT_s"}),
    "PC":  ({"xnT_s"}, {"qT_s", "kT_s", "v_s"}),
    "PB1": ({"xqT_s", "w1qT_s"}, {"hq_s", "rsc_s"}),
    "PB2": ({"hq_s", "rsc_s", "w2qT_s"}, {"g_s"}),
    "PD":  ({"qT_s", "kT_s", "v_s", "g_s"}, {"x1_s", "xn1_s", "xn1T_s"}),
    "PE1": ({"xn1T_s"}, {"q1T_s", "k1T_s", "v1_s"}),
    "PF1": ({"q1T_s", "k1T_s", "v1_s", "xn1_s"}, {"y_s", "gc_s"}),
    "PG1": ({"xn1T_s", "y_s", "x1_s", "gc_s", "weT_s"}, set()),
}
PHASE_ORDER = ["P0", "PA", "PC", "PB1", "PB2", "PD", "PE1", "PF1", "PG1"]


class K:

    PERSIST = ("cld", "cldg", "cinit", "pc_wl", "pe_wl", "pf_wl")

    def __init__(self, nc, dram):
        self.nc = nc
        self.dram = dram
        self.es = ExitStack()
        self._semh = {}
        self._base = {}
        self.cnt = {}
        self._phys = {}
        self._pool = []
        self._nalloc = 0
        self.g = {}
        self.store_sems = set()
        self.entry = {}

    def phase_begin(self):
        for name in list(self._semh):
            if name in self.store_sems or name in self.PERSIST:
                continue
            h = self._semh.pop(name)
            self._pool.append(h)
            self._base.pop(name, None)

    def sem(self, name):
        if name not in self._semh:
            if name in self.store_sems or name in self.PERSIST or not self._pool:
                h = self.es.enter_context(self.nc.semaphore(
                    f"s{self._nalloc}_{name}"))
                self._nalloc += 1
                self._phys[id(h)] = 0
            else:
                h = self._pool.pop()
            self._semh[name] = h
            self._base[name] = self._phys[id(h)]
            self.cnt[name] = 0
        return self._semh[name]

    def wg(self, eng, name, val):
        h = self.sem(name)
        eng.wait_ge(h, self._base[name] + val)

    def inc(self, instr, name, n=1):
        h = self.sem(name)
        instr.then_inc(h, n)
        self.cnt[name] += n
        self._phys[id(h)] += n
        return self.cnt[name]

    def inc2(self, instr, name, n=1):
        self.inc(instr, name, n)
        return instr

    def stinc(self, instr, name):
        if name not in self.store_sems:
            assert name not in self._semh, f"store sem {name} leased late"
            self.store_sems.add(name)
        return self.inc(instr, name, 16)

    def c(self, name):
        self.sem(name)
        return self.cnt[name]

    def declare_stores(self, *names):
        for nm in names:
            for par in (0, 1):
                n2 = f"{nm}{par}"
                self.store_sems.add(n2)
                self.sem(n2)



def _host_prep(inp):
    f = np.float32
    A = np.ascontiguousarray
    sh = {}
    sh["WqT"] = A(np.asarray(inp["L0_attn_Wq"], f).T)
    sh["WkT"] = A(np.asarray(inp["L0_attn_Wk"], f).T)
    sh["WvT"] = A(np.asarray(inp["L0_attn_Wv"], f).T)
    sh["bq_c"] = A(np.asarray(inp["L0_attn_bq"], f).reshape(DC, 128).T)
    sh["bk_c"] = A(np.asarray(inp["L0_attn_bk"], f).reshape(DC, 128).T)
    sh["bvRep"] = A(np.broadcast_to(np.asarray(inp["L0_attn_bv"], f)[None, :], (128, D)))
    sh["w1T"] = A(np.asarray(inp["L0_ffn_w1"], f).T)
    sh["w2T"] = A(np.asarray(inp["L0_ffn_w2"], f).T)
    cw = np.asarray(inp["L0_ffn_conv_w"], f).reshape(HC, 128, 3)
    sh["convw"] = A(cw.transpose(1, 0, 2).reshape(128, HC * 3))
    sh["convb"] = A(np.asarray(inp["L0_ffn_conv_b"], f).reshape(HC, 128).T)
    for nm, key in (("Wql", "L1_attn_Wq_l"), ("Wkl", "L1_attn_Wk_l"),
                    ("Wvl", "L1_attn_Wv_l"), ("Wol", "L1_attn_Wo_l"),
                    ("Wqg", "L1_attn_Wq_g"), ("Wkg", "L1_attn_Wk_g"),
                    ("Wvg", "L1_attn_Wv_g"), ("Wog", "L1_attn_Wo_g")):
        sh[nm] = A(np.asarray(inp[key], f))
    sh["weightT"] = A(np.asarray(inp["L1_ffn_weight"], f).T)
    sh["alpha"] = np.asarray(inp["L1_ffn_alpha"], f).reshape(1, 1)
    sh["bias1Rep"] = A(np.broadcast_to(np.asarray(inp["L1_ffn_bias"], f)[None, :], (128, D)))
    sh["iden"] = np.eye(128, dtype=f)
    sh["cmask"] = np.triu(np.full((128, 128), NEG, f), 1)
    gm = np.full((32, 32), NEG, f)
    for b in range(4):
        gm[b * 8:(b + 1) * 8, b * 8:(b + 1) * 8] = 0.0
    sh["gmask"] = gm
    sh["onesr"] = np.ones((1, 128), f)
    rs = np.zeros((128, 32 * 32), f)
    for r in range(32):
        rs[:, r * 32 + r] = 1.0
    sh["rowsel"] = A(rs)

    x_full = np.asarray(inp["L0_attn_x"], f)
    maps = []
    for c in range(NCORES):
        m = dict(sh)
        m["x"] = A(x_full[c * B_LOC:(c + 1) * B_LOC].reshape(T, D))
        maps.append(m)
    return maps



def emit_PINIT(k):
    nc, d, g = k.nc, k.dram, k.g
    mk = lambda n, s, t: k.es.enter_context(nc.sbuf_tensor(n, s, t))
    g["iden32"] = mk("g_iden32", [128, 128], F32)
    g["idenb"] = mk("g_idenb", [128, 128], BF16)
    g["cmaskb"] = mk("g_cmaskb", [128, 128], BF16)
    g["gmaskb"] = mk("g_gmaskb", [32, 32], BF16)
    g["onesr32"] = mk("g_onesr32", [1, 128], F32)
    g["onesrb"] = mk("g_onesrb", [1, 128], BF16)
    g["onescol"] = mk("g_onescol", [128, 1], F32)
    g["onescolb"] = mk("g_onescolb", [128, 1], BF16)
    with nc.Block() as blk:
        @blk.sync
        def _(sync):
            k.inc(sync.dma_start(out=g["iden32"][:], in_=d["iden"][:]), "cld", 16)
            k.inc(sync.dma_start(out=g["onesr32"][:], in_=d["onesr"][:]), "cld", 16)

        @blk.gpsimd
        def _(gp):
            k.inc(gp.dma_start(out=g["idenb"][:], in_=d["iden"][:]), "cldg", 16)
            k.inc(gp.dma_start(out=g["cmaskb"][:], in_=d["cmask"][:]), "cldg", 16)
            k.inc(gp.dma_start(out=g["gmaskb"][:], in_=d["gmask"][:]), "cldg", 16)
            k.inc(gp.dma_start(out=g["onesrb"][:], in_=d["onesr"][:]), "cldg", 16)

        @blk.vector
        def _(v):
            v.memset(g["onescol"][:], 1.0)
            k.inc(v.memset(g["onescolb"][:], 1.0), "cinit", 1)


def ldw(k, eng, name, j, n=16):
    k.wg(eng, f"{name}{j % 2}", n * (j // 2 + 1))


def ldi(k, eng, name, j, out, in_, n=16):
    return k.inc(eng.dma_start(out=out, in_=in_), f"{name}{j % 2}", n)


def sti(k, eng, name, j, out, in_):
    for par in (0, 1):
        nm = f"{name}{par}"
        if nm not in k.store_sems:
            k.store_sems.add(nm)
        k.sem(nm)
    return k.inc(eng.dma_start(out=out, in_=in_), f"{name}{j % 2}", 16)


def stw(k, eng, name, j, n=16):
    if j >= 2:
        k.wg(eng, f"{name}{j % 2}", n * (j // 2))


def _entry(k, eng):
    for nm, v in sorted(k.entry.items()):
        if v:
            eng.wait_ge(k.sem(nm), v)



def emit_P0(k):
    nc, d, g = k.nc, k.dram, k.g
    k.declare_stores("p0_st")
    with ExitStack() as es:
        sb = lambda n, s, t: es.enter_context(nc.sbuf_tensor(n, s, t))
        wb = [sb("p0_w0", [128, H], F32), sb("p0_w1", [128, H], F32)]
        wq_sb = [sb("p0_wq0", [128, H], BF16), sb("p0_wq1", [128, H], BF16)]
        t1 = sb("p0_t1", [128, H], F32)
        t2 = sb("p0_t2", [128, H], F32)
        ki = sb("p0_ki", [128, H], I32)
        db = sb("p0_db", [128, H], BF16)
        ai = sb("p0_ai", [128, H], I32)
        col = sb("p0_col", [128, 2], F32)
        st1 = sb("p0_st1", [1, 4], F32)
        bcol = sb("p0_bcol", [128, 4], F32)
        ps = es.enter_context(nc.psum_tensor("p0_ps", [128, 512], F32))
        blk = es.enter_context(nc.Block())

        cfgs = [("w1T", "w1qT_s", DC, H, "bit"),
                ("w2T", "w2qT_s", HC, D, "bit"),
                ("weightT", "weT_s", DC, D, "tern")]
        ld_before = []
        tot = 0
        for _w, _o, nch, _c, _kd in cfgs:
            ld_before.append(tot)
            tot += 2 * nch
        q_before = [0, DC, DC + HC]

        @blk.sync
        def _(sync):
            _entry(k, sync)
            k.inc(sync.dma_start(out=st1[0:1, 3:4], in_=d["alpha"][:]), "p0a", 16)
            for idx, (wname, _o, nch, colw, _kd) in enumerate(cfgs):
                for ps_ in range(2):
                    for c in range(nch):
                        gi = ld_before[idx] + ps_ * nch + c
                        if gi >= 2:
                            k.wg(sync, "p0r", gi - 1)
                        ldi(k, sync, "p0ld", gi, wb[gi % 2][:, 0:colw],
                            d[wname][c * 128:(c + 1) * 128, :])

        @blk.vector
        def _(v):
            _entry(k, v)
            for idx, (wname, _o, nch, colw, kind) in enumerate(cfgs):
                for c in range(nch):
                    gi = ld_before[idx] + c
                    ldw(k, v, "p0ld", gi)
                    if c == 0:
                        e = k.inc(v.tensor_reduce(col[:, 0:1], wb[gi % 2][:, 0:colw],
                                                  AX.X, ALU.add,
                                                  apply_absolute_value=True), "p0r")
                    else:
                        e = k.inc(v.tensor_reduce(col[:, 1:2], wb[gi % 2][:, 0:colw],
                                                  AX.X, ALU.add,
                                                  apply_absolute_value=True), "p0r")
                        k.wg(v, "p0r", e)
                        e = k.inc(v.tensor_tensor(col[:, 0:1], col[:, 0:1],
                                                  col[:, 1:2], ALU.add), "p0e")
                        k.wg(v, "p0e", e)
                k.inc(v.tensor_copy(col[:, 1:2], col[:, 0:1]), "p0v")
                k.wg(v, "p0m", 2 * idx + 1)
                e = k.inc(v.tensor_scalar(st1[0:1, 0:1], ps[0:1, 0:1],
                                          1.0 / (nch * 128 * colw), None,
                                          ALU.mult), "p0e")
                k.wg(v, "p0e", e)
                if kind == "bit":
                    e = k.inc(v.tensor_scalar(st1[0:1, 1:2], st1[0:1, 0:1],
                                              1e-5, None, ALU.max), "p0e")
                    k.wg(v, "p0e", e)
                    e = k.inc(v.reciprocal(st1[0:1, 2:3], st1[0:1, 1:2]), "p0e")
                    k.wg(v, "p0e", e)
                    e = k.inc(v.tensor_scalar(st1[0:1, 2:3], st1[0:1, 2:3],
                                              -1.0, None, ALU.mult), "p0e")
                    k.wg(v, "p0e", e)
                else:
                    k.wg(v, "p0a", 16)
                    e = k.inc(v.tensor_scalar(st1[0:1, 1:2], st1[0:1, 0:1],
                                              0.7, None, ALU.mult), "p0e")
                    k.wg(v, "p0e", e)
                k.inc(v.tensor_copy(st1[0:1, 0:1], st1[0:1, 0:1]), "p0v")
                k.wg(v, "p0m", 2 * idx + 2)
                if kind == "bit":
                    e = k.inc(v.tensor_copy(bcol[:, 0:3], ps[:, 0:3]), "p0e")
                else:
                    e = k.inc(v.tensor_copy(bcol[:, 1:4], ps[:, 1:4]), "p0e")
                k.wg(v, "p0e", e)
                if kind == "tern":
                    e = k.inc(v.tensor_scalar(bcol[:, 2:3], bcol[:, 2:3], -1.0,
                                              None, ALU.mult), "p0e")
                    k.wg(v, "p0e", e)
                for c in range(nch):
                    gi = ld_before[idx] + nch + c
                    ldw(k, v, "p0ld", gi)
                    gq = q_before[idx] + c
                    stw(k, v, "p0_st", gq)
                    wc = wb[gi % 2][:, 0:colw]
                    oc = wq_sb[gq % 2][:, 0:colw]
                    cw = colw
                    if k.cnt.get("p0q", 0):
                        k.wg(v, "p0q", k.cnt["p0q"])
                    def _e(ins):
                        ee = k.inc(ins, "p0e")
                        k.wg(v, "p0e", ee)
                        return ins

                    if kind == "bit":
                        k.inc(v.tensor_scalar(t1[:, 0:cw], wc, bcol[:, 2:3],
                                              None, ALU.mult), "p0r")
                        k.wg(v, "p0r", k.cnt["p0r"])
                        _e(v.tensor_scalar(t2[:, 0:cw], t1[:, 0:cw], C1, -C1,
                                           ALU.add, ALU.add))
                        _e(v.tensor_copy(ki[:, 0:cw], t2[:, 0:cw]))
                        _e(v.scalar_tensor_tensor(db[:, 0:cw], ki[:, 0:cw],
                                                  -1.0, t2[:, 0:cw], ALU.mult,
                                                  ALU.add))
                        _e(v.scalar_tensor_tensor(ai[:, 0:cw], db[:, 0:cw], 0.5,
                                                  ki[:, 0:cw], ALU.is_gt,
                                                  ALU.add))
                        _e(v.scalar_tensor_tensor(t1[:, 0:cw], db[:, 0:cw],
                                                  -0.5, ai[:, 0:cw], ALU.is_lt,
                                                  ALU.subtract))
                        _e(v.tensor_scalar(t2[:, 0:cw], t1[:, 0:cw], -1.0, 1.0,
                                           ALU.max, ALU.min))
                        ins = v.tensor_scalar(oc, t2[:, 0:cw], bcol[:, 1:2],
                                              None, ALU.mult)
                    else:
                        v.tensor_scalar(t1[:, 0:cw], wc, bcol[:, 1:2], None, ALU.is_gt)
                        k.inc(v.tensor_scalar(t2[:, 0:cw], wc, bcol[:, 2:3],
                                              None, ALU.is_lt), "p0r")
                        k.wg(v, "p0r", k.cnt["p0r"])
                        _e(v.tensor_tensor(t1[:, 0:cw], t1[:, 0:cw], t2[:, 0:cw],
                                           ALU.subtract))
                        ins = v.tensor_scalar(oc, t1[:, 0:cw], bcol[:, 3:4],
                                              None, ALU.mult)
                    k.inc(ins, "p0q")

        @blk.tensor
        def _(te):
            _entry(k, te)
            for idx, (wname, _o, nch, colw, kind) in enumerate(cfgs):
                k.wg(te, "p0v", 2 * idx + 1)
                k.inc(te.matmul(ps[0:1, 0:1], g["onescol"][:, 0:1], col[:, 0:1],
                                start=True, stop=True), "p0m")
                k.wg(te, "p0v", 2 * idx + 2)
                if kind == "bit":
                    te.matmul(ps[:, 0:1], g["onesr32"][0:1, :], st1[0:1, 0:1],
                              start=True, stop=True)
                    te.matmul(ps[:, 1:2], g["onesr32"][0:1, :], st1[0:1, 1:2],
                              start=True, stop=True)
                    k.inc(te.matmul(ps[:, 2:3], g["onesr32"][0:1, :],
                                    st1[0:1, 2:3], start=True, stop=True), "p0m")
                else:
                    te.matmul(ps[:, 1:2], g["onesr32"][0:1, :], st1[0:1, 1:2],
                              start=True, stop=True)
                    te.matmul(ps[:, 2:3], g["onesr32"][0:1, :], st1[0:1, 1:2],
                              start=True, stop=True)
                    k.inc(te.matmul(ps[:, 3:4], g["onesr32"][0:1, :],
                                    st1[0:1, 3:4], start=True, stop=True), "p0m")

        @blk.gpsimd
        def _(gp):
            _entry(k, gp)
            nq = 0
            for idx, (wname, oname, nch, colw, kind) in enumerate(cfgs):
                for c in range(nch):
                    nq += 1
                    k.wg(gp, "p0q", nq)
                    gq = q_before[idx] + c
                    sti(k, gp, "p0_st", gq, d[oname][c * 128:(c + 1) * 128, :],
                        wq_sb[gq % 2][:, 0:colw])



def emit_PA(k):
    nc, d, g = k.nc, k.dram, k.g
    k.declare_stores("pa_st")
    with ExitStack() as es:
        sb = lambda n, s, t: es.enter_context(nc.sbuf_tensor(n, s, t))
        xb = [sb("pa_x0", [128, D], F32), sb("pa_x1", [128, D], F32)]
        xn = [sb("pa_xn0", [128, D], F32), sb("pa_xn1", [128, D], F32)]
        sq = sb("pa_sq", [128, D], F32)
        stt = [sb("pa_st0", [128, 16], F32), sb("pa_st1", [128, 16], F32)]
        qt = sb("pa_qt", [128, D], F32)
        r1 = sb("pa_r1", [128, D], F32)
        k0 = sb("pa_k0", [128, D], I32)
        dd = sb("pa_dd", [128, D], BF16)
        aa = sb("pa_aa", [128, D], I32)
        fin = sb("pa_fin", [128, D], BF16)
        xq = [sb("pa_xq0", [128, D], BF16), sb("pa_xq1", [128, D], BF16)]
        xnT = [sb("pa_xnT0", [128, D], BF16), sb("pa_xnT1", [128, D], BF16)]
        xqT = [sb("pa_xqT0", [128, D], BF16), sb("pa_xqT1", [128, D], BF16)]
        psA = [es.enter_context(nc.psum_tensor("pa_psA0", [128, D], F32)),
               es.enter_context(nc.psum_tensor("pa_psA1", [128, D], F32))]
        psB = [es.enter_context(nc.psum_tensor("pa_psB0", [128, D], BF16)),
               es.enter_context(nc.psum_tensor("pa_psB1", [128, D], BF16))]
        blk = es.enter_context(nc.Block())
        xnT_d = d["xnT_s"].rearrange("(c p) t -> p c t", p=128)
        xqT_d = d["xqT_s"].rearrange("(c p) t -> p c t", p=128)

        @blk.sync
        def _(sync):
            _entry(k, sync)
            for i in range(NT):
                if i >= 2:
                    k.wg(sync, "pa_xn", i - 1)
                ldi(k, sync, "pa_ld", i, xb[i % 2][:],
                    d["x"][i * 128:(i + 1) * 128, :])

        @blk.vector
        def _(v):
            _entry(k, v)
            for i in range(NT):
                p = i % 2
                st_ = stt[p]
                ldw(k, v, "pa_ld", i)
                if i >= 2:
                    k.wg(v, "pa_xn", i - 1)
                e = k.inc(v.tensor_reduce(st_[:, 0:1], xb[p][:], AX.X, ALU.add),
                          "pa_ln")
                k.wg(v, "pa_ln", e)
                e = k.inc(v.tensor_scalar(st_[:, 2:3], st_[:, 0:1], 1.0 / D,
                                          None, ALU.mult), "pa_ln")
                k.wg(v, "pa_ln", e)
                e = k.inc(v.tensor_tensor(st_[:, 3:4], st_[:, 2:3], st_[:, 2:3],
                                          ALU.mult), "pa_ln")
                k.wg(v, "pa_sq", 2 * i + 1)
                k.wg(v, "pa_ln", e)
                e = k.inc(v.scalar_tensor_tensor(st_[:, 4:5], st_[:, 1:2],
                                                 1.0 / D, st_[:, 3:4], ALU.mult,
                                                 ALU.subtract), "pa_ln")
                k.wg(v, "pa_ln", e)
                e = k.inc(v.tensor_scalar(st_[:, 5:6], st_[:, 4:5], 1e-5, None,
                                          ALU.add), "pa_ln")
                k.wg(v, "pa_sq", 2 * i + 2)
                e = k.inc(v.reciprocal(st_[:, 7:8], st_[:, 6:7]), "pa_ln")
                k.wg(v, "pa_ln", e)
                e = k.inc(v.scalar_tensor_tensor(st_[:, 8:9], st_[:, 2:3], -1.0,
                                                 st_[:, 7:8], ALU.mult,
                                                 ALU.mult), "pa_ln")
                k.wg(v, "pa_xn", i + 1)
                e = k.inc(v.tensor_reduce(st_[:, 9:10], xn[p][:], AX.X, ALU.max,
                                          apply_absolute_value=True), "pa_ch2")
                k.wg(v, "pa_ch2", e)
                e = k.inc(v.tensor_scalar(st_[:, 10:11], st_[:, 9:10], 1e-5,
                                          None, ALU.max), "pa_ch2")
                k.wg(v, "pa_ch2", e)
                e = k.inc(v.reciprocal(st_[:, 13:14], st_[:, 10:11]), "pa_ch2")
                k.wg(v, "pa_ch2", e)
                e = k.inc(v.tensor_scalar(st_[:, 11:12], st_[:, 13:14], -127.0,
                                          None, ALU.mult), "pa_ch2")
                k.wg(v, "pa_ch2", e)
                e = k.inc(v.tensor_scalar(st_[:, 12:13], st_[:, 10:11], R127,
                                          None, ALU.mult), "pa_ch2")
                k.wg(v, "pa_ch2", e)
                e = k.inc(v.tensor_scalar(qt[:], xn[p][:], st_[:, 11:12],
                                          None, ALU.mult), "pa_ve")
                k.wg(v, "pa_ve", e)
                e = k.inc(v.tensor_scalar(r1[:], qt[:], C1, -C1, ALU.add,
                                          ALU.add), "pa_ve")
                k.wg(v, "pa_ve", e)
                e = k.inc(v.tensor_copy(k0[:], r1[:]), "pa_ve")
                k.wg(v, "pa_ve", e)
                e = k.inc(v.scalar_tensor_tensor(dd[:], k0[:], -1.0, r1[:],
                                                 ALU.mult, ALU.add), "pa_ve")
                k.wg(v, "pa_ve", e)
                e = k.inc(v.scalar_tensor_tensor(aa[:], dd[:], 0.5, k0[:],
                                                 ALU.is_gt, ALU.add), "pa_ve")
                k.wg(v, "pa_ve", e)
                e = k.inc(v.scalar_tensor_tensor(fin[:], dd[:], -0.5, aa[:],
                                                 ALU.is_lt, ALU.subtract),
                          "pa_ve")
                k.wg(v, "pa_ve", e)
                if i >= 2:
                    k.wg(v, "pa_mm", 2 * i - 2)
                k.inc(v.tensor_scalar(xq[p][:], fin[:], st_[:, 12:13], None,
                                      ALU.mult), "pa_ch")

        @blk.scalar
        def _(sc):
            _entry(k, sc)

            def evac(j):
                q = j % 2
                k.wg(sc, "pa_mm", 2 * j + 1)
                stw(k, sc, "pa_st", j, n=32)
                k.inc(sc.activation(xnT[q][:], psA[q][:], AF.Identity), "pa_ev")
                k.wg(sc, "pa_mm", 2 * j + 2)
                k.inc(sc.activation(xqT[q][:], psB[q][:], AF.Identity), "pa_ev")

            for i in range(NT):
                p = i % 2
                st_ = stt[p]
                ldw(k, sc, "pa_ld", i)
                if i >= 2:
                    k.wg(sc, "pa_xn", i - 1)
                k.inc(sc.activation(sq[:], xb[p][:], AF.Square,
                                    accum_out=st_[:, 1:2]), "pa_sq")
                k.wg(sc, "pa_ln", 7 * i + 5)
                k.inc(sc.activation(st_[:, 6:7], st_[:, 5:6], AF.Sqrt), "pa_sq")
                k.wg(sc, "pa_ln", 7 * (i + 1))
                if i >= 2:
                    k.wg(sc, "pa_mm", 2 * i - 2)
                k.inc(sc.activation(xn[p][:], xb[p][:], AF.Identity,
                                    bias=st_[:, 8:9], scale=st_[:, 7:8]),
                      "pa_xn")
                if i >= 1:
                    evac(i - 1)
            evac(NT - 1)

        @blk.tensor
        def _(te):
            _entry(k, te)
            for i in range(NT):
                p = i % 2
                k.wg(te, "pa_xn", i + 1)
                if i >= 2:
                    k.wg(te, "pa_ev", 2 * i - 2)
                for c in range(DC):
                    ins = te.transpose(psA[p][:, c * 128:(c + 1) * 128],
                                       xn[p][:, c * 128:(c + 1) * 128],
                                       g["iden32"][:])
                k.inc(ins, "pa_mm")
                k.wg(te, "pa_ch", i + 1)
                for c in range(DC):
                    ins = te.transpose(psB[p][:, c * 128:(c + 1) * 128],
                                       xq[p][:, c * 128:(c + 1) * 128],
                                       g["idenb"][:])
                k.inc(ins, "pa_mm")

        @blk.gpsimd
        def _(gp):
            _entry(k, gp)
            for i in range(NT):
                p = i % 2
                k.wg(gp, "pa_ev", 2 * i + 1)
                sti(k, gp, "pa_st", i, xnT_d[:, :, i * 128:(i + 1) * 128],
                    xnT[p].ap().rearrange("p (c t) -> p c t", c=DC))
                k.wg(gp, "pa_ev", 2 * i + 2)
                sti(k, gp, "pa_st", i, xqT_d[:, :, i * 128:(i + 1) * 128],
                    xqT[p].ap().rearrange("p (c t) -> p c t", c=DC))



def emit_PC(k):
    import os
    _parts = os.environ.get("PC_PARTS", "qkv")
    nc, d, g = k.nc, k.dram, k.g
    k.declare_stores("pc_st")
    NS = T // 512
    with ExitStack() as es:
        sb = lambda n, s, t: es.enter_context(nc.sbuf_tensor(n, s, t))
        wq = sb("pc_wq", [128, DC * D], BF16)
        wk = sb("pc_wk", [128, DC * D], BF16)
        wv = sb("pc_wv", [128, DC * D], BF16)
        bq = sb("pc_bq", [128, DC], F32)
        bk = sb("pc_bk", [128, DC], F32)
        bv = sb("pc_bv", [128, D], F32)
        xs = [sb("pc_x0", [128, DC * 512], BF16), sb("pc_x1", [128, DC * 512], BF16)]
        qs = [sb("pc_q0", [128, DC * 512], BF16), sb("pc_q1", [128, DC * 512], BF16)]
        ks_ = [sb("pc_k0", [128, DC * 512], BF16), sb("pc_k1", [128, DC * 512], BF16)]
        vs = [sb("pc_v0", [128, 4 * D], BF16), sb("pc_v1", [128, 4 * D], BF16)]
        psq = es.enter_context(nc.psum_tensor("pc_psq", [128, 1024], F32))
        psk = es.enter_context(nc.psum_tensor("pc_psk", [128, 1024], F32))
        psv = es.enter_context(nc.psum_tensor("pc_psv", [128, 2048], F32))
        blk = es.enter_context(nc.Block())
        xnT_d = d["xnT_s"].rearrange("(c p) t -> p c t", p=128)
        qT_d = d["qT_s"].rearrange("(c p) t -> p c t", p=128)
        kT_d = d["kT_s"].rearrange("(c p) t -> p c t", p=128)

        @blk.sync
        def _(sync):
            _entry(k, sync)
            k.inc(sync.dma_start(out=bq[:], in_=d["bq_c"][:]), "pc_wlb", 16)
            k.inc(sync.dma_start(out=bk[:], in_=d["bk_c"][:]), "pc_wlb", 16)
            k.inc(sync.dma_start(out=bv[:], in_=d["bvRep"][:]), "pc_wlb", 16)
            for s in range(NS):
                if s >= 2 and "v" in _parts:
                    k.wg(sync, "pc_mv", 4 * (s - 1))
                elif s >= 2:
                    k.wg(sync, "pc_mq", DC * (s - 1))
                ldi(k, sync, "pc_ld", s,
                    xs[s % 2].ap().rearrange("p (c t) -> p c t", c=DC),
                    xnT_d[:, :, s * 512:(s + 1) * 512])

        @blk.gpsimd
        def _(gp):
            _entry(k, gp)
            for wsb, wnm in ((wq, "WqT"), (wk, "WkT"), (wv, "WvT")):
                k.inc(gp.dma_start(
                    out=wsb.ap().rearrange("p (c e) -> p c e", c=DC),
                    in_=d[wnm].rearrange("(c p) e -> p c e", p=128)), "pc_wl", 16)
            for s in range(NS):
                p = s % 2
                if "q" in _parts:
                    k.wg(gp, "pc_eq", DC * (s + 1))
                    sti(k, gp, "pc_st", s, qT_d[:, :, s * 512:(s + 1) * 512],
                        qs[p].ap().rearrange("p (c t) -> p c t", c=DC))
                if "k" in _parts:
                    k.wg(gp, "pc_ek", DC * (s + 1))
                    sti(k, gp, "pc_st", s, kT_d[:, :, s * 512:(s + 1) * 512],
                        ks_[p].ap().rearrange("p (c t) -> p c t", c=DC))
                if "v" in _parts:
                    k.wg(gp, "pc_ev", 4 * (s + 1))
                    sti(k, gp, "pc_st", s,
                        d["v_s"][s * 512:(s + 1) * 512, :].rearrange(
                            "(tt p) e -> p tt e", p=128),
                        vs[p].ap().rearrange("p (tt e) -> p tt e", tt=4))

        @blk.tensor
        def _(te):
            _entry(k, te)
            k.wg(te, "pc_wl", 16 * 3)
            k.wg(te, "pc_wlb", 16 * 3)
            for s in range(NS):
                p = s % 2
                ldw(k, te, "pc_ld", s)
                for wsb, pst, msem, esem in ((wq, psq, "pc_mq", "pc_eq"),
                                             (wk, psk, "pc_mk", "pc_ek"))[
                                                 :("q" in _parts) + ("k" in _parts)]:
                    for ec in range(DC):
                        gq = s * DC + ec
                        bank = ec % 2
                        if gq >= 2:
                            k.wg(te, esem, gq - 1)
                        for dc2 in range(DC):
                            ins = te.matmul(pst[:, bank * 512:(bank + 1) * 512],
                                            wsb[:, dc2 * D + ec * 128:
                                                dc2 * D + (ec + 1) * 128],
                                            xs[p][:, dc2 * 512:(dc2 + 1) * 512],
                                            start=(dc2 == 0),
                                            stop=(dc2 == DC - 1))
                        k.inc(ins, msem)
                for tt in range(4 if "v" in _parts else 0):
                    gv = 4 * s + tt
                    bank = tt % 2
                    if gv >= 2:
                        k.wg(te, "pc_ev", gv - 1)
                    for dc2 in range(DC):
                        for half in range(2):
                            ins = te.matmul(
                                psv[:, bank * D + half * 512:
                                    bank * D + (half + 1) * 512],
                                xs[p][:, dc2 * 512 + tt * 128:
                                      dc2 * 512 + (tt + 1) * 128],
                                wv[:, dc2 * D + half * 512:
                                   dc2 * D + (half + 1) * 512],
                                start=(dc2 == 0), stop=(dc2 == DC - 1))
                    k.inc(ins, "pc_mv")

        @blk.scalar
        def _(sc):
            _entry(k, sc)
            for s in range(NS):
                p = s % 2
                stw(k, sc, "pc_st", s,
                    n=16 * (("q" in _parts) + ("k" in _parts) + ("v" in _parts)))
                for pst, bcol_, out, msem, esem in (
                        (psq, bq, qs, "pc_mq", "pc_eq"),
                        (psk, bk, ks_, "pc_mk", "pc_ek"))[
                            :("q" in _parts) + ("k" in _parts)]:
                    for ec in range(DC):
                        bank = ec % 2
                        k.wg(sc, msem, s * DC + ec + 1)
                        k.inc(sc.activation(out[p][:, ec * 512:(ec + 1) * 512],
                                            pst[:, bank * 512:(bank + 1) * 512],
                                            AF.Identity,
                                            bias=bcol_[:, ec:ec + 1]), esem)

        @blk.vector
        def _(v):
            _entry(k, v)
            for s in range(NS):
                p = s % 2
                stw(k, v, "pc_st", s,
                    n=16 * (("q" in _parts) + ("k" in _parts) + ("v" in _parts)))
                for tt in range(4 if "v" in _parts else 0):
                    bank = tt % 2
                    k.wg(v, "pc_mv", 4 * s + tt + 1)
                    k.inc(v.tensor_tensor(vs[p][:, tt * D:(tt + 1) * D],
                                          psv[:, bank * D:(bank + 1) * D],
                                          bv[:], ALU.add), "pc_ev")



def _pb1_slabs():
    out = []
    for seq in range(B_LOC):
        s0 = seq * S
        a = s0
        while a < s0 + S:
            n_out = min(510, s0 + S - a)
            zl = (a == s0)
            a0 = a if zl else a - 1
            a1 = min(a + n_out + 1, s0 + S)
            zr = (a1 == s0 + S) and (a + n_out == s0 + S)
            n_in = a1 - a0
            ofs = 1 if zl else 0
            out.append((seq, a, n_out, a0, n_in, ofs, zl, zr))
            a += n_out
    return out


def emit_PB1(k):
    nc, d, g = k.nc, k.dram, k.g
    k.declare_stores("pb_hqst", "pb_rscst")
    slabs = _pb1_slabs()
    NS = len(slabs)
    G = NS * HC
    chl = [(s, hc) + slabs[s] for s in range(NS) for hc in range(HC)]
    with ExitStack() as es:
        sb = lambda n, s, t: es.enter_context(nc.sbuf_tensor(n, s, t))
        w1 = sb("pb_w1", [128, DC * H], BF16)
        cwv = sb("pb_cw", [128, HC * 3], F32)
        cbv = sb("pb_cb", [128, HC], F32)
        xqs = [sb("pb_xq0", [128, DC * 512], BF16),
               sb("pb_xq1", [128, DC * 512], BF16)]
        cacc = [sb(f"pb_ca{i}", [128, 512], F32) for i in range(3)]
        t2m = [sb(f"pb_t2{i}", [128, 512], F32) for i in range(3)]
        sil = sb("pb_sil", [128, HC * 512], F32)
        asb = [sb("pb_ab0", [128, 512], BF16), sb("pb_ab1", [128, 512], BF16)]
        mx = sb("pb_mx", [128, 512], BF16)
        mt4 = sb("pb_mt4", [128, 8], F32)
        rsc4 = [sb("pb_rs0", [128, 4], F32), sb("pb_rs1", [128, 4], F32)]
        ng4 = sb("pb_ng4", [128, 4], F32)
        srow = sb("pb_srow", [1, 512], F32)
        srep = sb("pb_srep", [128, 512], F32)
        qtile = [sb("pb_qt0", [128, 512], F32), sb("pb_qt1", [128, 512], F32)]
        k0 = [sb("pb_k00", [128, 512], I32), sb("pb_k01", [128, 512], I32)]
        ddt = sb("pb_dd", [128, 512], BF16)
        aat = sb("pb_aa", [128, 512], BF16)
        hqs = sb("pb_hq", [128, HC * 512], BF16)
        psH = es.enter_context(nc.psum_tensor("pb_psH", [128, 2048], F32))
        psX = es.enter_context(nc.psum_tensor("pb_psX", [128, 512], BF16))
        psR = es.enter_context(nc.psum_tensor("pb_psR", [128, 512], F32))
        blk = es.enter_context(nc.Block())
        xqT_d = d["xqT_s"].rearrange("(c p) t -> p c t", p=128)
        hq_d = d["hq_s"].rearrange("(c p) t -> p c t", p=128)

        def hq_target(p, qr):
            return 16 * ((p // 2) * 4 + qr + 1)

        def rsc_target(p):
            n = 0
            for p2 in range(p % 2, p + 1, 2):
                n += (slabs[p2][2] + 127) // 128
            return 16 * n

        @blk.sync
        def _(sync):
            _entry(k, sync)
            k.inc(sync.dma_start(
                out=w1.ap().rearrange("p (c h) -> p c h", c=DC),
                in_=d["w1qT_s"].rearrange("(c p) h -> p c h", p=128)),
                "pb_wl", 16)
            k.inc(sync.dma_start(out=cwv[:], in_=d["convw"][:]), "pb_wl", 16)
            k.inc(sync.dma_start(out=cbv[:], in_=d["convb"][:]), "pb_wl", 16)

            def stores(p):
                a, n_out = slabs[p][1], slabs[p][2]
                for qr in range(4):
                    k.wg(sync, "pb_qc", 32 * p + 8 * (qr + 1))
                    k.inc(sync.dma_start(
                        out=hq_d[:, qr * 8:qr * 8 + 8, a:a + n_out],
                        in_=bass.AP(hqs, qr * 8 * 512,
                                    [[HC * 512, 128], [512, 8], [1, n_out]])),
                        f"pb_hqst{p % 2}", 16)
                    assert k.cnt[f"pb_hqst{p % 2}"] == hq_target(p, qr)
                k.wg(sync, "pb_rs", p + 1)
                ngrp = (n_out + 127) // 128
                for gi in range(ngrp):
                    w = min(128, n_out - gi * 128)
                    k.inc(sync.dma_start(
                        out=d["rsc_s"][a + gi * 128:a + gi * 128 + w, 0:1],
                        in_=rsc4[p % 2][0:w, gi:gi + 1]),
                        f"pb_rscst{p % 2}", 16)
                assert k.cnt[f"pb_rscst{p % 2}"] == rsc_target(p)

            for sl, (seq, a, n_out, a0, n_in, ofs, zl, zr) in enumerate(slabs):
                if sl >= 2:
                    k.wg(sync, "pb_mm", 32 * (sl - 1))
                ldi(k, sync, "pb_ld", sl,
                    bass.AP(xqs[sl % 2], 0,
                            [[DC * 512, 128], [512, DC], [1, n_in]]),
                    xqT_d[:, :, a0:a0 + n_in])
                if sl >= 2:
                    stores(sl - 2)
            stores(NS - 2)
            stores(NS - 1)

        @blk.tensor
        def _(te):
            _entry(k, te)
            k.wg(te, "pb_wl", 48)
            for s in range(NS + 1):

                def mchunks(rng):
                    seq, a, n_out, a0, n_in, ofs, zl, zr = slabs[s]
                    for hc in rng:
                        ghc = s * HC + hc
                        bank = hc % 4
                        if ghc >= 4:
                            k.wg(te, "pb_cv", ghc - 3)
                        for dc2 in range(DC):
                            ins = te.matmul(
                                psH[:, bank * 512 + ofs:bank * 512 + ofs + n_in],
                                w1[:, dc2 * H + hc * 128:dc2 * H + (hc + 1) * 128],
                                xqs[s % 2][:, dc2 * 512:dc2 * 512 + n_in],
                                start=(dc2 == 0), stop=(dc2 == DC - 1))
                        k.inc(ins, "pb_mm")

                if s < NS:
                    ldw(k, te, "pb_ld", s)
                    mchunks(range(0, 4))
                if s >= 1:
                    p = s - 1
                    n_out = slabs[p][2]
                    ngrp = (n_out + 127) // 128
                    k.wg(te, "pb_mx", 32 * (p + 1))
                    if p >= 1:
                        k.wg(te, "pb_s1", p)
                    for gi in range(ngrp):
                        w = min(128, n_out - gi * 128)
                        ins = te.transpose(psX[0:w, gi * 128:gi * 128 + 128],
                                           mx[:, gi * 128:gi * 128 + w],
                                           g["idenb"][:])
                    k.inc(ins, "pb_tx")
                    k.wg(te, "pb_ng", p + 1)
                    if p >= 1:
                        k.wg(te, "pb_s3", p)
                    for gi in range(ngrp):
                        ins = te.transpose(psR[0:1, gi * 128:(gi + 1) * 128],
                                           ng4[:, gi:gi + 1], g["iden32"][:])
                    k.inc(ins, "pb_tr")
                    k.wg(te, "pb_s2", p + 1)
                    k.inc(te.matmul(psR[:, 0:n_out], g["onesr32"][0:1, :],
                                    srow[0:1, 0:n_out], start=True, stop=True),
                          "pb_rp")
                if s < NS:
                    mchunks(range(4, HC))

        @blk.scalar
        def _(sc):
            _entry(k, sc)
            for gg in range(G + 2):
                if gg < G:
                    s, hc, seq, a, n_out, a0, n_in, ofs, zl, zr = chl[gg]
                    bank = hc % 4
                    k.wg(sc, "pb_mm", gg + 1)
                    if zl:
                        e = k.inc(sc.memzero(psH[:, bank * 512:bank * 512 + 1]),
                                  "pb_ae")
                        k.wg(sc, "pb_ae", e)
                    if zr:
                        e = k.inc(sc.memzero(
                            psH[:, bank * 512 + ofs + n_in:
                                bank * 512 + ofs + n_in + 1]), "pb_ae")
                        k.wg(sc, "pb_ae", e)
                    if gg >= 3:
                        k.wg(sc, "pb_p2", gg - 2)
                        k.wg(sc, "pb_cv", gg - 2)
                    k.inc(sc.activation(cacc[gg % 3][:, 0:n_out],
                                        psH[:, bank * 512 + 1:bank * 512 + 1 + n_out],
                                        AF.Identity,
                                        scale=cwv[:, hc * 3 + 1:hc * 3 + 2]),
                          "pb_t1")
                    k.inc(sc.activation(t2m[gg % 3][:, 0:n_out],
                                        psH[:, bank * 512:bank * 512 + n_out],
                                        AF.Identity,
                                        scale=cwv[:, hc * 3:hc * 3 + 1]),
                          "pb_t2")
                if gg >= 2:
                    s2, hc2, seq, a, n_out, a0, n_in, ofs, zl, zr = chl[gg - 2]
                    k.wg(sc, "pb_cv", gg - 1)
                    if gg - 33 >= 1:
                        k.wg(sc, "pb_qf", gg - 33)
                    k.inc(sc.activation(sil[:, hc2 * 512:hc2 * 512 + n_out],
                                        cacc[(gg - 2) % 3][:, 0:n_out], AF.Silu,
                                        bias=cbv[:, hc2:hc2 + 1]), "pb_sl")
                    k.wg(sc, "pb_sl", k.cnt["pb_sl"])
                    if gg >= 4:
                        k.wg(sc, "pb_mx", gg - 3)
                    k.inc(sc.activation(asb[hc2 % 2][:, 0:n_out],
                                        sil[:, hc2 * 512:hc2 * 512 + n_out],
                                        AF.Abs), "pb_ab")

        @blk.gpsimd
        def _(gp):
            _entry(k, gp)
            for s in range(NS + 1):
                for hc in range(HC):
                    if s < NS:
                        ghc = s * HC + hc
                        k.wg(gp, "pb_t1", ghc + 1)
                        k.wg(gp, "pb_t2", ghc + 1)
                        n_out = slabs[s][2]
                        ghc = s * HC + hc
                        k.inc(gp.tensor_tensor(cacc[ghc % 3][:, 0:n_out],
                                               cacc[ghc % 3][:, 0:n_out],
                                               t2m[ghc % 3][:, 0:n_out],
                                               ALU.add), "pb_p2")
                    if s >= 1:
                        p = s - 1
                        n_out = slabs[p][2]
                        if hc == 0:
                            k.wg(gp, "pb_s3", s)
                        k.wg(gp, "pb_sl", 32 * p + hc + 1)
                        if 32 * p + hc - 1 >= 1:
                            k.wg(gp, "pb_qc", 32 * p + hc - 1)
                        k.inc(gp.tensor_tensor(qtile[hc % 2][:, 0:n_out],
                                               sil[:, hc * 512:hc * 512 + n_out],
                                               srep[:, 0:n_out], ALU.mult),
                              "pb_qf")
                        k.wg(gp, "pb_qf", k.cnt["pb_qf"])
                        k.inc(gp.tensor_copy(k0[hc % 2][:, 0:n_out],
                                             qtile[hc % 2][:, 0:n_out]), "pb_k0")

        @blk.vector
        def _(v):
            _entry(k, v)
            v.memset(mt4[:], 0.0)
            e = k.inc(v.memset(ng4[:], 0.0), "pb_e")
            k.wg(v, "pb_e", e)

            def repair(p, hc):
                n_out = slabs[p][2]
                if hc == 0 and p >= 1:
                    k.wg(v, f"pb_hqst{(p - 1) % 2}", hq_target(p - 1, 3))
                k.wg(v, "pb_k0", 32 * p + hc + 1)
                k.wg(v, "pb_qc", k.cnt["pb_qc"])
                e = k.inc(v.scalar_tensor_tensor(ddt[:, 0:n_out],
                                                 k0[hc % 2][:, 0:n_out], -1.0,
                                                 qtile[hc % 2][:, 0:n_out],
                                                 ALU.mult, ALU.add), "pb_e")
                k.wg(v, "pb_e", e)
                e = k.inc(v.scalar_tensor_tensor(aat[:, 0:n_out],
                                                 ddt[:, 0:n_out], 0.5,
                                                 k0[hc % 2][:, 0:n_out],
                                                 ALU.is_gt, ALU.add), "pb_e")
                k.wg(v, "pb_e", e)
                k.inc(v.scalar_tensor_tensor(
                    hqs[:, hc * 512:hc * 512 + n_out], ddt[:, 0:n_out],
                    -0.5, aat[:, 0:n_out], ALU.is_lt, ALU.subtract), "pb_qc")

            def mxop(j):
                pj, hcj = j // HC, j % HC
                n_outj = slabs[pj][2]
                k.wg(v, "pb_ab", j + 1)
                if hcj == 0:
                    k.wg(v, "pb_tx", pj)
                    ins = v.tensor_copy(mx[:, 0:n_outj], asb[0][:, 0:n_outj])
                else:
                    ins = v.tensor_tensor(mx[:, 0:n_outj],
                                          asb[hcj % 2][:, 0:n_outj],
                                          mx[:, 0:n_outj], ALU.max)
                k.inc(ins, "pb_mx")
                k.wg(v, "pb_mx", k.cnt["pb_mx"])

            for s in range(NS + 1):
                if s >= 1:
                    mxop(s * HC - 2)
                    mxop(s * HC - 1)
                    p = s - 1
                    n_out = slabs[p][2]
                    ngrp = (n_out + 127) // 128
                    k.wg(v, "pb_tx", s)
                    for gi in range(ngrp):
                        w = min(128, n_out - gi * 128)
                        ins = v.tensor_reduce(mt4[0:w, gi:gi + 1],
                                              psX[0:w, gi * 128:(gi + 1) * 128],
                                              AX.X, ALU.max)
                        if gi == ngrp - 1:
                            k.inc(ins, "pb_s1")
                            k.wg(v, "pb_s1", p + 1)
                        else:
                            e = k.inc(ins, "pb_e")
                            k.wg(v, "pb_e", e)
                    e = k.inc(v.tensor_scalar(mt4[:, 4:4 + ngrp], mt4[:, 0:ngrp],
                                              1e-5, None, ALU.max), "pb_e")
                    k.wg(v, "pb_e", e)
                    if p >= 2:
                        v.wait_ge(k.sem(f"pb_rscst{(p - 2) % 2}"),
                                  rsc_target(p - 2))
                    e = k.inc(v.tensor_scalar(rsc4[p % 2][:, 0:ngrp],
                                              mt4[:, 4:4 + ngrp],
                                              R127, None, ALU.mult), "pb_rs")
                    k.wg(v, "pb_rs", e)
                    if p >= 1:
                        k.wg(v, "pb_tr", p)
                    e = k.inc(v.reciprocal(ng4[:, 0:ngrp], mt4[:, 4:4 + ngrp]),
                              "pb_e")
                    k.wg(v, "pb_e", e)
                    k.inc(v.tensor_scalar(ng4[:, 0:ngrp], ng4[:, 0:ngrp],
                                          -127.0, None, ALU.mult), "pb_ng")
                    k.wg(v, "pb_ng", p + 1)
                    k.wg(v, "pb_tr", p + 1)
                    k.inc(v.tensor_copy(srow[0:1, 0:ngrp * 128],
                                        psR[0:1, 0:ngrp * 128]), "pb_s2")
                    k.wg(v, "pb_s2", p + 1)
                    k.wg(v, "pb_rp", p + 1)
                    if p >= 1:
                        k.wg(v, "pb_qf", 32 * p)
                    k.inc(v.tensor_copy(srep[:, 0:n_out], psR[:, 0:n_out]),
                          "pb_s3")
                    k.wg(v, "pb_s3", k.cnt["pb_s3"])
                if s < NS:
                    n_out = slabs[s][2]
                    for hc in range(HC):
                        ghc = s * HC + hc
                        bank = hc % 4
                        k.wg(v, "pb_p2", ghc + 1)
                        k.inc(v.scalar_tensor_tensor(
                            cacc[ghc % 3][:, 0:n_out],
                            psH[:, bank * 512 + 2:bank * 512 + 2 + n_out],
                            cwv[:, hc * 3 + 2:hc * 3 + 3],
                            cacc[ghc % 3][:, 0:n_out], ALU.mult, ALU.add),
                            "pb_cv")
                        if hc >= 2:
                            mxop(ghc - 2)
                        if s >= 1 and hc >= 1:
                            repair(s - 1, hc - 1)
                    if s >= 1:
                        repair(s - 1, HC - 1)
                else:
                    for hc in range(HC):
                        repair(s - 1, hc)



def emit_PB2(k):
    nc, d, g = k.nc, k.dram, k.g
    k.declare_stores("p2_st")
    with ExitStack() as es:
        sb = lambda n, s, t: es.enter_context(nc.sbuf_tensor(n, s, t))
        w2 = sb("p2_w2", [128, HC * D], BF16)
        hqt = [sb("p2_h0", [128, HC * 128], BF16), sb("p2_h1", [128, HC * 128], BF16)]
        rsc = [sb("p2_r0", [128, 1], F32), sb("p2_r1", [128, 1], F32)]
        gt = [sb("p2_g0", [128, D], F32), sb("p2_g1", [128, D], F32)]
        ps = es.enter_context(nc.psum_tensor("p2_ps", [128, 2 * D], F32))
        blk = es.enter_context(nc.Block())
        hq_d = d["hq_s"].rearrange("(c p) t -> p c t", p=128)

        @blk.sync
        def _(sync):
            _entry(k, sync)
            k.inc(sync.dma_start(
                out=w2.ap().rearrange("p (c e) -> p c e", c=HC),
                in_=d["w2qT_s"].rearrange("(c p) e -> p c e", p=128)),
                "p2_wl", 16)
            for i in range(NT):
                if i >= 2:
                    k.wg(sync, "p2_mm", i - 1)
                ldi(k, sync, "p2_ld", i,
                    hqt[i % 2].ap().rearrange("p (c t) -> p c t", c=HC),
                    hq_d[:, :, i * 128:(i + 1) * 128])
                if i >= 2:
                    k.wg(sync, "p2_ev", 2 * (i - 1))
                ldi(k, sync, "p2_ld", i, rsc[i % 2][:],
                    d["rsc_s"][i * 128:(i + 1) * 128, :])

        @blk.tensor
        def _(te):
            _entry(k, te)
            k.wg(te, "p2_wl", 16)
            for i in range(NT):
                p = i % 2
                bank = i % 2
                ldw(k, te, "p2_ld", i, n=32)
                if i >= 2:
                    k.wg(te, "p2_ev", 2 * (i - 1))
                for hc in range(HC):
                    for half in range(2):
                        ins = te.matmul(
                            ps[:, bank * D + half * 512:bank * D + (half + 1) * 512],
                            hqt[p][:, hc * 128:(hc + 1) * 128],
                            w2[:, hc * D + half * 512:hc * D + (half + 1) * 512],
                            start=(hc == 0), stop=(hc == HC - 1))
                k.inc(ins, "p2_mm")

        @blk.scalar
        def _(sc):
            _entry(k, sc)
            for i in range(NT):
                p = i % 2
                bank = i % 2
                k.wg(sc, "p2_mm", i + 1)
                ldw(k, sc, "p2_ld", i, n=32)
                stw(k, sc, "p2_st", i)
                for half in range(2):
                    ins = sc.activation(gt[p][:, half * 512:(half + 1) * 512],
                                        ps[:, bank * D + half * 512:
                                           bank * D + (half + 1) * 512],
                                        AF.Silu, scale=rsc[p][:, 0:1])
                k.inc(ins, "p2_ev", 2)

        @blk.gpsimd
        def _(gp):
            _entry(k, gp)
            for i in range(NT):
                p = i % 2
                k.wg(gp, "p2_ev", 2 * (i + 1))
                sti(k, gp, "p2_st", i, d["g_s"][i * 128:(i + 1) * 128, :],
                    gt[p][:])


def _nB(j):
    n = 0
    for t in range(j + 1):
        if (t % 16) + 1 > 8:
            n += 1
    return n


def emit_PD(k):
    nc, d, g = k.nc, k.dram, k.g
    k.declare_stores("pd_stx1", "pd_stn", "pd_stnT")
    with ExitStack() as es:
        sb = lambda n, s, t: es.enter_context(nc.sbuf_tensor(n, s, t))
        kTb = [sb("pd_kT0", [128, DC * S], BF16), sb("pd_kT1", [128, DC * S], BF16)]
        vb = [sb("pd_v0", [128, 16 * D], BF16), sb("pd_v1", [128, 16 * D], BF16)]
        qT = [sb("pd_q0", [128, DC * 128], BF16), sb("pd_q1", [128, DC * 128], BF16)]
        gtl = [sb(f"pd_g{i}", [128, D], F32) for i in range(3)]
        xtl = [sb(f"pd_xx{i}", [128, D], F32) for i in range(3)]
        ptile = [sb("pd_p0", [128, S], BF16), sb("pd_p1", [128, S], BF16)]
        pT = [sb("pd_pT0", [128, S], BF16), sb("pd_pT1", [128, S], BF16)]
        stt = [sb("pd_st0", [128, 16], F32), sb("pd_st1", [128, 16], F32)]
        t_sb = sb("pd_t", [128, D], F32)
        x1sb = [sb(f"pd_x1{i}", [128, D], F32) for i in range(3)]
        sq = sb("pd_sq", [128, D], F32)
        epsc = sb("pd_eps", [128, 1], F32)
        xn1 = [sb("pd_n0", [128, D], BF16), sb("pd_n1", [128, D], BF16)]
        xn1T = [sb("pd_nT0", [128, D], BF16), sb("pd_nT1", [128, D], BF16)]
        psS = es.enter_context(nc.psum_tensor("pd_psS", [128, 2048], F32))
        psT = es.enter_context(nc.psum_tensor("pd_psT", [128, 1024], BF16))
        psN = es.enter_context(nc.psum_tensor("pd_psN", [128, 1024], BF16))
        psC = es.enter_context(nc.psum_tensor("pd_psC", [128, D], F32))
        blk = es.enter_context(nc.Block())
        qT_d = d["qT_s"].rearrange("(c p) t -> p c t", p=128)
        kT_d = d["kT_s"].rearrange("(c p) t -> p c t", p=128)
        xn1T_d = d["xn1T_s"].rearrange("(c p) t -> p c t", p=128)

        def nkb_of(j):
            return (j % 16) + 1

        @blk.sync
        def _(sync):
            _entry(k, sync)
            for b in range(B_LOC):
                if b >= 2:
                    k.wg(sync, "pd_cx", 16 * (b - 1))
                ldi(k, sync, "pd_ldk", b,
                    kTb[b % 2].ap().rearrange("p (c t) -> p c t", c=DC),
                    kT_d[:, :, b * S:(b + 1) * S])
                ldi(k, sync, "pd_ldk", b,
                    vb[b % 2].ap().rearrange("p (kc e) -> p kc e", kc=16),
                    d["v_s"][b * S:(b + 1) * S, :].rearrange(
                        "(kc p) e -> p kc e", p=128))
                for qi in range(16):
                    i = b * 16 + qi
                    if i >= 2:
                        k.wg(sync, "pd_mm", i - 1)
                    ldi(k, sync, "pd_ldq", i,
                        qT[i % 2].ap().rearrange("p (c t) -> p c t", c=DC),
                        qT_d[:, :, i * 128:(i + 1) * 128])
                    if i >= 3:
                        k.wg(sync, "pd_x1p", i - 2)
                    k.inc(sync.dma_start(out=gtl[i % 3][:],
                                         in_=d["g_s"][i * 128:(i + 1) * 128, :]),
                          f"pd_ldg{i % 3}", 16)
                    k.inc(sync.dma_start(out=xtl[i % 3][:],
                                         in_=d["x"][i * 128:(i + 1) * 128, :]),
                          f"pd_ldg{i % 3}", 16)

        @blk.tensor
        def _(te):
            _entry(k, te)
            for i in range(NT + 6):
                if i < NT:
                    b, qi = i // 16, i % 16
                    pb, p = b % 2, i % 2
                    jd = qi // 4
                    wd = (qi % 4 + 1) * 128
                    if qi == 0:
                        ldw(k, te, "pd_ldk", b, n=32)
                    ldw(k, te, "pd_ldq", i)
                    if i >= 1:
                        k.wg(te, "pd_exp", i)
                        k.wg(te, "pd_mmax", i)
                    for j in range(jd + 1):
                        wj = 512 if j < jd else wd
                        for dc2 in range(DC):
                            ins = te.matmul(
                                psS[:, j * 512:j * 512 + wj],
                                qT[p][:, dc2 * 128:(dc2 + 1) * 128],
                                kTb[pb][:, dc2 * S + j * 512:dc2 * S + j * 512 + wj],
                                start=(dc2 == 0),
                                stop=(dc2 == DC - 1 and j != jd))
                        if j == jd:
                            ins = te.matmul(psS[:, j * 512 + wd - 128:j * 512 + wd],
                                            g["idenb"][:], g["cmaskb"][:],
                                            start=False, stop=True)
                    k.inc(ins, "pd_mm")
                if 1 <= i <= NT:
                    j = i - 1
                    cA = min(nkb_of(j), 8)
                    k.wg(te, "pd_exp", j + 1)
                    if j >= 1:
                        k.wg(te, "pd_pvB", _nB(j - 1))
                        k.wg(te, "pd_pvA", j)
                    for kc in range(cA):
                        ins = te.transpose(psT[:, kc * 128:(kc + 1) * 128],
                                           ptile[j % 2][:, kc * 128:(kc + 1) * 128],
                                           g["idenb"][:])
                    k.inc(ins, "pd_tpA")
                if 5 <= i <= NT + 4:
                    j = i - 5
                    k.wg(te, "pd_xn1", j + 1)
                    if j >= 1:
                        k.wg(te, "pd_ntev", j)
                    for c in range(DC):
                        ins = te.transpose(psN[:, c * 128:(c + 1) * 128],
                                           xn1[j % 2][:, c * 128:(c + 1) * 128],
                                           g["idenb"][:])
                    k.inc(ins, "pd_tn")
                if 1 <= i <= NT:
                    j = i - 1
                    if nkb_of(j) > 8:
                        k.wg(te, "pd_pvA", j + 1)
                        for kc in range(8, nkb_of(j)):
                            ins = te.transpose(
                                psT[:, (kc - 8) * 128:(kc - 7) * 128],
                                ptile[j % 2][:, kc * 128:(kc + 1) * 128],
                                g["idenb"][:])
                        k.inc(ins, "pd_tpB")
                if 2 <= i <= NT + 1:
                    m = i - 2
                    bm = m // 16
                    nkb = nkb_of(m)
                    k.wg(te, "pd_pvA", m + 1)
                    k.wg(te, "pd_pvB", _nB(m))
                    if m >= 1:
                        k.wg(te, "pd_gm", m)
                    for kc in range(nkb):
                        for half in range(2):
                            ins = te.matmul(
                                psC[:, half * 512:(half + 1) * 512],
                                pT[m % 2][:, kc * 128:(kc + 1) * 128],
                                vb[bm % 2][:, kc * D + half * 512:
                                           kc * D + (half + 1) * 512],
                                start=(kc == 0), stop=(kc == nkb - 1))
                    k.inc(ins, "pd_cx")

        @blk.vector
        def _(v):
            _entry(k, v)
            e = k.inc(v.memset(epsc[:], 1e-5), "pd_einit")
            for i in range(NT + 6):
                if i < NT:
                    qi = i % 16
                    nk = 512 * (qi // 4) + (qi % 4 + 1) * 128
                    st_ = stt[i % 2]
                    k.wg(v, "pd_mm", i + 1)
                    if i >= 2:
                        k.wg(v, "pd_exp", i - 1)
                    e = k.inc(v.tensor_reduce(st_[:, 0:1], psS[:, 0:nk], AX.X,
                                              ALU.max), "pd_e")
                    k.wg(v, "pd_e", e)
                    k.inc(v.tensor_scalar(st_[:, 1:2], st_[:, 0:1], -1.0 / 32.0,
                                          None, ALU.mult), "pd_mmax")
                if 2 <= i <= NT + 1:
                    m = i - 2
                    if m >= 2:
                        k.wg(v, "pd_gm", m - 1)
                    e = k.inc(v.reciprocal(stt[m % 2][:, 3:4],
                                           stt[m % 2][:, 2:3]), "pd_rs")
                    k.wg(v, "pd_rs", e)
                if 3 <= i <= NT + 2:
                    m = i - 3
                    k.wg(v, "pd_cx", m + 1)
                    k.wg(v, f"pd_ldg{m % 3}", 32 * (m // 3 + 1))
                    if m >= 1:
                        k.wg(v, "pd_x1p", m)
                    k.inc(v.scalar_tensor_tensor(t_sb[:], psC[:],
                                                 stt[m % 2][:, 3:4],
                                                 gtl[m % 3][:],
                                                 ALU.mult, ALU.mult), "pd_gm")
                if 1 <= i <= NT:
                    j = i - 1
                    cA = min(nkb_of(j), 8)
                    k.wg(v, "pd_tpA", j + 1)
                    if j >= 2:
                        k.wg(v, "pd_cx", j - 1)
                    k.inc(v.tensor_copy(pT[j % 2][:, 0:cA * 128],
                                        psT[:, 0:cA * 128]), "pd_pvA")
                if 1 <= i <= NT:
                    j = i - 1
                    if nkb_of(j) > 8:
                        k.wg(v, "pd_tpB", _nB(j))
                        k.inc(v.tensor_copy(
                            pT[j % 2][:, 1024:nkb_of(j) * 128],
                            psT[:, 0:(nkb_of(j) - 8) * 128]), "pd_pvB")
                if 5 <= i <= NT + 4:
                    j = i - 5
                    k.wg(v, "pd_tn", j + 1)
                    stw(k, v, "pd_stnT", j)
                    k.inc(v.tensor_copy(xn1T[j % 2][:], psN[:]), "pd_ntev")

        @blk.gpsimd
        def _(gp):
            _entry(k, gp)
            for i in range(NT + 6):
                if 4 <= i <= NT + 3:
                    m = i - 4
                    k.wg(gp, "pd_gm", m + 1)
                    k.wg(gp, f"pd_ldg{m % 3}", 32 * (m // 3 + 1))
                    stw(k, gp, "pd_stx1", m)
                    if m >= 3:
                        k.wg(gp, "pd_xn1", m - 2)
                    k.inc(gp.tensor_tensor(x1sb[m % 3][:], t_sb[:],
                                           xtl[m % 3][:], ALU.add), "pd_x1p")

        @blk.scalar
        def _(sc):
            _entry(k, sc)

            def ae(ins):
                e = k.inc(ins, "pd_ae")
                k.wg(sc, "pd_ae", e)

            for i in range(NT + 6):
                if i < NT:
                    qi = i % 16
                    nk = 512 * (qi // 4) + (qi % 4 + 1) * 128
                    st_ = stt[i % 2]
                    k.wg(sc, "pd_mmax", i + 1)
                    if i >= 2:
                        k.wg(sc, "pd_tpA", i - 1)
                        k.wg(sc, "pd_tpB", _nB(i - 2))
                        k.wg(sc, "pd_rs", i - 1)
                    k.inc(sc.activation(ptile[i % 2][:, 0:nk], psS[:, 0:nk],
                                        AF.Exp, bias=st_[:, 1:2],
                                        scale=1.0 / 32.0,
                                        accum_out=st_[:, 2:3]), "pd_exp")
                if 4 <= i <= NT + 3:
                    j = i - 4
                    st_ = stt[j % 2]
                    k.wg(sc, "pd_x1p", j + 1)
                    if j == 0:
                        k.wg(sc, "pd_einit", 1)
                    ae(sc.activation(sq[:], x1sb[j % 3][:], AF.Identity,
                                     accum_out=st_[:, 4:5]))
                    ae(sc.activation(sq[:], x1sb[j % 3][:], AF.Square,
                                     accum_out=st_[:, 5:6]))
                    sti(k, sc, "pd_stx1", j, d["x1_s"][j * 128:(j + 1) * 128, :],
                        x1sb[j % 3][:])
                    ae(sc.activation(st_[:, 6:7], st_[:, 4:5], AF.Identity,
                                     scale=1.0 / D))
                    ae(sc.activation(st_[:, 7:8], st_[:, 6:7], AF.Square))
                    ae(sc.activation(st_[:, 8:9], st_[:, 7:8], AF.Identity,
                                     scale=-1.0))
                    ae(sc.activation(st_[:, 9:10], st_[:, 5:6], AF.Identity,
                                     scale=1.0 / D, bias=st_[:, 8:9]))
                    ae(sc.activation(st_[:, 10:11], st_[:, 9:10], AF.Ln,
                                     bias=epsc[:, 0:1]))
                    ae(sc.activation(st_[:, 13:14], st_[:, 10:11], AF.Exp,
                                     scale=-0.5))
                    ae(sc.activation(st_[:, 11:12], st_[:, 6:7], AF.Identity,
                                     scale=st_[:, 13:14]))
                    ae(sc.activation(st_[:, 12:13], st_[:, 11:12], AF.Identity,
                                     scale=-1.0))
                    if j >= 2:
                        k.wg(sc, "pd_tn", j - 1)
                    stw(k, sc, "pd_stn", j)
                    k.inc(sc.activation(xn1[j % 2][:], x1sb[j % 3][:],
                                        AF.Identity, bias=st_[:, 12:13],
                                        scale=st_[:, 13:14]), "pd_xn1")
                    k.wg(sc, "pd_xn1", j + 1)
                    sti(k, sc, "pd_stn", j, d["xn1_s"][j * 128:(j + 1) * 128, :],
                        xn1[j % 2][:])
                if 5 <= i <= NT + 4:
                    j = i - 5
                    k.wg(sc, "pd_ntev", j + 1)
                    sti(k, sc, "pd_stnT", j,
                        xn1T_d[:, :, j * 128:(j + 1) * 128],
                        xn1T[j % 2].ap().rearrange("p (c t) -> p c t", c=DC))



def emit_PE1(k):
    nc, d, g = k.nc, k.dram, k.g
    k.declare_stores("pe_st")
    NS = T // 512
    with ExitStack() as es:
        sb = lambda n, s, t: es.enter_context(nc.sbuf_tensor(n, s, t))
        wq = sb("pe_wq", [128, DC * D], BF16)
        wk = sb("pe_wk", [128, DC * D], BF16)
        wv = sb("pe_wv", [128, DC * D], BF16)
        xs = [sb("pe_x0", [128, DC * 512], BF16), sb("pe_x1", [128, DC * 512], BF16)]
        qs = [sb("pe_q0", [128, DC * 512], BF16), sb("pe_q1", [128, DC * 512], BF16)]
        ks_ = [sb("pe_k0", [128, DC * 512], BF16), sb("pe_k1", [128, DC * 512], BF16)]
        vs = [sb("pe_v0", [128, 4 * D], BF16), sb("pe_v1", [128, 4 * D], BF16)]
        psq = es.enter_context(nc.psum_tensor("pe_psq", [128, 1024], F32))
        psk = es.enter_context(nc.psum_tensor("pe_psk", [128, 1024], F32))
        psv = es.enter_context(nc.psum_tensor("pe_psv", [128, 2048], F32))
        blk = es.enter_context(nc.Block())
        xn1T_d = d["xn1T_s"].rearrange("(c p) t -> p c t", p=128)
        q1T_d = d["q1T_s"].rearrange("(c p) t -> p c t", p=128)
        k1T_d = d["k1T_s"].rearrange("(c p) t -> p c t", p=128)

        @blk.sync
        def _(sync):
            _entry(k, sync)
            for s in range(NS):
                if s >= 2:
                    k.wg(sync, "pe_mv", 4 * (s - 1))
                ldi(k, sync, "pe_ld", s,
                    xs[s % 2].ap().rearrange("p (c t) -> p c t", c=DC),
                    xn1T_d[:, :, s * 512:(s + 1) * 512])

        @blk.gpsimd
        def _(gp):
            _entry(k, gp)
            for wsb, wnm in ((wq, "Wql"), (wk, "Wkl"), (wv, "Wvl")):
                k.inc(gp.dma_start(
                    out=wsb.ap().rearrange("p (c e) -> p c e", c=DC),
                    in_=d[wnm].rearrange("(c p) e -> p c e", p=128)), "pe_wl", 16)
            for s in range(NS):
                p = s % 2
                k.wg(gp, "pe_eq", DC * (s + 1))
                sti(k, gp, "pe_st", s, q1T_d[:, :, s * 512:(s + 1) * 512],
                    qs[p].ap().rearrange("p (c t) -> p c t", c=DC))
                k.wg(gp, "pe_ek", DC * (s + 1))
                sti(k, gp, "pe_st", s, k1T_d[:, :, s * 512:(s + 1) * 512],
                    ks_[p].ap().rearrange("p (c t) -> p c t", c=DC))
                k.wg(gp, "pe_ev", 4 * (s + 1))
                sti(k, gp, "pe_st", s,
                    d["v1_s"][s * 512:(s + 1) * 512, :].rearrange(
                        "(tt p) e -> p tt e", p=128),
                    vs[p].ap().rearrange("p (tt e) -> p tt e", tt=4))

        @blk.tensor
        def _(te):
            _entry(k, te)
            k.wg(te, "pe_wl", 16 * 3)
            for s in range(NS):
                p = s % 2
                ldw(k, te, "pe_ld", s)
                for wsb, pst, msem, esem in ((wq, psq, "pe_mq", "pe_eq"),
                                             (wk, psk, "pe_mk", "pe_ek")):
                    for ec in range(DC):
                        gq = s * DC + ec
                        bank = ec % 2
                        if gq >= 2:
                            k.wg(te, esem, gq - 1)
                        for dc2 in range(DC):
                            ins = te.matmul(pst[:, bank * 512:(bank + 1) * 512],
                                            wsb[:, dc2 * D + ec * 128:
                                                dc2 * D + (ec + 1) * 128],
                                            xs[p][:, dc2 * 512:(dc2 + 1) * 512],
                                            start=(dc2 == 0),
                                            stop=(dc2 == DC - 1))
                        k.inc(ins, msem)
                for tt in range(4):
                    gv = 4 * s + tt
                    bank = tt % 2
                    if gv >= 2:
                        k.wg(te, "pe_ev", gv - 1)
                    for dc2 in range(DC):
                        for half in range(2):
                            ins = te.matmul(
                                psv[:, bank * D + half * 512:
                                    bank * D + (half + 1) * 512],
                                xs[p][:, dc2 * 512 + tt * 128:
                                      dc2 * 512 + (tt + 1) * 128],
                                wv[:, dc2 * D + half * 512:
                                   dc2 * D + (half + 1) * 512],
                                start=(dc2 == 0), stop=(dc2 == DC - 1))
                    k.inc(ins, "pe_mv")

        @blk.scalar
        def _(sc):
            _entry(k, sc)
            for s in range(NS):
                p = s % 2
                stw(k, sc, "pe_st", s, n=48)
                for pst, out, msem, esem in ((psq, qs, "pe_mq", "pe_eq"),
                                             (psk, ks_, "pe_mk", "pe_ek")):
                    for ec in range(DC):
                        bank = ec % 2
                        k.wg(sc, msem, s * DC + ec + 1)
                        k.inc(sc.activation(out[p][:, ec * 512:(ec + 1) * 512],
                                            pst[:, bank * 512:(bank + 1) * 512],
                                            AF.Identity), esem)

        @blk.vector
        def _(v):
            _entry(k, v)
            for s in range(NS):
                p = s % 2
                stw(k, v, "pe_st", s, n=48)
                for tt in range(4):
                    bank = tt % 2
                    k.wg(v, "pe_mv", 4 * s + tt + 1)
                    k.inc(v.tensor_copy(vs[p][:, tt * D:(tt + 1) * D],
                                        psv[:, bank * D:(bank + 1) * D]),
                          "pe_ev")





def _b_of(hl):
    return (hl % 2) * 4 + hl // 2


def emit_PF1(k):
    nc, d, g = k.nc, k.dram, k.g
    k.declare_stores("pf_yst", "pf_gin", "pf_gcst")
    NBC = B_LOC * (S // 256)
    with ExitStack() as es:
        sb = lambda n, s, t: es.enter_context(nc.sbuf_tensor(n, s, t))
        wol = sb("pf_wol", [128, DC * D], BF16)
        wg = {nm: sb("pf_" + nm, [128, DC * D], BF16)
              for nm in ("Wqg", "Wkg", "Wvg", "Wog")}
        q1 = [sb("pf_q0", [128, DC * 256], BF16), sb("pf_q1", [128, DC * 256], BF16)]
        k1 = [sb("pf_k0", [128, DC * 256], BF16), sb("pf_k1", [128, DC * 256], BF16)]
        v1 = [sb("pf_v0", [128, 2 * D], BF16), sb("pf_v1", [128, 2 * D], BF16)]
        xn1 = [sb("pf_xn0", [128, 2 * D], BF16), sb("pf_xn1", [128, 2 * D], BF16)]
        p_sb = sb("pf_p", [128, 4096], BF16)
        pT_sb = sb("pf_pT", [128, 2 * 2048], BF16)
        ctx_sb = sb("pf_ctx", [128, D], BF16)
        ctxT_sb = sb("pf_ctxT", [128, D], BF16)
        y_sb = [sb("pf_y0", [128, D], BF16), sb("pf_y1", [128, D], BF16)]
        stt = sb("pf_stt", [128, 24], F32)
        grow = sb("pf_grow", [1, D], F32)
        gin = sb("pf_gin", [32, D], F32)
        ginT = sb("pf_ginT", [128, 8 * 32], BF16)
        qgT = sb("pf_qgT", [64, 16 * 32], BF16)
        kgT = sb("pf_kgT", [64, 16 * 32], BF16)
        vg = sb("pf_vg", [32, D], BF16)
        pg = sb("pf_pg", [32, 512], BF16)
        pgT = sb("pf_pgT", [32, 512], BF16)
        ctxg = sb("pf_ctxg", [32, D], BF16)
        ctxgT = sb("pf_ctxgT", [128, 8 * 32], BF16)
        gst = sb("pf_gst", [32, 24], F32)
        gc_sb = sb("pf_gc", [32, D], F32)
        psS = es.enter_context(nc.psum_tensor("pf_psS", [128, 2048], F32))
        psT = es.enter_context(nc.psum_tensor("pf_psT", [128, 2048], BF16))
        psC = es.enter_context(nc.psum_tensor("pf_psC", [128, 1024], F32))
        blk = es.enter_context(nc.Block())
        q1T_d = d["q1T_s"].rearrange("(c p) t -> p c t", p=128)
        k1T_d = d["k1T_s"].rearrange("(c p) t -> p c t", p=128)

        @blk.sync
        def _(sync):
            _entry(k, sync)
            for bc in range(NBC):
                t0 = bc * 256
                p = bc % 2
                if bc >= 2:
                    k.wg(sync, "pf_mm", 4 * (bc - 1))
                ldi(k, sync, "pf_ld", bc,
                    q1[p].ap().rearrange("p (c t) -> p c t", c=DC),
                    q1T_d[:, :, t0:t0 + 256])
                ldi(k, sync, "pf_ld", bc,
                    k1[p].ap().rearrange("p (c t) -> p c t", c=DC),
                    k1T_d[:, :, t0:t0 + 256])
                if bc >= 2:
                    k.wg(sync, "pf_cx", 2 * (bc - 1))
                ldi(k, sync, "pf_ld", bc,
                    v1[p].ap().rearrange("p (kc e) -> p kc e", kc=2),
                    d["v1_s"][t0:t0 + 256, :].rearrange(
                        "(kc p) e -> p kc e", p=128))
                if bc >= 2:
                    k.wg(sync, "pf_y", 2 * (bc - 1))
                ldi(k, sync, "pf_ld", bc,
                    xn1[p].ap().rearrange("p (tt e) -> p tt e", tt=2),
                    d["xn1_s"][t0:t0 + 256, :].rearrange(
                        "(tt p) e -> p tt e", p=128))
            k.wg(sync, "pf_gin0", 16 * (NBC // 2))
            k.wg(sync, "pf_gin1", 16 * (NBC // 2))
            k.inc(sync.dma_start(out=gin[:], in_=d["gc_s"][:]), "pf_gld", 16)

        @blk.tensor
        def _(te):
            _entry(k, te)
            k.wg(te, "pf_wl", 16 * 5)
            for bc in range(NBC):
                p = bc % 2
                ldw(k, te, "pf_ld", bc, n=64)
                for qs_ in range(2):
                    iq = 2 * bc + qs_
                    for r in range(2):
                        gr = 2 * iq + r
                        if gr >= 1:
                            k.wg(te, "pf_ex", gr)
                        if r == 0 and iq >= 1:
                            k.wg(te, "pf_gn", iq)
                        for hl in range(8):
                            j = (8 * r + hl) % 2
                            ec = (8 * r + hl) // 2
                            cS = (hl % 2) * 1024 + (hl // 2) * 256
                            ins = te.matmul(
                                psS[:, cS:cS + 256],
                                q1[p][64 * j:64 * (j + 1),
                                      ec * 256 + qs_ * 128:
                                      ec * 256 + (qs_ + 1) * 128],
                                k1[p][64 * j:64 * (j + 1),
                                      ec * 256:(ec + 1) * 256],
                                start=True, stop=True,
                                tile_position=(64 * j, 0))
                        k.inc(ins, "pf_mm")
                        k.wg(te, "pf_ex", gr + 1)
                        if gr >= 1:
                            k.wg(te, "pf_pev", gr)
                        if iq >= 1:
                            k.wg(te, "pf_ctev", iq)
                        for b in range(8):
                            for kc in range(2):
                                ins = te.transpose(
                                    psT[:, (b * 2 + kc) * 128:
                                        (b * 2 + kc + 1) * 128],
                                    p_sb[:, r * 2048 + b * 256 + kc * 128:
                                         r * 2048 + b * 256 + (kc + 1) * 128],
                                    g["idenb"][:])
                        k.inc(ins, "pf_tp")
                    k.wg(te, "pf_pev", 2 * iq + 2)
                    if iq >= 1:
                        k.wg(te, "pf_y", iq)
                    for h in range(16):
                        r, hl = h // 8, h % 8
                        b = _b_of(hl)
                        for kc in range(2):
                            ins = te.matmul(
                                psC[:, h * 64:(h + 1) * 64],
                                pT_sb[:, r * 2048 + (b * 2 + kc) * 128:
                                      r * 2048 + (b * 2 + kc + 1) * 128],
                                v1[p][:, kc * D + h * 64:kc * D + (h + 1) * 64],
                                start=(kc == 0), stop=(kc == 1))
                    k.inc(ins, "pf_cx")
                    k.wg(te, "pf_cev", iq + 1)
                    for c in range(DC):
                        ins = te.transpose(psT[:, c * 128:(c + 1) * 128],
                                           ctx_sb[:, c * 128:(c + 1) * 128],
                                           g["idenb"][:])
                    k.inc(ins, "pf_ctt")
                    k.wg(te, "pf_ctev", iq + 1)
                    k.wg(te, "pf_cev", iq + 1)
                    for ec in range(DC):
                        for half in range(2):
                            ins = te.matmul(
                                psC[:, half * 512:(half + 1) * 512],
                                ctxT_sb[:, ec * 128:(ec + 1) * 128],
                                wol[:, ec * D + half * 512:
                                    ec * D + (half + 1) * 512],
                                start=(ec == 0), stop=(ec == DC - 1))
                    k.inc(ins, "pf_wo")
                    k.wg(te, "pf_y", iq + 1)
                    k.wg(te, "pf_ex", 2 * iq + 2)
                    te.matmul(psS[0:1, 0:512], g["onescolb"][:, 0:1],
                              y_sb[qs_][:, 0:512], start=True, stop=True)
                    k.inc(te.matmul(psS[0:1, 512:1024], g["onescolb"][:, 0:1],
                                    y_sb[qs_][:, 512:1024], start=True,
                                    stop=True), "pf_sum")

            k.wg(te, "pf_gld", 16)
            k.wg(te, "pf_y", 2 * NBC)
            k.wg(te, "pf_gn", 2 * NBC)
            for c in range(DC):
                ins = te.transpose(psC[:, c * 32:(c + 1) * 32],
                                   gin[:, c * 128:(c + 1) * 128],
                                   g["iden32"][0:32, 0:32])
            k.inc(ins, "pf_g1")
            k.wg(te, "pf_g2", 1)
            for wnm, ofs in (("Wqg", 0), ("Wkg", 256)):
                for ec in range(DC):
                    for dc2 in range(DC):
                        ins = te.matmul(
                            psS[:, ofs + ec * 32:ofs + (ec + 1) * 32],
                            wg[wnm][:, dc2 * D + ec * 128:dc2 * D + (ec + 1) * 128],
                            ginT[:, dc2 * 32:(dc2 + 1) * 32],
                            start=(dc2 == 0), stop=(dc2 == DC - 1))
            k.inc(ins, "pf_g3")
            for dc2 in range(DC):
                for half in range(2):
                    ins = te.matmul(
                        psS[0:32, 512 + half * 512:1024 + half * 512],
                        ginT[:, dc2 * 32:(dc2 + 1) * 32],
                        wg["Wvg"][:, dc2 * D + half * 512:
                                  dc2 * D + (half + 1) * 512],
                        start=(dc2 == 0), stop=(dc2 == DC - 1))
            k.inc(ins, "pf_g4")
            k.wg(te, "pf_g5", 1)
            for h in range(16):
                te.matmul(psC[0:32, h * 32:(h + 1) * 32],
                          g["idenb"][0:32, 0:32], g["gmaskb"][:],
                          start=True, stop=False)
                ins = te.matmul(psC[0:32, h * 32:(h + 1) * 32],
                                qgT[:, h * 32:(h + 1) * 32],
                                kgT[:, h * 32:(h + 1) * 32],
                                start=False, stop=True)
            k.inc(ins, "pf_g6")
            k.wg(te, "pf_g7", 1)
            for h in range(16):
                ins = te.transpose(psT[0:32, h * 32:(h + 1) * 32],
                                   pg[:, h * 32:(h + 1) * 32],
                                   g["idenb"][0:32, 0:32])
            k.inc(ins, "pf_g8")
            k.wg(te, "pf_g9", 1)
            for h in range(16):
                ins = te.matmul(psC[0:32, h * 64:(h + 1) * 64],
                                pgT[:, h * 32:(h + 1) * 32],
                                vg[:, h * 64:(h + 1) * 64],
                                start=True, stop=True)
            k.inc(ins, "pf_g10")
            k.wg(te, "pf_g11", 1)
            for c in range(DC):
                ins = te.transpose(psT[:, 1024 + c * 32:1024 + (c + 1) * 32],
                                   ctxg[:, c * 128:(c + 1) * 128],
                                   g["idenb"][0:32, 0:32])
            k.inc(ins, "pf_g12")
            k.wg(te, "pf_g13", 1)
            for ec in range(DC):
                for half in range(2):
                    ins = te.matmul(
                        psS[0:32, half * 512:(half + 1) * 512],
                        ctxgT[:, ec * 32:(ec + 1) * 32],
                        wg["Wog"][:, ec * D + half * 512:
                                  ec * D + (half + 1) * 512],
                        start=(ec == 0), stop=(ec == DC - 1))
            k.inc(ins, "pf_g14")

        @blk.vector
        def _(v):
            _entry(k, v)
            for bc in range(NBC):
                p = bc % 2
                for qs_ in range(2):
                    iq = 2 * bc + qs_
                    for r in range(2):
                        gr = 2 * iq + r
                        k.wg(v, "pf_mm", gr + 1)
                        if gr >= 2:
                            k.wg(v, "pf_ex", gr - 1)
                        e = k.inc(v.tensor_reduce(stt[:, 2 * r:2 * r + 1],
                                                  psS[:, 0:2048], AX.X, ALU.max),
                                  "pf_e")
                        k.wg(v, "pf_e", e)
                        k.inc(v.tensor_scalar(stt[:, 2 * r + 1:2 * r + 2],
                                              stt[:, 2 * r:2 * r + 1],
                                              -0.125, None, ALU.mult), "pf_mx")
                        k.wg(v, "pf_ex", gr + 1)
                        k.inc(v.reciprocal(stt[:, 4 + 8 * r:12 + 8 * r],
                                           stt[:, 4 + 8 * r:12 + 8 * r]),
                              "pf_sm")
                    k.wg(v, "pf_cx", iq + 1)
                    k.wg(v, "pf_sm", 2 * iq + 2)
                    if iq >= 1:
                        k.wg(v, "pf_ctt", iq)
                    k.inc(v.tensor_tensor(
                        bass.AP(ctx_sb, 0, [[1024, 128], [64, 16], [1, 64]]),
                        bass.AP(psC, 0, [[1024, 128], [64, 16], [1, 64]]),
                        bass.AP(stt, 4, [[24, 128], [1, 16], [0, 64]]),
                        ALU.mult), "pf_cev")
                    k.wg(v, "pf_ctt", iq + 1)
                    if iq >= 1:
                        k.wg(v, "pf_wo", iq)
                    k.inc(v.tensor_copy(ctxT_sb[:], psT[:, 0:1024]), "pf_ctev")
                    k.wg(v, "pf_wo", iq + 1)
                    ldw(k, v, "pf_ld", bc, n=64)
                    stw(k, v, "pf_yst", iq)
                    if iq >= 2:
                        k.wg(v, "pf_sum", iq - 1)
                    k.inc(v.tensor_tensor(y_sb[qs_][:], psC[:, 0:1024],
                                          xn1[p][:, qs_ * D:(qs_ + 1) * D],
                                          ALU.add), "pf_y")
                    k.wg(v, "pf_sum", iq + 1)
                    if qs_ == 0:
                        stw(k, v, "pf_gin", bc + 1)
                        k.inc(v.tensor_scalar(grow[0:1, :], psS[0:1, 0:1024],
                                              1.0 / 256.0, None, ALU.mult),
                              "pf_gn")
                        k.wg(v, "pf_gn", k.cnt["pf_gn"])
                    else:
                        k.inc(v.scalar_tensor_tensor(grow[0:1, :],
                                                     psS[0:1, 0:1024],
                                                     1.0 / 256.0, grow[0:1, :],
                                                     ALU.mult, ALU.add),
                              "pf_gn")

            k.wg(v, "pf_g1", 1)
            k.inc(v.tensor_copy(ginT[:], psC[:, 0:256]), "pf_g2")
            k.wg(v, "pf_g3", 1)
            for ec in range(DC):
                v.tensor_copy(qgT[0:64, (2 * ec) * 32:(2 * ec + 1) * 32],
                              psS[0:64, ec * 32:(ec + 1) * 32])
                v.tensor_copy(qgT[0:64, (2 * ec + 1) * 32:(2 * ec + 2) * 32],
                              psS[64:128, ec * 32:(ec + 1) * 32])
                v.tensor_copy(kgT[0:64, (2 * ec) * 32:(2 * ec + 1) * 32],
                              psS[0:64, 256 + ec * 32:256 + (ec + 1) * 32])
                v.tensor_copy(kgT[0:64, (2 * ec + 1) * 32:(2 * ec + 2) * 32],
                              psS[64:128, 256 + ec * 32:256 + (ec + 1) * 32])
            k.wg(v, "pf_g4", 1)
            k.inc(v.tensor_copy(vg[:], psS[0:32, 512:1536]), "pf_g5")
            k.wg(v, "pf_g6", 1)
            e = k.inc(v.tensor_reduce(gst[:, 0:1], psC[0:32, 0:512], AX.X,
                                      ALU.max), "pf_e")
            k.wg(v, "pf_e", e)
            k.inc(v.tensor_scalar(gst[:, 1:2], gst[:, 0:1], -0.125, None,
                                  ALU.mult), "pf_gmx")
            k.wg(v, "pf_gex", 1)
            e = k.inc(v.tensor_reduce(
                bass.AP(gst, 4, [[24, 32], [1, 16], [1, 1]]),
                bass.AP(pg, 0, [[512, 32], [32, 16], [1, 32]]),
                AX.X, ALU.add), "pf_e")
            k.wg(v, "pf_e", e)
            e = k.inc(v.reciprocal(gst[:, 4:20], gst[:, 4:20]), "pf_e")
            k.wg(v, "pf_e", e)
            for h in range(16):
                ins = v.tensor_scalar(pg[:, h * 32:(h + 1) * 32],
                                      pg[:, h * 32:(h + 1) * 32],
                                      gst[:, 4 + h:5 + h], None, ALU.mult)
            k.inc(ins, "pf_g7")
            k.wg(v, "pf_g8", 1)
            k.inc(v.tensor_copy(pgT[:], psT[0:32, 0:512]), "pf_g9")
            k.wg(v, "pf_g10", 1)
            k.inc(v.tensor_copy(ctxg[:], psC[0:32, 0:1024]), "pf_g11")
            k.wg(v, "pf_g12", 1)
            k.inc(v.tensor_copy(ctxgT[:], psT[:, 1024:1280]), "pf_g13")
            k.wg(v, "pf_g14", 1)
            k.inc(v.tensor_tensor(gc_sb[:], psS[0:32, 0:1024], gin[:], ALU.add),
                  "pf_g15")

        @blk.scalar
        def _(sc):
            _entry(k, sc)
            for bc in range(NBC):
                for qs_ in range(2):
                    iq = 2 * bc + qs_
                    for r in range(2):
                        gr = 2 * iq + r
                        k.wg(sc, "pf_mx", gr + 1)
                        if gr >= 2:
                            k.wg(sc, "pf_tp", gr - 1)
                        if iq >= 1:
                            k.wg(sc, "pf_cev", iq)
                        for b in range(8):
                            hb = (b % 4) * 2 + b // 4
                            ins = sc.activation(
                                p_sb[:, r * 2048 + b * 256:
                                     r * 2048 + (b + 1) * 256],
                                psS[:, b * 256:(b + 1) * 256], AF.Exp,
                                bias=stt[:, 2 * r + 1:2 * r + 2],
                                scale=0.125,
                                accum_out=stt[:, 4 + 8 * r + hb:
                                              5 + 8 * r + hb])
                        k.inc(ins, "pf_ex")
                        k.wg(sc, "pf_tp", gr + 1)
                        if iq >= 1:
                            k.wg(sc, "pf_cx", iq)
                        k.inc(sc.activation(pT_sb[:, r * 2048:(r + 1) * 2048],
                                            psT[:], AF.Identity), "pf_pev")
            k.wg(sc, "pf_gmx", 1)
            k.inc(sc.activation(pg[:], psC[0:32, 0:512], AF.Exp,
                                bias=gst[:, 1:2], scale=0.125), "pf_gex")

        @blk.gpsimd
        def _(gp):
            _entry(k, gp)
            k.inc(gp.dma_start(
                out=wol.ap().rearrange("p (c e) -> p c e", c=DC),
                in_=d["Wol"].rearrange("(c p) e -> p c e", p=128)), "pf_wl", 16)
            for nm in ("Wqg", "Wkg", "Wvg", "Wog"):
                k.inc(gp.dma_start(
                    out=wg[nm].ap().rearrange("p (c e) -> p c e", c=DC),
                    in_=d[nm].rearrange("(c p) e -> p c e", p=128)), "pf_wl", 16)
            for bc in range(NBC):
                for qs_ in range(2):
                    iq = 2 * bc + qs_
                    k.wg(gp, "pf_y", iq + 1)
                    sti(k, gp, "pf_yst", iq, d["y_s"][iq * 128:(iq + 1) * 128, :],
                        y_sb[qs_][:])
                    if qs_ == 1:
                        k.wg(gp, "pf_gn", iq + 1)
                        sti(k, gp, "pf_gin", bc, d["gc_s"][bc:bc + 1, :],
                            grow[0:1, :])
            k.wg(gp, "pf_g15", 1)
            sti(k, gp, "pf_gcst", 0, d["gc_s"][:], gc_sb[:])



def emit_PG1(k):
    nc, d, g = k.nc, k.dram, k.g
    k.declare_stores("pg_st")
    with ExitStack() as es:
        sb = lambda n, s, t: es.enter_context(nc.sbuf_tensor(n, s, t))
        we = sb("pg_we", [128, DC * D], BF16)
        b1 = sb("pg_b1", [128, D], F32)
        xT = [sb("pg_xT0", [128, DC * 128], BF16), sb("pg_xT1", [128, DC * 128], BF16)]
        yt = [sb("pg_y0", [128, D], BF16), sb("pg_y1", [128, D], BF16)]
        x1t = [sb("pg_x10", [128, D], F32), sb("pg_x11", [128, D], F32)]
        grow = [sb("pg_gr0", [1, D], F32), sb("pg_gr1", [1, D], F32)]
        t1 = sb("pg_t1", [128, D], F32)
        g1 = sb("pg_g1", [128, D], F32)
        s1 = sb("pg_s1", [128, D], F32)
        t2 = sb("pg_t2", [128, D], F32)
        outb = [sb("pg_o0", [128, D], F32), sb("pg_o1", [128, D], F32)]
        psW = es.enter_context(nc.psum_tensor("pg_psW", [128, 2048], F32))
        psR = es.enter_context(nc.psum_tensor("pg_psR", [128, 1024], F32))
        blk = es.enter_context(nc.Block())
        xn1T_d = d["xn1T_s"].rearrange("(c p) t -> p c t", p=128)

        @blk.sync
        def _(sync):
            _entry(k, sync)
            k.inc(sync.dma_start(
                out=we.ap().rearrange("p (c e) -> p c e", c=DC),
                in_=d["weT_s"].rearrange("(c p) e -> p c e", p=128)), "pg_wl", 16)
            k.inc(sync.dma_start(out=b1[:], in_=d["bias1Rep"][:]), "pg_wl", 16)
            for i in range(NT):
                p = i % 2
                if i % 2 == 0:
                    ch = i // 2
                    if ch >= 2:
                        k.wg(sync, "pg_rep", ch - 1)
                    ldi(k, sync, "pg_ldr", ch, grow[ch % 2][:],
                        d["gc_s"][ch:ch + 1, :])
                if i >= 2:
                    k.wg(sync, "pg_mm", i - 1)
                ldi(k, sync, "pg_ldx", i,
                    xT[p].ap().rearrange("p (c t) -> p c t", c=DC),
                    xn1T_d[:, :, i * 128:(i + 1) * 128])
                if i >= 2:
                    k.wg(sync, "pg_s1", i - 1)
                ldi(k, sync, "pg_ldy", i, yt[p][:],
                    d["y_s"][i * 128:(i + 1) * 128, :])
                if i >= 2:
                    k.wg(sync, "pg_out", i - 1)
                ldi(k, sync, "pg_ldy", i, x1t[p][:],
                    d["x1_s"][i * 128:(i + 1) * 128, :])
            k.wg(sync, "pg_st0", 16 * (NT // 2))
            k.wg(sync, "pg_st1", 16 * (NT // 2))

        @blk.tensor
        def _(te):
            _entry(k, te)
            k.wg(te, "pg_wl", 32)
            for i in range(NT):
                p = i % 2
                if i % 2 == 0:
                    ch = i // 2
                    ldw(k, te, "pg_ldr", ch)
                    if ch >= 1:
                        k.wg(te, "pg_s1", 2 * ch)
                    te.matmul(psR[:, 0:512], g["onesr32"][0:1, :],
                              grow[ch % 2][0:1, 0:512], start=True, stop=True)
                    k.inc(te.matmul(psR[:, 512:1024], g["onesr32"][0:1, :],
                                    grow[ch % 2][0:1, 512:1024], start=True,
                                    stop=True), "pg_rep")
                ldw(k, te, "pg_ldx", i)
                if i >= 2:
                    k.wg(te, "pg_t1", i - 1)
                for dc2 in range(DC):
                    for half in range(2):
                        ins = te.matmul(
                            psW[:, p * 1024 + half * 512:
                                p * 1024 + (half + 1) * 512],
                            xT[p][:, dc2 * 128:(dc2 + 1) * 128],
                            we[:, dc2 * D + half * 512:dc2 * D + (half + 1) * 512],
                            start=(dc2 == 0), stop=(dc2 == DC - 1))
                k.inc(ins, "pg_mm")

        @blk.vector
        def _(v):
            _entry(k, v)
            for i in range(NT):
                p = i % 2
                ch = i // 2
                k.wg(v, "pg_mm", i + 1)
                if i >= 1:
                    k.wg(v, "pg_g1", i)
                k.inc(v.tensor_tensor(t1[:], psW[:, p * 1024:(p + 1) * 1024],
                                      b1[:], ALU.add), "pg_t1")
                k.wg(v, "pg_rep", ch + 1)
                ldw(k, v, "pg_ldy", i, n=32)
                if i >= 1:
                    k.wg(v, "pg_out", i)
                k.inc(v.tensor_tensor(s1[:], psR[:, 0:1024], yt[p][:], ALU.add),
                      "pg_s1")

        @blk.scalar
        def _(sc):
            _entry(k, sc)
            for i in range(NT):
                k.wg(sc, "pg_t1", i + 1)
                if i >= 1:
                    k.wg(sc, "pg_out", i)
                k.inc(sc.activation(g1[:], t1[:], AF.Silu), "pg_g1")

        @blk.gpsimd
        def _(gp):
            _entry(k, gp)
            for i in range(NT):
                p = i % 2
                k.wg(gp, "pg_g1", i + 1)
                k.wg(gp, "pg_s1", i + 1)
                e = k.inc(gp.tensor_tensor(t2[:], s1[:], g1[:], ALU.mult),
                          "pg_e")
                k.wg(gp, "pg_e", e)
                stw(k, gp, "pg_st", i)
                k.inc(gp.tensor_tensor(outb[p][:], t2[:], x1t[p][:], ALU.add),
                      "pg_out")
                k.wg(gp, "pg_out", i + 1)
                sti(k, gp, "pg_st", i, d["y_out"][i * 128:(i + 1) * 128, :],
                    outb[p][:])



EMITTERS = {"P0": emit_P0, "PA": emit_PA, "PC": emit_PC, "PB1": emit_PB1,
            "PB2": emit_PB2, "PD": emit_PD, "PE1": emit_PE1, "PF1": emit_PF1,
            "PG1": emit_PG1}


def _group_io(group):
    later = []
    seen = False
    for ph in PHASE_ORDER:
        if ph in group:
            seen = True
            continue
        if seen or ph not in group:
            later.append(ph)
    later = [ph for ph in PHASE_ORDER if ph not in group and
             PHASE_ORDER.index(ph) > max(PHASE_ORDER.index(p) for p in group)]
    later_cons = set()
    for ph in later:
        later_cons |= PHASE_IO[ph][0]
    cons = set()
    prod = set()
    for ph in group:
        cons |= PHASE_IO[ph][0]
        prod |= PHASE_IO[ph][1]
    ext_in = cons - prod
    ext_out = prod & later_cons
    internal = prod - ext_out
    return ext_in, ext_out, internal


def build_program(group, expose_all=False):
    nc = bass.Bass("TRN2", target_bir_lowering=False)
    dram = {}
    for nm, (shape, dt) in REAL_IN.items():
        dram[nm] = nc.dram_tensor(nm, shape, dt, kind="ExternalInput").ap()
    ext_in, ext_out, internal = _group_io(group)
    if expose_all:
        ext_out = ext_out | internal
        internal = set()
    for nm in sorted(ext_in):
        shape, dt = SCRATCH[nm]
        dram[nm] = nc.dram_tensor(nm, shape, dt, kind="ExternalInput").ap()
    for nm in sorted(ext_out):
        shape, dt = SCRATCH[nm]
        dram[nm] = nc.dram_tensor(nm, shape, dt, kind="ExternalOutput").ap()
    for nm in sorted(internal):
        shape, dt = SCRATCH[nm]
        dram[nm] = nc.dram_tensor(nm, shape, dt, kind="Internal").ap()
    if "PG1" in group:
        dram["y_out"] = nc.dram_tensor("y_out", [T, D], F32,
                                       kind="ExternalOutput").ap()
    k = K(nc, dram)
    emit_PINIT(k)
    for ph in group:
        k.entry = {nm: k.cnt[nm] for nm in
                   list(k.store_sems) + ["cld", "cldg", "cinit"]
                   if k.cnt.get(nm, 0)}
        k.phase_begin()
        EMITTERS[ph](k)
    with nc.Block() as blk:
        @blk.sync
        def _(sync):
            for nm in sorted(k.store_sems):
                sync.wait_ge(k.sem(nm), k.cnt[nm])
    k.es.close()
    return nc, sorted(ext_in), sorted(ext_out | ({"y_out"} if "PG1" in group
                                                 else set()))


import os as _os
if _os.environ.get("KM_UNFUSED"):
    PLAN = [("P0",), ("PA",), ("PC",), ("PB1",), ("PB2",), ("PD",), ("PE1",),
            ("PF1",), ("PG1",)]
else:
    PLAN = [tuple(PHASE_ORDER)]

_CACHE = {}


def kernel(**inputs):
    in_maps = _host_prep(inputs)
    pool = [dict(m) for m in in_maps]
    for group in PLAN:
        key = tuple(group)
        if key not in _CACHE:
            _CACHE[key] = build_program(list(group))
        nc, ext_in, ext_out = _CACHE[key]
        need = list(REAL_IN) + [nm for nm in ext_in if nm in SCRATCH]
        maps = [{nm: pool[c][nm] for nm in need} for c in range(NCORES)]
        res = run_bass_kernel_spmd(nc, maps, core_ids=list(range(NCORES)))
        for c in range(NCORES):
            for nm in ext_out:
                pool[c][nm] = np.asarray(res.results[c][nm])
    out = np.concatenate(
        [np.asarray(pool[c]["y_out"], dtype=np.float32).reshape(B_LOC, S, D)
         for c in range(NCORES)], axis=0)
    return out


# version 13
# speedup vs baseline: 1.0044x; 1.0032x over previous
import numpy as np
import concourse.bass as bass
import concourse.mybir as mybir
from concourse.bass_utils import run_bass_kernel_spmd
from contextlib import ExitStack

F32 = mybir.dt.float32
BF16 = mybir.dt.bfloat16
I32 = mybir.dt.int32
AF = mybir.ActivationFunctionType
ALU = mybir.AluOpType
AX = mybir.AxisListType

NCORES = 8
B_LOC = 4
S = 2048
D = 1024
H = 4096
T = B_LOC * S
NT = T // 128
DC = D // 128
HC = H // 128
C1 = 12582912.0
NEG = -1.0e30
R127 = float(np.float32(1.0) / np.float32(127.0))

SCRATCH = {
    "xnT_s":  ([D, T], BF16),
    "xqT_s":  ([D, T], BF16),
    "qT_s":   ([D, T], BF16),
    "kT_s":   ([D, T], BF16),
    "v_s":    ([T, D], BF16),
    "rsc_s":  ([T, 1], F32),
    "hq_s":   ([H, T], BF16),
    "g_s":    ([T, D], F32),
    "x1_s":   ([T, D], F32),
    "xn1_s":  ([T, D], BF16),
    "xn1T_s": ([D, T], BF16),
    "q1T_s":  ([D, T], BF16),
    "k1T_s":  ([D, T], BF16),
    "v1_s":   ([T, D], BF16),
    "y_s":    ([T, D], BF16),
    "gc_s":   ([32, D], F32),
    "w1qT_s": ([D, H], BF16),
    "w2qT_s": ([H, D], BF16),
    "weT_s":  ([D, D], BF16),
}

REAL_IN = {
    "x":      ([T, D], F32),
    "WqT":    ([D, D], F32),
    "WkT":    ([D, D], F32),
    "WvT":    ([D, D], F32),
    "bq_c":   ([128, DC], F32),
    "bk_c":   ([128, DC], F32),
    "bvRep":  ([128, D], F32),
    "w1T":    ([D, H], F32),
    "w2T":    ([H, D], F32),
    "convw":  ([128, HC * 3], F32),
    "convb":  ([128, HC], F32),
    "Wql":    ([D, D], F32),
    "Wkl":    ([D, D], F32),
    "Wvl":    ([D, D], F32),
    "Wol":    ([D, D], F32),
    "Wqg":    ([D, D], F32),
    "Wkg":    ([D, D], F32),
    "Wvg":    ([D, D], F32),
    "Wog":    ([D, D], F32),
    "weightT": ([D, D], F32),
    "alpha":  ([1, 1], F32),
    "bias1Rep": ([128, D], F32),
    "iden":   ([128, 128], F32),
    "cmask":  ([128, 128], F32),
    "gmask":  ([32, 32], F32),
    "onesr":  ([1, 128], F32),
    "rowsel": ([128, 32 * 32], F32),
}
REAL_OUT = {"y_out": ([T, D], F32)}

PHASE_IO = {
    "P0":  (set(), {"w1qT_s", "w2qT_s", "weT_s"}),
    "PA":  (set(), {"xnT_s", "xqT_s"}),
    "PC":  ({"xnT_s"}, {"qT_s", "kT_s", "v_s"}),
    "PB1": ({"xqT_s", "w1qT_s"}, {"hq_s", "rsc_s"}),
    "PB2": ({"hq_s", "rsc_s", "w2qT_s"}, {"g_s"}),
    "PD":  ({"qT_s", "kT_s", "v_s", "g_s"}, {"x1_s", "xn1_s", "xn1T_s"}),
    "PE1": ({"xn1T_s"}, {"q1T_s", "k1T_s", "v1_s"}),
    "PF1": ({"q1T_s", "k1T_s", "v1_s", "xn1_s"}, {"y_s", "gc_s"}),
    "PG1": ({"xn1T_s", "y_s", "x1_s", "gc_s", "weT_s"}, set()),
}
PHASE_ORDER = ["P0", "PA", "PC", "PB1", "PB2", "PD", "PE1", "PF1", "PG1"]


class K:

    PERSIST = ("cld", "cldg", "cinit", "pc_wl", "pe_wl", "pf_wl")

    def __init__(self, nc, dram):
        self.nc = nc
        self.dram = dram
        self.es = ExitStack()
        self._semh = {}
        self._base = {}
        self.cnt = {}
        self._phys = {}
        self._pool = []
        self._nalloc = 0
        self.g = {}
        self.store_sems = set()
        self.entry = {}

    def phase_begin(self):
        for name in list(self._semh):
            if name in self.store_sems or name in self.PERSIST:
                continue
            h = self._semh.pop(name)
            self._pool.append(h)
            self._base.pop(name, None)

    def sem(self, name):
        if name not in self._semh:
            if name in self.store_sems or name in self.PERSIST or not self._pool:
                h = self.es.enter_context(self.nc.semaphore(
                    f"s{self._nalloc}_{name}"))
                self._nalloc += 1
                self._phys[id(h)] = 0
            else:
                h = self._pool.pop()
            self._semh[name] = h
            self._base[name] = self._phys[id(h)]
            self.cnt[name] = 0
        return self._semh[name]

    def wg(self, eng, name, val):
        h = self.sem(name)
        eng.wait_ge(h, self._base[name] + val)

    def inc(self, instr, name, n=1):
        h = self.sem(name)
        instr.then_inc(h, n)
        self.cnt[name] += n
        self._phys[id(h)] += n
        return self.cnt[name]

    def inc2(self, instr, name, n=1):
        self.inc(instr, name, n)
        return instr

    def stinc(self, instr, name):
        if name not in self.store_sems:
            assert name not in self._semh, f"store sem {name} leased late"
            self.store_sems.add(name)
        return self.inc(instr, name, 16)

    def c(self, name):
        self.sem(name)
        return self.cnt[name]

    def declare_stores(self, *names):
        for nm in names:
            for par in (0, 1):
                n2 = f"{nm}{par}"
                self.store_sems.add(n2)
                self.sem(n2)



def _host_prep(inp):
    f = np.float32
    A = np.ascontiguousarray
    sh = {}
    sh["WqT"] = A(np.asarray(inp["L0_attn_Wq"], f).T)
    sh["WkT"] = A(np.asarray(inp["L0_attn_Wk"], f).T)
    sh["WvT"] = A(np.asarray(inp["L0_attn_Wv"], f).T)
    sh["bq_c"] = A(np.asarray(inp["L0_attn_bq"], f).reshape(DC, 128).T)
    sh["bk_c"] = A(np.asarray(inp["L0_attn_bk"], f).reshape(DC, 128).T)
    sh["bvRep"] = A(np.broadcast_to(np.asarray(inp["L0_attn_bv"], f)[None, :], (128, D)))
    sh["w1T"] = A(np.asarray(inp["L0_ffn_w1"], f).T)
    sh["w2T"] = A(np.asarray(inp["L0_ffn_w2"], f).T)
    cw = np.asarray(inp["L0_ffn_conv_w"], f).reshape(HC, 128, 3)
    sh["convw"] = A(cw.transpose(1, 0, 2).reshape(128, HC * 3))
    sh["convb"] = A(np.asarray(inp["L0_ffn_conv_b"], f).reshape(HC, 128).T)
    for nm, key in (("Wql", "L1_attn_Wq_l"), ("Wkl", "L1_attn_Wk_l"),
                    ("Wvl", "L1_attn_Wv_l"), ("Wol", "L1_attn_Wo_l"),
                    ("Wqg", "L1_attn_Wq_g"), ("Wkg", "L1_attn_Wk_g"),
                    ("Wvg", "L1_attn_Wv_g"), ("Wog", "L1_attn_Wo_g")):
        sh[nm] = A(np.asarray(inp[key], f))
    sh["weightT"] = A(np.asarray(inp["L1_ffn_weight"], f).T)
    sh["alpha"] = np.asarray(inp["L1_ffn_alpha"], f).reshape(1, 1)
    sh["bias1Rep"] = A(np.broadcast_to(np.asarray(inp["L1_ffn_bias"], f)[None, :], (128, D)))
    sh["iden"] = np.eye(128, dtype=f)
    sh["cmask"] = np.triu(np.full((128, 128), NEG, f), 1)
    gm = np.full((32, 32), NEG, f)
    for b in range(4):
        gm[b * 8:(b + 1) * 8, b * 8:(b + 1) * 8] = 0.0
    sh["gmask"] = gm
    sh["onesr"] = np.ones((1, 128), f)
    rs = np.zeros((128, 32 * 32), f)
    for r in range(32):
        rs[:, r * 32 + r] = 1.0
    sh["rowsel"] = A(rs)

    x_full = np.asarray(inp["L0_attn_x"], f)
    maps = []
    for c in range(NCORES):
        m = dict(sh)
        m["x"] = A(x_full[c * B_LOC:(c + 1) * B_LOC].reshape(T, D))
        maps.append(m)
    return maps



def emit_PINIT(k):
    nc, d, g = k.nc, k.dram, k.g
    mk = lambda n, s, t: k.es.enter_context(nc.sbuf_tensor(n, s, t))
    g["iden32"] = mk("g_iden32", [128, 128], F32)
    g["idenb"] = mk("g_idenb", [128, 128], BF16)
    g["cmaskb"] = mk("g_cmaskb", [128, 128], BF16)
    g["gmaskb"] = mk("g_gmaskb", [32, 32], BF16)
    g["onesr32"] = mk("g_onesr32", [1, 128], F32)
    g["onesrb"] = mk("g_onesrb", [1, 128], BF16)
    g["onescol"] = mk("g_onescol", [128, 1], F32)
    g["onescolb"] = mk("g_onescolb", [128, 1], BF16)
    with nc.Block() as blk:
        @blk.sync
        def _(sync):
            k.inc(sync.dma_start(out=g["iden32"][:], in_=d["iden"][:]), "cld", 16)
            k.inc(sync.dma_start(out=g["onesr32"][:], in_=d["onesr"][:]), "cld", 16)

        @blk.gpsimd
        def _(gp):
            k.inc(gp.dma_start(out=g["idenb"][:], in_=d["iden"][:]), "cldg", 16)
            k.inc(gp.dma_start(out=g["cmaskb"][:], in_=d["cmask"][:]), "cldg", 16)
            k.inc(gp.dma_start(out=g["gmaskb"][:], in_=d["gmask"][:]), "cldg", 16)
            k.inc(gp.dma_start(out=g["onesrb"][:], in_=d["onesr"][:]), "cldg", 16)

        @blk.vector
        def _(v):
            v.memset(g["onescol"][:], 1.0)
            k.inc(v.memset(g["onescolb"][:], 1.0), "cinit", 1)


def ldw(k, eng, name, j, n=16):
    k.wg(eng, f"{name}{j % 2}", n * (j // 2 + 1))


def ldi(k, eng, name, j, out, in_, n=16):
    return k.inc(eng.dma_start(out=out, in_=in_), f"{name}{j % 2}", n)


def sti(k, eng, name, j, out, in_):
    for par in (0, 1):
        nm = f"{name}{par}"
        if nm not in k.store_sems:
            k.store_sems.add(nm)
        k.sem(nm)
    return k.inc(eng.dma_start(out=out, in_=in_), f"{name}{j % 2}", 16)


def stw(k, eng, name, j, n=16):
    if j >= 2:
        k.wg(eng, f"{name}{j % 2}", n * (j // 2))


def _entry(k, eng):
    for nm, v in sorted(k.entry.items()):
        if v:
            eng.wait_ge(k.sem(nm), v)



def emit_P0(k):
    nc, d, g = k.nc, k.dram, k.g
    k.declare_stores("p0_st")
    with ExitStack() as es:
        sb = lambda n, s, t: es.enter_context(nc.sbuf_tensor(n, s, t))
        wb = [sb("p0_w0", [128, H], F32), sb("p0_w1", [128, H], F32)]
        wq_sb = [sb("p0_wq0", [128, H], BF16), sb("p0_wq1", [128, H], BF16)]
        t1 = sb("p0_t1", [128, H], F32)
        t2 = sb("p0_t2", [128, H], F32)
        ki = sb("p0_ki", [128, H], I32)
        db = sb("p0_db", [128, H], BF16)
        ai = sb("p0_ai", [128, H], I32)
        col = sb("p0_col", [128, 2], F32)
        st1 = sb("p0_st1", [1, 4], F32)
        bcol = sb("p0_bcol", [128, 4], F32)
        ps = es.enter_context(nc.psum_tensor("p0_ps", [128, 512], F32))
        blk = es.enter_context(nc.Block())

        cfgs = [("w1T", "w1qT_s", DC, H, "bit"),
                ("w2T", "w2qT_s", HC, D, "bit"),
                ("weightT", "weT_s", DC, D, "tern")]
        ld_before = []
        tot = 0
        for _w, _o, nch, _c, _kd in cfgs:
            ld_before.append(tot)
            tot += 2 * nch
        q_before = [0, DC, DC + HC]

        @blk.sync
        def _(sync):
            _entry(k, sync)
            k.inc(sync.dma_start(out=st1[0:1, 3:4], in_=d["alpha"][:]), "p0a", 16)
            for idx, (wname, _o, nch, colw, _kd) in enumerate(cfgs):
                for ps_ in range(2):
                    for c in range(nch):
                        gi = ld_before[idx] + ps_ * nch + c
                        if gi >= 2:
                            k.wg(sync, "p0r", gi - 1)
                        ldi(k, sync, "p0ld", gi, wb[gi % 2][:, 0:colw],
                            d[wname][c * 128:(c + 1) * 128, :])

        @blk.vector
        def _(v):
            _entry(k, v)
            for idx, (wname, _o, nch, colw, kind) in enumerate(cfgs):
                for c in range(nch):
                    gi = ld_before[idx] + c
                    ldw(k, v, "p0ld", gi)
                    if c == 0:
                        e = k.inc(v.tensor_reduce(col[:, 0:1], wb[gi % 2][:, 0:colw],
                                                  AX.X, ALU.add,
                                                  apply_absolute_value=True), "p0r")
                    else:
                        e = k.inc(v.tensor_reduce(col[:, 1:2], wb[gi % 2][:, 0:colw],
                                                  AX.X, ALU.add,
                                                  apply_absolute_value=True), "p0r")
                        k.wg(v, "p0r", e)
                        e = k.inc(v.tensor_tensor(col[:, 0:1], col[:, 0:1],
                                                  col[:, 1:2], ALU.add), "p0e")
                        k.wg(v, "p0e", e)
                k.inc(v.tensor_copy(col[:, 1:2], col[:, 0:1]), "p0v")
                k.wg(v, "p0m", 2 * idx + 1)
                e = k.inc(v.tensor_scalar(st1[0:1, 0:1], ps[0:1, 0:1],
                                          1.0 / (nch * 128 * colw), None,
                                          ALU.mult), "p0e")
                k.wg(v, "p0e", e)
                if kind == "bit":
                    e = k.inc(v.tensor_scalar(st1[0:1, 1:2], st1[0:1, 0:1],
                                              1e-5, None, ALU.max), "p0e")
                    k.wg(v, "p0e", e)
                    e = k.inc(v.reciprocal(st1[0:1, 2:3], st1[0:1, 1:2]), "p0e")
                    k.wg(v, "p0e", e)
                    e = k.inc(v.tensor_scalar(st1[0:1, 2:3], st1[0:1, 2:3],
                                              -1.0, None, ALU.mult), "p0e")
                    k.wg(v, "p0e", e)
                else:
                    k.wg(v, "p0a", 16)
                    e = k.inc(v.tensor_scalar(st1[0:1, 1:2], st1[0:1, 0:1],
                                              0.7, None, ALU.mult), "p0e")
                    k.wg(v, "p0e", e)
                k.inc(v.tensor_copy(st1[0:1, 0:1], st1[0:1, 0:1]), "p0v")
                k.wg(v, "p0m", 2 * idx + 2)
                if kind == "bit":
                    e = k.inc(v.tensor_copy(bcol[:, 0:3], ps[:, 0:3]), "p0e")
                else:
                    e = k.inc(v.tensor_copy(bcol[:, 1:4], ps[:, 1:4]), "p0e")
                k.wg(v, "p0e", e)
                if kind == "tern":
                    e = k.inc(v.tensor_scalar(bcol[:, 2:3], bcol[:, 2:3], -1.0,
                                              None, ALU.mult), "p0e")
                    k.wg(v, "p0e", e)
                for c in range(nch):
                    gi = ld_before[idx] + nch + c
                    ldw(k, v, "p0ld", gi)
                    gq = q_before[idx] + c
                    stw(k, v, "p0_st", gq)
                    wc = wb[gi % 2][:, 0:colw]
                    oc = wq_sb[gq % 2][:, 0:colw]
                    cw = colw
                    if k.cnt.get("p0q", 0):
                        k.wg(v, "p0q", k.cnt["p0q"])
                    def _e(ins):
                        ee = k.inc(ins, "p0e")
                        k.wg(v, "p0e", ee)
                        return ins

                    if kind == "bit":
                        k.inc(v.tensor_scalar(t1[:, 0:cw], wc, bcol[:, 2:3],
                                              None, ALU.mult), "p0r")
                        k.wg(v, "p0r", k.cnt["p0r"])
                        _e(v.tensor_scalar(t2[:, 0:cw], t1[:, 0:cw], C1, -C1,
                                           ALU.add, ALU.add))
                        _e(v.tensor_copy(ki[:, 0:cw], t2[:, 0:cw]))
                        _e(v.scalar_tensor_tensor(db[:, 0:cw], ki[:, 0:cw],
                                                  -1.0, t2[:, 0:cw], ALU.mult,
                                                  ALU.add))
                        _e(v.scalar_tensor_tensor(ai[:, 0:cw], db[:, 0:cw], 0.5,
                                                  ki[:, 0:cw], ALU.is_gt,
                                                  ALU.add))
                        _e(v.scalar_tensor_tensor(t1[:, 0:cw], db[:, 0:cw],
                                                  -0.5, ai[:, 0:cw], ALU.is_lt,
                                                  ALU.subtract))
                        _e(v.tensor_scalar(t2[:, 0:cw], t1[:, 0:cw], -1.0, 1.0,
                                           ALU.max, ALU.min))
                        ins = v.tensor_scalar(oc, t2[:, 0:cw], bcol[:, 1:2],
                                              None, ALU.mult)
                    else:
                        v.tensor_scalar(t1[:, 0:cw], wc, bcol[:, 1:2], None, ALU.is_gt)
                        k.inc(v.tensor_scalar(t2[:, 0:cw], wc, bcol[:, 2:3],
                                              None, ALU.is_lt), "p0r")
                        k.wg(v, "p0r", k.cnt["p0r"])
                        _e(v.tensor_tensor(t1[:, 0:cw], t1[:, 0:cw], t2[:, 0:cw],
                                           ALU.subtract))
                        ins = v.tensor_scalar(oc, t1[:, 0:cw], bcol[:, 3:4],
                                              None, ALU.mult)
                    k.inc(ins, "p0q")

        @blk.tensor
        def _(te):
            _entry(k, te)
            for idx, (wname, _o, nch, colw, kind) in enumerate(cfgs):
                k.wg(te, "p0v", 2 * idx + 1)
                k.inc(te.matmul(ps[0:1, 0:1], g["onescol"][:, 0:1], col[:, 0:1],
                                start=True, stop=True), "p0m")
                k.wg(te, "p0v", 2 * idx + 2)
                if kind == "bit":
                    te.matmul(ps[:, 0:1], g["onesr32"][0:1, :], st1[0:1, 0:1],
                              start=True, stop=True)
                    te.matmul(ps[:, 1:2], g["onesr32"][0:1, :], st1[0:1, 1:2],
                              start=True, stop=True)
                    k.inc(te.matmul(ps[:, 2:3], g["onesr32"][0:1, :],
                                    st1[0:1, 2:3], start=True, stop=True), "p0m")
                else:
                    te.matmul(ps[:, 1:2], g["onesr32"][0:1, :], st1[0:1, 1:2],
                              start=True, stop=True)
                    te.matmul(ps[:, 2:3], g["onesr32"][0:1, :], st1[0:1, 1:2],
                              start=True, stop=True)
                    k.inc(te.matmul(ps[:, 3:4], g["onesr32"][0:1, :],
                                    st1[0:1, 3:4], start=True, stop=True), "p0m")

        @blk.gpsimd
        def _(gp):
            _entry(k, gp)
            nq = 0
            for idx, (wname, oname, nch, colw, kind) in enumerate(cfgs):
                for c in range(nch):
                    nq += 1
                    k.wg(gp, "p0q", nq)
                    gq = q_before[idx] + c
                    sti(k, gp, "p0_st", gq, d[oname][c * 128:(c + 1) * 128, :],
                        wq_sb[gq % 2][:, 0:colw])



def emit_PA(k):
    nc, d, g = k.nc, k.dram, k.g
    k.declare_stores("pa_st")
    with ExitStack() as es:
        sb = lambda n, s, t: es.enter_context(nc.sbuf_tensor(n, s, t))
        xb = [sb("pa_x0", [128, D], F32), sb("pa_x1", [128, D], F32)]
        xn = [sb("pa_xn0", [128, D], F32), sb("pa_xn1", [128, D], F32)]
        sq = sb("pa_sq", [128, D], F32)
        stt = [sb("pa_st0", [128, 16], F32), sb("pa_st1", [128, 16], F32)]
        qt = sb("pa_qt", [128, D], F32)
        r1 = sb("pa_r1", [128, D], F32)
        k0 = sb("pa_k0", [128, D], I32)
        dd = sb("pa_dd", [128, D], BF16)
        aa = sb("pa_aa", [128, D], I32)
        fin = sb("pa_fin", [128, D], BF16)
        xq = [sb("pa_xq0", [128, D], BF16), sb("pa_xq1", [128, D], BF16)]
        xnT = [sb("pa_xnT0", [128, D], BF16), sb("pa_xnT1", [128, D], BF16)]
        xqT = [sb("pa_xqT0", [128, D], BF16), sb("pa_xqT1", [128, D], BF16)]
        psA = [es.enter_context(nc.psum_tensor("pa_psA0", [128, D], F32)),
               es.enter_context(nc.psum_tensor("pa_psA1", [128, D], F32))]
        psB = [es.enter_context(nc.psum_tensor("pa_psB0", [128, D], BF16)),
               es.enter_context(nc.psum_tensor("pa_psB1", [128, D], BF16))]
        blk = es.enter_context(nc.Block())
        xnT_d = d["xnT_s"].rearrange("(c p) t -> p c t", p=128)
        xqT_d = d["xqT_s"].rearrange("(c p) t -> p c t", p=128)

        @blk.sync
        def _(sync):
            _entry(k, sync)
            for i in range(NT):
                if i >= 2:
                    k.wg(sync, "pa_xn", i - 1)
                ldi(k, sync, "pa_ld", i, xb[i % 2][:],
                    d["x"][i * 128:(i + 1) * 128, :])

        @blk.vector
        def _(v):
            _entry(k, v)
            for i in range(NT):
                p = i % 2
                st_ = stt[p]
                ldw(k, v, "pa_ld", i)
                if i >= 2:
                    k.wg(v, "pa_xn", i - 1)
                e = k.inc(v.tensor_reduce(st_[:, 0:1], xb[p][:], AX.X, ALU.add),
                          "pa_ln")
                k.wg(v, "pa_ln", e)
                e = k.inc(v.tensor_scalar(st_[:, 2:3], st_[:, 0:1], 1.0 / D,
                                          None, ALU.mult), "pa_ln")
                k.wg(v, "pa_ln", e)
                e = k.inc(v.tensor_tensor(st_[:, 3:4], st_[:, 2:3], st_[:, 2:3],
                                          ALU.mult), "pa_ln")
                k.wg(v, "pa_sq", 2 * i + 1)
                k.wg(v, "pa_ln", e)
                e = k.inc(v.scalar_tensor_tensor(st_[:, 4:5], st_[:, 1:2],
                                                 1.0 / D, st_[:, 3:4], ALU.mult,
                                                 ALU.subtract), "pa_ln")
                k.wg(v, "pa_ln", e)
                e = k.inc(v.tensor_scalar(st_[:, 5:6], st_[:, 4:5], 1e-5, None,
                                          ALU.add), "pa_ln")
                k.wg(v, "pa_sq", 2 * i + 2)
                e = k.inc(v.reciprocal(st_[:, 7:8], st_[:, 6:7]), "pa_ln")
                k.wg(v, "pa_ln", e)
                e = k.inc(v.scalar_tensor_tensor(st_[:, 8:9], st_[:, 2:3], -1.0,
                                                 st_[:, 7:8], ALU.mult,
                                                 ALU.mult), "pa_ln")
                k.wg(v, "pa_xn", i + 1)
                e = k.inc(v.tensor_reduce(st_[:, 9:10], xn[p][:], AX.X, ALU.max,
                                          apply_absolute_value=True), "pa_ch2")
                k.wg(v, "pa_ch2", e)
                e = k.inc(v.tensor_scalar(st_[:, 10:11], st_[:, 9:10], 1e-5,
                                          None, ALU.max), "pa_ch2")
                k.wg(v, "pa_ch2", e)
                e = k.inc(v.reciprocal(st_[:, 13:14], st_[:, 10:11]), "pa_ch2")
                k.wg(v, "pa_ch2", e)
                e = k.inc(v.tensor_scalar(st_[:, 11:12], st_[:, 13:14], -127.0,
                                          None, ALU.mult), "pa_ch2")
                k.wg(v, "pa_ch2", e)
                e = k.inc(v.tensor_scalar(st_[:, 12:13], st_[:, 10:11], R127,
                                          None, ALU.mult), "pa_ch2")
                k.wg(v, "pa_ch2", e)
                e = k.inc(v.tensor_scalar(qt[:], xn[p][:], st_[:, 11:12],
                                          None, ALU.mult), "pa_ve")
                k.wg(v, "pa_ve", e)
                e = k.inc(v.tensor_scalar(r1[:], qt[:], C1, -C1, ALU.add,
                                          ALU.add), "pa_ve")
                k.wg(v, "pa_ve", e)
                e = k.inc(v.tensor_copy(k0[:], r1[:]), "pa_ve")
                k.wg(v, "pa_ve", e)
                e = k.inc(v.scalar_tensor_tensor(dd[:], k0[:], -1.0, r1[:],
                                                 ALU.mult, ALU.add), "pa_ve")
                k.wg(v, "pa_ve", e)
                e = k.inc(v.scalar_tensor_tensor(aa[:], dd[:], 0.5, k0[:],
                                                 ALU.is_gt, ALU.add), "pa_ve")
                k.wg(v, "pa_ve", e)
                e = k.inc(v.scalar_tensor_tensor(fin[:], dd[:], -0.5, aa[:],
                                                 ALU.is_lt, ALU.subtract),
                          "pa_ve")
                k.wg(v, "pa_ve", e)
                if i >= 2:
                    k.wg(v, "pa_mm", 2 * i - 2)
                k.inc(v.tensor_scalar(xq[p][:], fin[:], st_[:, 12:13], None,
                                      ALU.mult), "pa_ch")

        @blk.scalar
        def _(sc):
            _entry(k, sc)

            def evac(j):
                q = j % 2
                k.wg(sc, "pa_mm", 2 * j + 1)
                stw(k, sc, "pa_st", j, n=32)
                k.inc(sc.activation(xnT[q][:], psA[q][:], AF.Identity), "pa_ev")
                k.wg(sc, "pa_mm", 2 * j + 2)
                k.inc(sc.activation(xqT[q][:], psB[q][:], AF.Identity), "pa_ev")

            for i in range(NT):
                p = i % 2
                st_ = stt[p]
                ldw(k, sc, "pa_ld", i)
                if i >= 2:
                    k.wg(sc, "pa_xn", i - 1)
                k.inc(sc.activation(sq[:], xb[p][:], AF.Square,
                                    accum_out=st_[:, 1:2]), "pa_sq")
                k.wg(sc, "pa_ln", 7 * i + 5)
                k.inc(sc.activation(st_[:, 6:7], st_[:, 5:6], AF.Sqrt), "pa_sq")
                k.wg(sc, "pa_ln", 7 * (i + 1))
                if i >= 2:
                    k.wg(sc, "pa_mm", 2 * i - 2)
                k.inc(sc.activation(xn[p][:], xb[p][:], AF.Identity,
                                    bias=st_[:, 8:9], scale=st_[:, 7:8]),
                      "pa_xn")
                if i >= 1:
                    evac(i - 1)
            evac(NT - 1)

        @blk.tensor
        def _(te):
            _entry(k, te)
            for i in range(NT):
                p = i % 2
                k.wg(te, "pa_xn", i + 1)
                if i >= 2:
                    k.wg(te, "pa_ev", 2 * i - 2)
                for c in range(DC):
                    ins = te.transpose(psA[p][:, c * 128:(c + 1) * 128],
                                       xn[p][:, c * 128:(c + 1) * 128],
                                       g["iden32"][:])
                k.inc(ins, "pa_mm")
                k.wg(te, "pa_ch", i + 1)
                for c in range(DC):
                    ins = te.transpose(psB[p][:, c * 128:(c + 1) * 128],
                                       xq[p][:, c * 128:(c + 1) * 128],
                                       g["idenb"][:])
                k.inc(ins, "pa_mm")

        @blk.gpsimd
        def _(gp):
            _entry(k, gp)
            for i in range(NT):
                p = i % 2
                k.wg(gp, "pa_ev", 2 * i + 1)
                sti(k, gp, "pa_st", i, xnT_d[:, :, i * 128:(i + 1) * 128],
                    xnT[p].ap().rearrange("p (c t) -> p c t", c=DC))
                k.wg(gp, "pa_ev", 2 * i + 2)
                sti(k, gp, "pa_st", i, xqT_d[:, :, i * 128:(i + 1) * 128],
                    xqT[p].ap().rearrange("p (c t) -> p c t", c=DC))



def emit_PC(k):
    import os
    _parts = os.environ.get("PC_PARTS", "qkv")
    nc, d, g = k.nc, k.dram, k.g
    k.declare_stores("pc_st")
    NS = T // 512
    with ExitStack() as es:
        sb = lambda n, s, t: es.enter_context(nc.sbuf_tensor(n, s, t))
        wq = sb("pc_wq", [128, DC * D], BF16)
        wk = sb("pc_wk", [128, DC * D], BF16)
        wv = sb("pc_wv", [128, DC * D], BF16)
        bq = sb("pc_bq", [128, DC], F32)
        bk = sb("pc_bk", [128, DC], F32)
        bv = sb("pc_bv", [128, D], F32)
        xs = [sb("pc_x0", [128, DC * 512], BF16), sb("pc_x1", [128, DC * 512], BF16)]
        qs = [sb("pc_q0", [128, DC * 512], BF16), sb("pc_q1", [128, DC * 512], BF16)]
        ks_ = [sb("pc_k0", [128, DC * 512], BF16), sb("pc_k1", [128, DC * 512], BF16)]
        vs = [sb("pc_v0", [128, 4 * D], BF16), sb("pc_v1", [128, 4 * D], BF16)]
        psq = es.enter_context(nc.psum_tensor("pc_psq", [128, 1024], F32))
        psk = es.enter_context(nc.psum_tensor("pc_psk", [128, 1024], F32))
        psv = es.enter_context(nc.psum_tensor("pc_psv", [128, 2048], F32))
        blk = es.enter_context(nc.Block())
        xnT_d = d["xnT_s"].rearrange("(c p) t -> p c t", p=128)
        qT_d = d["qT_s"].rearrange("(c p) t -> p c t", p=128)
        kT_d = d["kT_s"].rearrange("(c p) t -> p c t", p=128)

        @blk.sync
        def _(sync):
            _entry(k, sync)
            k.inc(sync.dma_start(out=bq[:], in_=d["bq_c"][:]), "pc_wlb", 16)
            k.inc(sync.dma_start(out=bk[:], in_=d["bk_c"][:]), "pc_wlb", 16)
            k.inc(sync.dma_start(out=bv[:], in_=d["bvRep"][:]), "pc_wlb", 16)
            for s in range(NS):
                if s >= 2 and "v" in _parts:
                    k.wg(sync, "pc_mv", 4 * (s - 1))
                elif s >= 2:
                    k.wg(sync, "pc_mq", DC * (s - 1))
                ldi(k, sync, "pc_ld", s,
                    xs[s % 2].ap().rearrange("p (c t) -> p c t", c=DC),
                    xnT_d[:, :, s * 512:(s + 1) * 512])

        @blk.gpsimd
        def _(gp):
            _entry(k, gp)
            for wsb, wnm in ((wq, "WqT"), (wk, "WkT"), (wv, "WvT")):
                k.inc(gp.dma_start(
                    out=wsb.ap().rearrange("p (c e) -> p c e", c=DC),
                    in_=d[wnm].rearrange("(c p) e -> p c e", p=128)), "pc_wl", 16)
            for s in range(NS):
                p = s % 2
                if "q" in _parts:
                    k.wg(gp, "pc_eq", DC * (s + 1))
                    sti(k, gp, "pc_st", s, qT_d[:, :, s * 512:(s + 1) * 512],
                        qs[p].ap().rearrange("p (c t) -> p c t", c=DC))
                if "k" in _parts:
                    k.wg(gp, "pc_ek", DC * (s + 1))
                    sti(k, gp, "pc_st", s, kT_d[:, :, s * 512:(s + 1) * 512],
                        ks_[p].ap().rearrange("p (c t) -> p c t", c=DC))
                if "v" in _parts:
                    k.wg(gp, "pc_ev", 4 * (s + 1))
                    sti(k, gp, "pc_st", s,
                        d["v_s"][s * 512:(s + 1) * 512, :].rearrange(
                            "(tt p) e -> p tt e", p=128),
                        vs[p].ap().rearrange("p (tt e) -> p tt e", tt=4))

        @blk.tensor
        def _(te):
            _entry(k, te)
            k.wg(te, "pc_wl", 16 * 3)
            k.wg(te, "pc_wlb", 16 * 3)
            for s in range(NS):
                p = s % 2
                ldw(k, te, "pc_ld", s)
                for wsb, pst, msem, esem in ((wq, psq, "pc_mq", "pc_eq"),
                                             (wk, psk, "pc_mk", "pc_ek"))[
                                                 :("q" in _parts) + ("k" in _parts)]:
                    for ec in range(DC):
                        gq = s * DC + ec
                        bank = ec % 2
                        if gq >= 2:
                            k.wg(te, esem, gq - 1)
                        for dc2 in range(DC):
                            ins = te.matmul(pst[:, bank * 512:(bank + 1) * 512],
                                            wsb[:, dc2 * D + ec * 128:
                                                dc2 * D + (ec + 1) * 128],
                                            xs[p][:, dc2 * 512:(dc2 + 1) * 512],
                                            start=(dc2 == 0),
                                            stop=(dc2 == DC - 1))
                        k.inc(ins, msem)
                for tt in range(4 if "v" in _parts else 0):
                    gv = 4 * s + tt
                    bank = tt % 2
                    if gv >= 2:
                        k.wg(te, "pc_ev", gv - 1)
                    for dc2 in range(DC):
                        for half in range(2):
                            ins = te.matmul(
                                psv[:, bank * D + half * 512:
                                    bank * D + (half + 1) * 512],
                                xs[p][:, dc2 * 512 + tt * 128:
                                      dc2 * 512 + (tt + 1) * 128],
                                wv[:, dc2 * D + half * 512:
                                   dc2 * D + (half + 1) * 512],
                                start=(dc2 == 0), stop=(dc2 == DC - 1))
                    k.inc(ins, "pc_mv")

        @blk.scalar
        def _(sc):
            _entry(k, sc)
            for s in range(NS):
                p = s % 2
                stw(k, sc, "pc_st", s,
                    n=16 * (("q" in _parts) + ("k" in _parts) + ("v" in _parts)))
                for pst, bcol_, out, msem, esem in (
                        (psq, bq, qs, "pc_mq", "pc_eq"),
                        (psk, bk, ks_, "pc_mk", "pc_ek"))[
                            :("q" in _parts) + ("k" in _parts)]:
                    for ec in range(DC):
                        bank = ec % 2
                        k.wg(sc, msem, s * DC + ec + 1)
                        k.inc(sc.activation(out[p][:, ec * 512:(ec + 1) * 512],
                                            pst[:, bank * 512:(bank + 1) * 512],
                                            AF.Identity,
                                            bias=bcol_[:, ec:ec + 1]), esem)

        @blk.vector
        def _(v):
            _entry(k, v)
            for s in range(NS):
                p = s % 2
                stw(k, v, "pc_st", s,
                    n=16 * (("q" in _parts) + ("k" in _parts) + ("v" in _parts)))
                for tt in range(4 if "v" in _parts else 0):
                    bank = tt % 2
                    k.wg(v, "pc_mv", 4 * s + tt + 1)
                    k.inc(v.tensor_tensor(vs[p][:, tt * D:(tt + 1) * D],
                                          psv[:, bank * D:(bank + 1) * D],
                                          bv[:], ALU.add), "pc_ev")



def _pb1_slabs():
    out = []
    for seq in range(B_LOC):
        s0 = seq * S
        a = s0
        while a < s0 + S:
            n_out = min(510, s0 + S - a)
            zl = (a == s0)
            a0 = a if zl else a - 1
            a1 = min(a + n_out + 1, s0 + S)
            zr = (a1 == s0 + S) and (a + n_out == s0 + S)
            n_in = a1 - a0
            ofs = 1 if zl else 0
            out.append((seq, a, n_out, a0, n_in, ofs, zl, zr))
            a += n_out
    return out


def emit_PB1(k):
    nc, d, g = k.nc, k.dram, k.g
    k.declare_stores("pb_hqst", "pb_rscst")
    slabs = _pb1_slabs()
    NS = len(slabs)
    G = NS * HC
    chl = [(s, hc) + slabs[s] for s in range(NS) for hc in range(HC)]
    with ExitStack() as es:
        sb = lambda n, s, t: es.enter_context(nc.sbuf_tensor(n, s, t))
        w1 = sb("pb_w1", [128, DC * H], BF16)
        cwv = sb("pb_cw", [128, HC * 3], F32)
        cbv = sb("pb_cb", [128, HC], F32)
        xqs = [sb("pb_xq0", [128, DC * 512], BF16),
               sb("pb_xq1", [128, DC * 512], BF16)]
        cacc = [sb(f"pb_ca{i}", [128, 512], F32) for i in range(3)]
        t2m = [sb(f"pb_t2{i}", [128, 512], F32) for i in range(3)]
        sil = sb("pb_sil", [128, HC * 512], F32)
        asb = [sb("pb_ab0", [128, 512], BF16), sb("pb_ab1", [128, 512], BF16)]
        mx = sb("pb_mx", [128, 512], BF16)
        mt4 = sb("pb_mt4", [128, 8], F32)
        rsc4 = [sb("pb_rs0", [128, 4], F32), sb("pb_rs1", [128, 4], F32)]
        ng4 = sb("pb_ng4", [128, 4], F32)
        srow = sb("pb_srow", [1, 512], F32)
        srep = sb("pb_srep", [128, 512], F32)
        qtile = [sb("pb_qt0", [128, 512], F32), sb("pb_qt1", [128, 512], F32)]
        k0 = [sb("pb_k00", [128, 512], I32), sb("pb_k01", [128, 512], I32)]
        ddt = sb("pb_dd", [128, 512], BF16)
        aat = sb("pb_aa", [128, 512], BF16)
        hqs = sb("pb_hq", [128, HC * 512], BF16)
        psH = es.enter_context(nc.psum_tensor("pb_psH", [128, 2048], F32))
        psX = es.enter_context(nc.psum_tensor("pb_psX", [128, 512], BF16))
        psR = es.enter_context(nc.psum_tensor("pb_psR", [128, 512], F32))
        blk = es.enter_context(nc.Block())
        xqT_d = d["xqT_s"].rearrange("(c p) t -> p c t", p=128)
        hq_d = d["hq_s"].rearrange("(c p) t -> p c t", p=128)

        def hq_target(p, qr):
            return 16 * ((p // 2) * 4 + qr + 1)

        def rsc_target(p):
            n = 0
            for p2 in range(p % 2, p + 1, 2):
                n += (slabs[p2][2] + 127) // 128
            return 16 * n

        @blk.sync
        def _(sync):
            _entry(k, sync)
            k.inc(sync.dma_start(
                out=w1.ap().rearrange("p (c h) -> p c h", c=DC),
                in_=d["w1qT_s"].rearrange("(c p) h -> p c h", p=128)),
                "pb_wl", 16)
            k.inc(sync.dma_start(out=cwv[:], in_=d["convw"][:]), "pb_wl", 16)
            k.inc(sync.dma_start(out=cbv[:], in_=d["convb"][:]), "pb_wl", 16)

            def stores(p):
                a, n_out = slabs[p][1], slabs[p][2]
                for qr in range(4):
                    k.wg(sync, "pb_qc", 32 * p + 8 * (qr + 1))
                    k.inc(sync.dma_start(
                        out=hq_d[:, qr * 8:qr * 8 + 8, a:a + n_out],
                        in_=bass.AP(hqs, qr * 8 * 512,
                                    [[HC * 512, 128], [512, 8], [1, n_out]])),
                        f"pb_hqst{p % 2}", 16)
                    assert k.cnt[f"pb_hqst{p % 2}"] == hq_target(p, qr)
                k.wg(sync, "pb_rs", p + 1)
                ngrp = (n_out + 127) // 128
                for gi in range(ngrp):
                    w = min(128, n_out - gi * 128)
                    k.inc(sync.dma_start(
                        out=d["rsc_s"][a + gi * 128:a + gi * 128 + w, 0:1],
                        in_=rsc4[p % 2][0:w, gi:gi + 1]),
                        f"pb_rscst{p % 2}", 16)
                assert k.cnt[f"pb_rscst{p % 2}"] == rsc_target(p)

            for sl, (seq, a, n_out, a0, n_in, ofs, zl, zr) in enumerate(slabs):
                if sl >= 2:
                    k.wg(sync, "pb_mm", 32 * (sl - 1))
                ldi(k, sync, "pb_ld", sl,
                    bass.AP(xqs[sl % 2], 0,
                            [[DC * 512, 128], [512, DC], [1, n_in]]),
                    xqT_d[:, :, a0:a0 + n_in])
                if sl >= 2:
                    stores(sl - 2)
            stores(NS - 2)
            stores(NS - 1)

        @blk.tensor
        def _(te):
            _entry(k, te)
            k.wg(te, "pb_wl", 48)
            for s in range(NS + 1):

                def mchunks(rng):
                    seq, a, n_out, a0, n_in, ofs, zl, zr = slabs[s]
                    for hc in rng:
                        ghc = s * HC + hc
                        bank = hc % 4
                        if ghc >= 4:
                            k.wg(te, "pb_cv", ghc - 3)
                        for dc2 in range(DC):
                            ins = te.matmul(
                                psH[:, bank * 512 + ofs:bank * 512 + ofs + n_in],
                                w1[:, dc2 * H + hc * 128:dc2 * H + (hc + 1) * 128],
                                xqs[s % 2][:, dc2 * 512:dc2 * 512 + n_in],
                                start=(dc2 == 0), stop=(dc2 == DC - 1))
                        k.inc(ins, "pb_mm")

                if s < NS:
                    ldw(k, te, "pb_ld", s)
                    mchunks(range(0, 4))
                if s >= 1:
                    p = s - 1
                    n_out = slabs[p][2]
                    ngrp = (n_out + 127) // 128
                    k.wg(te, "pb_mx", 32 * (p + 1))
                    if p >= 1:
                        k.wg(te, "pb_s1", p)
                    for gi in range(ngrp):
                        w = min(128, n_out - gi * 128)
                        ins = te.transpose(psX[0:w, gi * 128:gi * 128 + 128],
                                           mx[:, gi * 128:gi * 128 + w],
                                           g["idenb"][:])
                    k.inc(ins, "pb_tx")
                    k.wg(te, "pb_ng", p + 1)
                    if p >= 1:
                        k.wg(te, "pb_s3", p)
                    for gi in range(ngrp):
                        ins = te.transpose(psR[0:1, gi * 128:(gi + 1) * 128],
                                           ng4[:, gi:gi + 1], g["iden32"][:])
                    k.inc(ins, "pb_tr")
                    k.wg(te, "pb_s2", p + 1)
                    k.inc(te.matmul(psR[:, 0:n_out], g["onesr32"][0:1, :],
                                    srow[0:1, 0:n_out], start=True, stop=True),
                          "pb_rp")
                if s < NS:
                    mchunks(range(4, HC))

        @blk.scalar
        def _(sc):
            _entry(k, sc)
            for gg in range(G + 2):
                if gg < G:
                    s, hc, seq, a, n_out, a0, n_in, ofs, zl, zr = chl[gg]
                    bank = hc % 4
                    k.wg(sc, "pb_mm", gg + 1)
                    if zl:
                        e = k.inc(sc.memzero(psH[:, bank * 512:bank * 512 + 1]),
                                  "pb_ae")
                        k.wg(sc, "pb_ae", e)
                    if zr:
                        e = k.inc(sc.memzero(
                            psH[:, bank * 512 + ofs + n_in:
                                bank * 512 + ofs + n_in + 1]), "pb_ae")
                        k.wg(sc, "pb_ae", e)
                    if gg >= 3:
                        k.wg(sc, "pb_p2", gg - 2)
                        k.wg(sc, "pb_cv", gg - 2)
                    k.inc(sc.activation(cacc[gg % 3][:, 0:n_out],
                                        psH[:, bank * 512 + 1:bank * 512 + 1 + n_out],
                                        AF.Identity,
                                        scale=cwv[:, hc * 3 + 1:hc * 3 + 2]),
                          "pb_t1")
                    k.inc(sc.activation(t2m[gg % 3][:, 0:n_out],
                                        psH[:, bank * 512:bank * 512 + n_out],
                                        AF.Identity,
                                        scale=cwv[:, hc * 3:hc * 3 + 1]),
                          "pb_t2")
                if gg >= 2:
                    s2, hc2, seq, a, n_out, a0, n_in, ofs, zl, zr = chl[gg - 2]
                    k.wg(sc, "pb_cv", gg - 1)
                    if gg - 33 >= 1:
                        k.wg(sc, "pb_qf", gg - 33)
                    k.inc(sc.activation(sil[:, hc2 * 512:hc2 * 512 + n_out],
                                        cacc[(gg - 2) % 3][:, 0:n_out], AF.Silu,
                                        bias=cbv[:, hc2:hc2 + 1]), "pb_sl")
                    k.wg(sc, "pb_sl", k.cnt["pb_sl"])
                    if gg >= 4:
                        k.wg(sc, "pb_mx", gg - 3)
                    k.inc(sc.activation(asb[hc2 % 2][:, 0:n_out],
                                        sil[:, hc2 * 512:hc2 * 512 + n_out],
                                        AF.Abs), "pb_ab")

        @blk.gpsimd
        def _(gp):
            _entry(k, gp)
            for s in range(NS + 1):
                for hc in range(HC):
                    if s < NS:
                        ghc = s * HC + hc
                        k.wg(gp, "pb_t1", ghc + 1)
                        k.wg(gp, "pb_t2", ghc + 1)
                        n_out = slabs[s][2]
                        ghc = s * HC + hc
                        k.inc(gp.tensor_tensor(cacc[ghc % 3][:, 0:n_out],
                                               cacc[ghc % 3][:, 0:n_out],
                                               t2m[ghc % 3][:, 0:n_out],
                                               ALU.add), "pb_p2")
                    if s >= 1:
                        p = s - 1
                        n_out = slabs[p][2]
                        if hc == 0:
                            k.wg(gp, "pb_s3", s)
                        k.wg(gp, "pb_sl", 32 * p + hc + 1)
                        if 32 * p + hc - 1 >= 1:
                            k.wg(gp, "pb_qc", 32 * p + hc - 1)
                        k.inc(gp.tensor_tensor(qtile[hc % 2][:, 0:n_out],
                                               sil[:, hc * 512:hc * 512 + n_out],
                                               srep[:, 0:n_out], ALU.mult),
                              "pb_qf")
                        k.wg(gp, "pb_qf", k.cnt["pb_qf"])
                        k.inc(gp.tensor_copy(k0[hc % 2][:, 0:n_out],
                                             qtile[hc % 2][:, 0:n_out]), "pb_k0")

        @blk.vector
        def _(v):
            _entry(k, v)
            v.memset(mt4[:], 0.0)
            e = k.inc(v.memset(ng4[:], 0.0), "pb_e")
            k.wg(v, "pb_e", e)

            def repair(p, hc):
                n_out = slabs[p][2]
                if hc == 0 and p >= 1:
                    k.wg(v, f"pb_hqst{(p - 1) % 2}", hq_target(p - 1, 3))
                k.wg(v, "pb_k0", 32 * p + hc + 1)
                k.wg(v, "pb_qc", k.cnt["pb_qc"])
                e = k.inc(v.scalar_tensor_tensor(ddt[:, 0:n_out],
                                                 k0[hc % 2][:, 0:n_out], -1.0,
                                                 qtile[hc % 2][:, 0:n_out],
                                                 ALU.mult, ALU.add), "pb_e")
                k.wg(v, "pb_e", e)
                e = k.inc(v.scalar_tensor_tensor(aat[:, 0:n_out],
                                                 ddt[:, 0:n_out], 0.5,
                                                 k0[hc % 2][:, 0:n_out],
                                                 ALU.is_gt, ALU.add), "pb_e")
                k.wg(v, "pb_e", e)
                k.inc(v.scalar_tensor_tensor(
                    hqs[:, hc * 512:hc * 512 + n_out], ddt[:, 0:n_out],
                    -0.5, aat[:, 0:n_out], ALU.is_lt, ALU.subtract), "pb_qc")

            def mxop(j):
                pj, hcj = j // HC, j % HC
                n_outj = slabs[pj][2]
                k.wg(v, "pb_ab", j + 1)
                if hcj == 0:
                    k.wg(v, "pb_tx", pj)
                    ins = v.tensor_copy(mx[:, 0:n_outj], asb[0][:, 0:n_outj])
                else:
                    ins = v.tensor_tensor(mx[:, 0:n_outj],
                                          asb[hcj % 2][:, 0:n_outj],
                                          mx[:, 0:n_outj], ALU.max)
                k.inc(ins, "pb_mx")
                k.wg(v, "pb_mx", k.cnt["pb_mx"])

            for s in range(NS + 1):
                if s >= 1:
                    mxop(s * HC - 2)
                    mxop(s * HC - 1)
                    p = s - 1
                    n_out = slabs[p][2]
                    ngrp = (n_out + 127) // 128
                    k.wg(v, "pb_tx", s)
                    for gi in range(ngrp):
                        w = min(128, n_out - gi * 128)
                        ins = v.tensor_reduce(mt4[0:w, gi:gi + 1],
                                              psX[0:w, gi * 128:(gi + 1) * 128],
                                              AX.X, ALU.max)
                        if gi == ngrp - 1:
                            k.inc(ins, "pb_s1")
                            k.wg(v, "pb_s1", p + 1)
                        else:
                            e = k.inc(ins, "pb_e")
                            k.wg(v, "pb_e", e)
                    e = k.inc(v.tensor_scalar(mt4[:, 4:4 + ngrp], mt4[:, 0:ngrp],
                                              1e-5, None, ALU.max), "pb_e")
                    k.wg(v, "pb_e", e)
                    if p >= 2:
                        v.wait_ge(k.sem(f"pb_rscst{(p - 2) % 2}"),
                                  rsc_target(p - 2))
                    e = k.inc(v.tensor_scalar(rsc4[p % 2][:, 0:ngrp],
                                              mt4[:, 4:4 + ngrp],
                                              R127, None, ALU.mult), "pb_rs")
                    k.wg(v, "pb_rs", e)
                    if p >= 1:
                        k.wg(v, "pb_tr", p)
                    e = k.inc(v.reciprocal(ng4[:, 0:ngrp], mt4[:, 4:4 + ngrp]),
                              "pb_e")
                    k.wg(v, "pb_e", e)
                    k.inc(v.tensor_scalar(ng4[:, 0:ngrp], ng4[:, 0:ngrp],
                                          -127.0, None, ALU.mult), "pb_ng")
                    k.wg(v, "pb_ng", p + 1)
                    k.wg(v, "pb_tr", p + 1)
                    k.inc(v.tensor_copy(srow[0:1, 0:ngrp * 128],
                                        psR[0:1, 0:ngrp * 128]), "pb_s2")
                    k.wg(v, "pb_s2", p + 1)
                    k.wg(v, "pb_rp", p + 1)
                    if p >= 1:
                        k.wg(v, "pb_qf", 32 * p)
                    k.inc(v.tensor_copy(srep[:, 0:n_out], psR[:, 0:n_out]),
                          "pb_s3")
                    k.wg(v, "pb_s3", k.cnt["pb_s3"])
                if s < NS:
                    n_out = slabs[s][2]
                    for hc in range(HC):
                        ghc = s * HC + hc
                        bank = hc % 4
                        k.wg(v, "pb_p2", ghc + 1)
                        k.inc(v.scalar_tensor_tensor(
                            cacc[ghc % 3][:, 0:n_out],
                            psH[:, bank * 512 + 2:bank * 512 + 2 + n_out],
                            cwv[:, hc * 3 + 2:hc * 3 + 3],
                            cacc[ghc % 3][:, 0:n_out], ALU.mult, ALU.add),
                            "pb_cv")
                        if hc >= 2:
                            mxop(ghc - 2)
                        if s >= 1 and hc >= 1:
                            repair(s - 1, hc - 1)
                    if s >= 1:
                        repair(s - 1, HC - 1)
                else:
                    for hc in range(HC):
                        repair(s - 1, hc)



def emit_PB2(k):
    nc, d, g = k.nc, k.dram, k.g
    k.declare_stores("p2_st")
    with ExitStack() as es:
        sb = lambda n, s, t: es.enter_context(nc.sbuf_tensor(n, s, t))
        w2 = sb("p2_w2", [128, HC * D], BF16)
        hqt = [sb("p2_h0", [128, HC * 128], BF16), sb("p2_h1", [128, HC * 128], BF16)]
        rsc = [sb("p2_r0", [128, 1], F32), sb("p2_r1", [128, 1], F32)]
        gt = [sb("p2_g0", [128, D], F32), sb("p2_g1", [128, D], F32)]
        ps = es.enter_context(nc.psum_tensor("p2_ps", [128, 2 * D], F32))
        blk = es.enter_context(nc.Block())
        hq_d = d["hq_s"].rearrange("(c p) t -> p c t", p=128)

        @blk.sync
        def _(sync):
            _entry(k, sync)
            k.inc(sync.dma_start(
                out=w2.ap().rearrange("p (c e) -> p c e", c=HC),
                in_=d["w2qT_s"].rearrange("(c p) e -> p c e", p=128)),
                "p2_wl", 16)
            for i in range(NT):
                if i >= 2:
                    k.wg(sync, "p2_mm", i - 1)
                ldi(k, sync, "p2_ld", i,
                    hqt[i % 2].ap().rearrange("p (c t) -> p c t", c=HC),
                    hq_d[:, :, i * 128:(i + 1) * 128])
                if i >= 2:
                    k.wg(sync, "p2_ev", 2 * (i - 1))
                ldi(k, sync, "p2_ld", i, rsc[i % 2][:],
                    d["rsc_s"][i * 128:(i + 1) * 128, :])

        @blk.tensor
        def _(te):
            _entry(k, te)
            k.wg(te, "p2_wl", 16)
            for i in range(NT):
                p = i % 2
                bank = i % 2
                ldw(k, te, "p2_ld", i, n=32)
                if i >= 2:
                    k.wg(te, "p2_ev", 2 * (i - 1))
                for hc in range(HC):
                    for half in range(2):
                        ins = te.matmul(
                            ps[:, bank * D + half * 512:bank * D + (half + 1) * 512],
                            hqt[p][:, hc * 128:(hc + 1) * 128],
                            w2[:, hc * D + half * 512:hc * D + (half + 1) * 512],
                            start=(hc == 0), stop=(hc == HC - 1))
                k.inc(ins, "p2_mm")

        @blk.scalar
        def _(sc):
            _entry(k, sc)
            for i in range(NT):
                p = i % 2
                bank = i % 2
                k.wg(sc, "p2_mm", i + 1)
                ldw(k, sc, "p2_ld", i, n=32)
                stw(k, sc, "p2_st", i)
                for half in range(2):
                    ins = sc.activation(gt[p][:, half * 512:(half + 1) * 512],
                                        ps[:, bank * D + half * 512:
                                           bank * D + (half + 1) * 512],
                                        AF.Silu, scale=rsc[p][:, 0:1])
                k.inc(ins, "p2_ev", 2)

        @blk.gpsimd
        def _(gp):
            _entry(k, gp)
            for i in range(NT):
                p = i % 2
                k.wg(gp, "p2_ev", 2 * (i + 1))
                sti(k, gp, "p2_st", i, d["g_s"][i * 128:(i + 1) * 128, :],
                    gt[p][:])


def _nB(j):
    n = 0
    for t in range(j + 1):
        if (t % 16) + 1 > 8:
            n += 1
    return n


def emit_PD(k):
    nc, d, g = k.nc, k.dram, k.g
    k.declare_stores("pd_stx1", "pd_stn", "pd_stnT")
    with ExitStack() as es:
        sb = lambda n, s, t: es.enter_context(nc.sbuf_tensor(n, s, t))
        kTb = [sb("pd_kT0", [128, DC * S], BF16), sb("pd_kT1", [128, DC * S], BF16)]
        vb = [sb("pd_v0", [128, 16 * D], BF16), sb("pd_v1", [128, 16 * D], BF16)]
        qT = [sb("pd_q0", [128, DC * 128], BF16), sb("pd_q1", [128, DC * 128], BF16)]
        gtl = [sb(f"pd_g{i}", [128, D], F32) for i in range(3)]
        xtl = [sb(f"pd_xx{i}", [128, D], F32) for i in range(3)]
        ptile = [sb("pd_p0", [128, S], BF16), sb("pd_p1", [128, S], BF16)]
        pT = [sb("pd_pT0", [128, S], BF16), sb("pd_pT1", [128, S], BF16)]
        stt = [sb("pd_st0", [128, 16], F32), sb("pd_st1", [128, 16], F32)]
        t_sb = sb("pd_t", [128, D], F32)
        x1sb = [sb(f"pd_x1{i}", [128, D], F32) for i in range(3)]
        sq = sb("pd_sq", [128, D], F32)
        epsc = sb("pd_eps", [128, 1], F32)
        xn1 = [sb("pd_n0", [128, D], BF16), sb("pd_n1", [128, D], BF16)]
        xn1T = [sb("pd_nT0", [128, D], BF16), sb("pd_nT1", [128, D], BF16)]
        psS = es.enter_context(nc.psum_tensor("pd_psS", [128, 2048], F32))
        psT = es.enter_context(nc.psum_tensor("pd_psT", [128, 1024], BF16))
        psN = es.enter_context(nc.psum_tensor("pd_psN", [128, 1024], BF16))
        psC = es.enter_context(nc.psum_tensor("pd_psC", [128, D], F32))
        blk = es.enter_context(nc.Block())
        qT_d = d["qT_s"].rearrange("(c p) t -> p c t", p=128)
        kT_d = d["kT_s"].rearrange("(c p) t -> p c t", p=128)
        xn1T_d = d["xn1T_s"].rearrange("(c p) t -> p c t", p=128)

        def nkb_of(j):
            return (j % 16) + 1

        @blk.sync
        def _(sync):
            _entry(k, sync)
            for b in range(B_LOC):
                if b >= 2:
                    k.wg(sync, "pd_cx", 16 * (b - 1))
                ldi(k, sync, "pd_ldk", b,
                    kTb[b % 2].ap().rearrange("p (c t) -> p c t", c=DC),
                    kT_d[:, :, b * S:(b + 1) * S])
                ldi(k, sync, "pd_ldk", b,
                    vb[b % 2].ap().rearrange("p (kc e) -> p kc e", kc=16),
                    d["v_s"][b * S:(b + 1) * S, :].rearrange(
                        "(kc p) e -> p kc e", p=128))
                for qi in range(16):
                    i = b * 16 + qi
                    if i >= 2:
                        k.wg(sync, "pd_mm", i - 1)
                    ldi(k, sync, "pd_ldq", i,
                        qT[i % 2].ap().rearrange("p (c t) -> p c t", c=DC),
                        qT_d[:, :, i * 128:(i + 1) * 128])
                    if i >= 3:
                        k.wg(sync, "pd_x1p", i - 2)
                    k.inc(sync.dma_start(out=gtl[i % 3][:],
                                         in_=d["g_s"][i * 128:(i + 1) * 128, :]),
                          f"pd_ldg{i % 3}", 16)
                    k.inc(sync.dma_start(out=xtl[i % 3][:],
                                         in_=d["x"][i * 128:(i + 1) * 128, :]),
                          f"pd_ldg{i % 3}", 16)

        @blk.tensor
        def _(te):
            _entry(k, te)
            for i in range(NT + 6):
                if i < NT:
                    b, qi = i // 16, i % 16
                    pb, p = b % 2, i % 2
                    jd = qi // 4
                    wd = (qi % 4 + 1) * 128
                    if qi == 0:
                        ldw(k, te, "pd_ldk", b, n=32)
                    ldw(k, te, "pd_ldq", i)
                    if i >= 1:
                        k.wg(te, "pd_exp", i)
                        k.wg(te, "pd_mmax", i)
                    for j in range(jd + 1):
                        wj = 512 if j < jd else wd
                        for dc2 in range(DC):
                            ins = te.matmul(
                                psS[:, j * 512:j * 512 + wj],
                                qT[p][:, dc2 * 128:(dc2 + 1) * 128],
                                kTb[pb][:, dc2 * S + j * 512:dc2 * S + j * 512 + wj],
                                start=(dc2 == 0),
                                stop=(dc2 == DC - 1 and j != jd))
                        if j == jd:
                            ins = te.matmul(psS[:, j * 512 + wd - 128:j * 512 + wd],
                                            g["idenb"][:], g["cmaskb"][:],
                                            start=False, stop=True)
                    k.inc(ins, "pd_mm")
                if 1 <= i <= NT:
                    j = i - 1
                    cA = min(nkb_of(j), 8)
                    k.wg(te, "pd_exp", j + 1)
                    if j >= 1:
                        k.wg(te, "pd_pvB", _nB(j - 1))
                        k.wg(te, "pd_pvA", j)
                    for kc in range(cA):
                        ins = te.transpose(psT[:, kc * 128:(kc + 1) * 128],
                                           ptile[j % 2][:, kc * 128:(kc + 1) * 128],
                                           g["idenb"][:])
                    k.inc(ins, "pd_tpA")
                if 5 <= i <= NT + 4:
                    j = i - 5
                    k.wg(te, "pd_xn1", j + 1)
                    if j >= 1:
                        k.wg(te, "pd_ntev", j)
                    for c in range(DC):
                        ins = te.transpose(psN[:, c * 128:(c + 1) * 128],
                                           xn1[j % 2][:, c * 128:(c + 1) * 128],
                                           g["idenb"][:])
                    k.inc(ins, "pd_tn")
                if 1 <= i <= NT:
                    j = i - 1
                    if nkb_of(j) > 8:
                        k.wg(te, "pd_pvA", j + 1)
                        for kc in range(8, nkb_of(j)):
                            ins = te.transpose(
                                psT[:, (kc - 8) * 128:(kc - 7) * 128],
                                ptile[j % 2][:, kc * 128:(kc + 1) * 128],
                                g["idenb"][:])
                        k.inc(ins, "pd_tpB")
                if 2 <= i <= NT + 1:
                    m = i - 2
                    bm = m // 16
                    nkb = nkb_of(m)
                    k.wg(te, "pd_pvA", m + 1)
                    k.wg(te, "pd_pvB", _nB(m))
                    if m >= 1:
                        k.wg(te, "pd_gm", m)
                    for kc in range(nkb):
                        for half in range(2):
                            ins = te.matmul(
                                psC[:, half * 512:(half + 1) * 512],
                                pT[m % 2][:, kc * 128:(kc + 1) * 128],
                                vb[bm % 2][:, kc * D + half * 512:
                                           kc * D + (half + 1) * 512],
                                start=(kc == 0), stop=(kc == nkb - 1))
                    k.inc(ins, "pd_cx")

        @blk.vector
        def _(v):
            _entry(k, v)
            e = k.inc(v.memset(epsc[:], 1e-5), "pd_einit")
            for i in range(NT + 6):
                if i < NT:
                    qi = i % 16
                    nk = 512 * (qi // 4) + (qi % 4 + 1) * 128
                    st_ = stt[i % 2]
                    k.wg(v, "pd_mm", i + 1)
                    if i >= 2:
                        k.wg(v, "pd_exp", i - 1)
                    e = k.inc(v.tensor_reduce(st_[:, 0:1], psS[:, 0:nk], AX.X,
                                              ALU.max), "pd_e")
                    k.wg(v, "pd_e", e)
                    k.inc(v.tensor_scalar(st_[:, 1:2], st_[:, 0:1], -1.0 / 32.0,
                                          None, ALU.mult), "pd_mmax")
                if 2 <= i <= NT + 1:
                    m = i - 2
                    if m >= 2:
                        k.wg(v, "pd_gm", m - 1)
                    e = k.inc(v.reciprocal(stt[m % 2][:, 3:4],
                                           stt[m % 2][:, 2:3]), "pd_rs")
                    k.wg(v, "pd_rs", e)
                if 1 <= i <= NT:
                    j = i - 1
                    cA = min(nkb_of(j), 8)
                    k.wg(v, "pd_tpA", j + 1)
                    if j >= 2:
                        k.wg(v, "pd_cx", j - 1)
                    k.inc(v.tensor_copy(pT[j % 2][:, 0:cA * 128],
                                        psT[:, 0:cA * 128]), "pd_pvA")
                if 1 <= i <= NT:
                    j = i - 1
                    if nkb_of(j) > 8:
                        k.wg(v, "pd_tpB", _nB(j))
                        k.inc(v.tensor_copy(
                            pT[j % 2][:, 1024:nkb_of(j) * 128],
                            psT[:, 0:(nkb_of(j) - 8) * 128]), "pd_pvB")
                if 3 <= i <= NT + 2:
                    m = i - 3
                    k.wg(v, "pd_cx", m + 1)
                    k.wg(v, f"pd_ldg{m % 3}", 32 * (m // 3 + 1))
                    if m >= 1:
                        k.wg(v, "pd_x1p", m)
                    k.inc(v.scalar_tensor_tensor(t_sb[:], psC[:],
                                                 stt[m % 2][:, 3:4],
                                                 gtl[m % 3][:],
                                                 ALU.mult, ALU.mult), "pd_gm")
                if 5 <= i <= NT + 4:
                    j = i - 5
                    k.wg(v, "pd_tn", j + 1)
                    stw(k, v, "pd_stnT", j)
                    k.inc(v.tensor_copy(xn1T[j % 2][:], psN[:]), "pd_ntev")

        @blk.gpsimd
        def _(gp):
            _entry(k, gp)
            for i in range(NT + 6):
                if 4 <= i <= NT + 3:
                    m = i - 4
                    k.wg(gp, "pd_gm", m + 1)
                    k.wg(gp, f"pd_ldg{m % 3}", 32 * (m // 3 + 1))
                    stw(k, gp, "pd_stx1", m)
                    if m >= 3:
                        k.wg(gp, "pd_xn1", m - 2)
                    k.inc(gp.tensor_tensor(x1sb[m % 3][:], t_sb[:],
                                           xtl[m % 3][:], ALU.add), "pd_x1p")

        @blk.scalar
        def _(sc):
            _entry(k, sc)

            def ae(ins):
                e = k.inc(ins, "pd_ae")
                k.wg(sc, "pd_ae", e)

            for i in range(NT + 6):
                if i < NT:
                    qi = i % 16
                    nk = 512 * (qi // 4) + (qi % 4 + 1) * 128
                    st_ = stt[i % 2]
                    k.wg(sc, "pd_mmax", i + 1)
                    if i >= 2:
                        k.wg(sc, "pd_tpA", i - 1)
                        k.wg(sc, "pd_tpB", _nB(i - 2))
                        k.wg(sc, "pd_rs", i - 1)
                    k.inc(sc.activation(ptile[i % 2][:, 0:nk], psS[:, 0:nk],
                                        AF.Exp, bias=st_[:, 1:2],
                                        scale=1.0 / 32.0,
                                        accum_out=st_[:, 2:3]), "pd_exp")
                if 4 <= i <= NT + 3:
                    j = i - 4
                    st_ = stt[j % 2]
                    k.wg(sc, "pd_x1p", j + 1)
                    if j == 0:
                        k.wg(sc, "pd_einit", 1)
                    ae(sc.activation(sq[:], x1sb[j % 3][:], AF.Identity,
                                     accum_out=st_[:, 4:5]))
                    ae(sc.activation(sq[:], x1sb[j % 3][:], AF.Square,
                                     accum_out=st_[:, 5:6]))
                    sti(k, sc, "pd_stx1", j, d["x1_s"][j * 128:(j + 1) * 128, :],
                        x1sb[j % 3][:])
                    ae(sc.activation(st_[:, 6:7], st_[:, 4:5], AF.Identity,
                                     scale=1.0 / D))
                    ae(sc.activation(st_[:, 7:8], st_[:, 6:7], AF.Square))
                    ae(sc.activation(st_[:, 8:9], st_[:, 7:8], AF.Identity,
                                     scale=-1.0))
                    ae(sc.activation(st_[:, 9:10], st_[:, 5:6], AF.Identity,
                                     scale=1.0 / D, bias=st_[:, 8:9]))
                    ae(sc.activation(st_[:, 10:11], st_[:, 9:10], AF.Ln,
                                     bias=epsc[:, 0:1]))
                    ae(sc.activation(st_[:, 13:14], st_[:, 10:11], AF.Exp,
                                     scale=-0.5))
                    ae(sc.activation(st_[:, 11:12], st_[:, 6:7], AF.Identity,
                                     scale=st_[:, 13:14]))
                    ae(sc.activation(st_[:, 12:13], st_[:, 11:12], AF.Identity,
                                     scale=-1.0))
                    if j >= 2:
                        k.wg(sc, "pd_tn", j - 1)
                    stw(k, sc, "pd_stn", j)
                    k.inc(sc.activation(xn1[j % 2][:], x1sb[j % 3][:],
                                        AF.Identity, bias=st_[:, 12:13],
                                        scale=st_[:, 13:14]), "pd_xn1")
                    k.wg(sc, "pd_xn1", j + 1)
                    sti(k, sc, "pd_stn", j, d["xn1_s"][j * 128:(j + 1) * 128, :],
                        xn1[j % 2][:])
                if 5 <= i <= NT + 4:
                    j = i - 5
                    k.wg(sc, "pd_ntev", j + 1)
                    sti(k, sc, "pd_stnT", j,
                        xn1T_d[:, :, j * 128:(j + 1) * 128],
                        xn1T[j % 2].ap().rearrange("p (c t) -> p c t", c=DC))



def emit_PE1(k):
    nc, d, g = k.nc, k.dram, k.g
    k.declare_stores("pe_st")
    NS = T // 512
    with ExitStack() as es:
        sb = lambda n, s, t: es.enter_context(nc.sbuf_tensor(n, s, t))
        wq = sb("pe_wq", [128, DC * D], BF16)
        wk = sb("pe_wk", [128, DC * D], BF16)
        wv = sb("pe_wv", [128, DC * D], BF16)
        xs = [sb("pe_x0", [128, DC * 512], BF16), sb("pe_x1", [128, DC * 512], BF16)]
        qs = [sb("pe_q0", [128, DC * 512], BF16), sb("pe_q1", [128, DC * 512], BF16)]
        ks_ = [sb("pe_k0", [128, DC * 512], BF16), sb("pe_k1", [128, DC * 512], BF16)]
        vs = [sb("pe_v0", [128, 4 * D], BF16), sb("pe_v1", [128, 4 * D], BF16)]
        psq = es.enter_context(nc.psum_tensor("pe_psq", [128, 1024], F32))
        psk = es.enter_context(nc.psum_tensor("pe_psk", [128, 1024], F32))
        psv = es.enter_context(nc.psum_tensor("pe_psv", [128, 2048], F32))
        blk = es.enter_context(nc.Block())
        xn1T_d = d["xn1T_s"].rearrange("(c p) t -> p c t", p=128)
        q1T_d = d["q1T_s"].rearrange("(c p) t -> p c t", p=128)
        k1T_d = d["k1T_s"].rearrange("(c p) t -> p c t", p=128)

        @blk.sync
        def _(sync):
            _entry(k, sync)
            for s in range(NS):
                if s >= 2:
                    k.wg(sync, "pe_mv", 4 * (s - 1))
                ldi(k, sync, "pe_ld", s,
                    xs[s % 2].ap().rearrange("p (c t) -> p c t", c=DC),
                    xn1T_d[:, :, s * 512:(s + 1) * 512])

        @blk.gpsimd
        def _(gp):
            _entry(k, gp)
            for wsb, wnm in ((wq, "Wql"), (wk, "Wkl"), (wv, "Wvl")):
                k.inc(gp.dma_start(
                    out=wsb.ap().rearrange("p (c e) -> p c e", c=DC),
                    in_=d[wnm].rearrange("(c p) e -> p c e", p=128)), "pe_wl", 16)
            for s in range(NS):
                p = s % 2
                k.wg(gp, "pe_eq", DC * (s + 1))
                sti(k, gp, "pe_st", s, q1T_d[:, :, s * 512:(s + 1) * 512],
                    qs[p].ap().rearrange("p (c t) -> p c t", c=DC))
                k.wg(gp, "pe_ek", DC * (s + 1))
                sti(k, gp, "pe_st", s, k1T_d[:, :, s * 512:(s + 1) * 512],
                    ks_[p].ap().rearrange("p (c t) -> p c t", c=DC))
                k.wg(gp, "pe_ev", 4 * (s + 1))
                sti(k, gp, "pe_st", s,
                    d["v1_s"][s * 512:(s + 1) * 512, :].rearrange(
                        "(tt p) e -> p tt e", p=128),
                    vs[p].ap().rearrange("p (tt e) -> p tt e", tt=4))

        @blk.tensor
        def _(te):
            _entry(k, te)
            k.wg(te, "pe_wl", 16 * 3)
            for s in range(NS):
                p = s % 2
                ldw(k, te, "pe_ld", s)
                for wsb, pst, msem, esem in ((wq, psq, "pe_mq", "pe_eq"),
                                             (wk, psk, "pe_mk", "pe_ek")):
                    for ec in range(DC):
                        gq = s * DC + ec
                        bank = ec % 2
                        if gq >= 2:
                            k.wg(te, esem, gq - 1)
                        for dc2 in range(DC):
                            ins = te.matmul(pst[:, bank * 512:(bank + 1) * 512],
                                            wsb[:, dc2 * D + ec * 128:
                                                dc2 * D + (ec + 1) * 128],
                                            xs[p][:, dc2 * 512:(dc2 + 1) * 512],
                                            start=(dc2 == 0),
                                            stop=(dc2 == DC - 1))
                        k.inc(ins, msem)
                for tt in range(4):
                    gv = 4 * s + tt
                    bank = tt % 2
                    if gv >= 2:
                        k.wg(te, "pe_ev", gv - 1)
                    for dc2 in range(DC):
                        for half in range(2):
                            ins = te.matmul(
                                psv[:, bank * D + half * 512:
                                    bank * D + (half + 1) * 512],
                                xs[p][:, dc2 * 512 + tt * 128:
                                      dc2 * 512 + (tt + 1) * 128],
                                wv[:, dc2 * D + half * 512:
                                   dc2 * D + (half + 1) * 512],
                                start=(dc2 == 0), stop=(dc2 == DC - 1))
                    k.inc(ins, "pe_mv")

        @blk.scalar
        def _(sc):
            _entry(k, sc)
            for s in range(NS):
                p = s % 2
                stw(k, sc, "pe_st", s, n=48)
                for pst, out, msem, esem in ((psq, qs, "pe_mq", "pe_eq"),
                                             (psk, ks_, "pe_mk", "pe_ek")):
                    for ec in range(DC):
                        bank = ec % 2
                        k.wg(sc, msem, s * DC + ec + 1)
                        k.inc(sc.activation(out[p][:, ec * 512:(ec + 1) * 512],
                                            pst[:, bank * 512:(bank + 1) * 512],
                                            AF.Identity), esem)

        @blk.vector
        def _(v):
            _entry(k, v)
            for s in range(NS):
                p = s % 2
                stw(k, v, "pe_st", s, n=48)
                for tt in range(4):
                    bank = tt % 2
                    k.wg(v, "pe_mv", 4 * s + tt + 1)
                    k.inc(v.tensor_copy(vs[p][:, tt * D:(tt + 1) * D],
                                        psv[:, bank * D:(bank + 1) * D]),
                          "pe_ev")





def _b_of(hl):
    return (hl % 2) * 4 + hl // 2


def emit_PF1(k):
    nc, d, g = k.nc, k.dram, k.g
    k.declare_stores("pf_yst", "pf_gin", "pf_gcst")
    NBC = B_LOC * (S // 256)
    with ExitStack() as es:
        sb = lambda n, s, t: es.enter_context(nc.sbuf_tensor(n, s, t))
        wol = sb("pf_wol", [128, DC * D], BF16)
        wg = {nm: sb("pf_" + nm, [128, DC * D], BF16)
              for nm in ("Wqg", "Wkg", "Wvg", "Wog")}
        q1 = [sb("pf_q0", [128, DC * 256], BF16), sb("pf_q1", [128, DC * 256], BF16)]
        k1 = [sb("pf_k0", [128, DC * 256], BF16), sb("pf_k1", [128, DC * 256], BF16)]
        v1 = [sb("pf_v0", [128, 2 * D], BF16), sb("pf_v1", [128, 2 * D], BF16)]
        xn1 = [sb("pf_xn0", [128, 2 * D], BF16), sb("pf_xn1", [128, 2 * D], BF16)]
        p_sb = sb("pf_p", [128, 4096], BF16)
        pT_sb = sb("pf_pT", [128, 2 * 2048], BF16)
        ctx_sb = sb("pf_ctx", [128, D], BF16)
        ctxT_sb = sb("pf_ctxT", [128, D], BF16)
        y_sb = [sb("pf_y0", [128, D], BF16), sb("pf_y1", [128, D], BF16)]
        stt = sb("pf_stt", [128, 24], F32)
        grow = sb("pf_grow", [1, D], F32)
        gin = sb("pf_gin", [32, D], F32)
        ginT = sb("pf_ginT", [128, 8 * 32], BF16)
        qgT = sb("pf_qgT", [64, 16 * 32], BF16)
        kgT = sb("pf_kgT", [64, 16 * 32], BF16)
        vg = sb("pf_vg", [32, D], BF16)
        pg = sb("pf_pg", [32, 512], BF16)
        pgT = sb("pf_pgT", [32, 512], BF16)
        ctxg = sb("pf_ctxg", [32, D], BF16)
        ctxgT = sb("pf_ctxgT", [128, 8 * 32], BF16)
        gst = sb("pf_gst", [32, 24], F32)
        gc_sb = sb("pf_gc", [32, D], F32)
        psS = es.enter_context(nc.psum_tensor("pf_psS", [128, 2048], F32))
        psT = es.enter_context(nc.psum_tensor("pf_psT", [128, 2048], BF16))
        psC = es.enter_context(nc.psum_tensor("pf_psC", [128, 1024], F32))
        blk = es.enter_context(nc.Block())
        q1T_d = d["q1T_s"].rearrange("(c p) t -> p c t", p=128)
        k1T_d = d["k1T_s"].rearrange("(c p) t -> p c t", p=128)

        @blk.sync
        def _(sync):
            _entry(k, sync)
            for bc in range(NBC):
                t0 = bc * 256
                p = bc % 2
                if bc >= 2:
                    k.wg(sync, "pf_mm", 4 * (bc - 1))
                ldi(k, sync, "pf_ld", bc,
                    q1[p].ap().rearrange("p (c t) -> p c t", c=DC),
                    q1T_d[:, :, t0:t0 + 256])
                ldi(k, sync, "pf_ld", bc,
                    k1[p].ap().rearrange("p (c t) -> p c t", c=DC),
                    k1T_d[:, :, t0:t0 + 256])
                if bc >= 2:
                    k.wg(sync, "pf_cx", 2 * (bc - 1))
                ldi(k, sync, "pf_ld", bc,
                    v1[p].ap().rearrange("p (kc e) -> p kc e", kc=2),
                    d["v1_s"][t0:t0 + 256, :].rearrange(
                        "(kc p) e -> p kc e", p=128))
                if bc >= 2:
                    k.wg(sync, "pf_y", 2 * (bc - 1))
                ldi(k, sync, "pf_ld", bc,
                    xn1[p].ap().rearrange("p (tt e) -> p tt e", tt=2),
                    d["xn1_s"][t0:t0 + 256, :].rearrange(
                        "(tt p) e -> p tt e", p=128))
            k.wg(sync, "pf_gin0", 16 * (NBC // 2))
            k.wg(sync, "pf_gin1", 16 * (NBC // 2))
            k.inc(sync.dma_start(out=gin[:], in_=d["gc_s"][:]), "pf_gld", 16)

        @blk.tensor
        def _(te):
            _entry(k, te)
            k.wg(te, "pf_wl", 16 * 5)
            for bc in range(NBC):
                p = bc % 2
                ldw(k, te, "pf_ld", bc, n=64)
                for qs_ in range(2):
                    iq = 2 * bc + qs_
                    for r in range(2):
                        gr = 2 * iq + r
                        if gr >= 1:
                            k.wg(te, "pf_ex", gr)
                        if r == 0 and iq >= 1:
                            k.wg(te, "pf_gn", iq)
                        for hl in range(8):
                            j = (8 * r + hl) % 2
                            ec = (8 * r + hl) // 2
                            cS = (hl % 2) * 1024 + (hl // 2) * 256
                            ins = te.matmul(
                                psS[:, cS:cS + 256],
                                q1[p][64 * j:64 * (j + 1),
                                      ec * 256 + qs_ * 128:
                                      ec * 256 + (qs_ + 1) * 128],
                                k1[p][64 * j:64 * (j + 1),
                                      ec * 256:(ec + 1) * 256],
                                start=True, stop=True,
                                tile_position=(64 * j, 0))
                        k.inc(ins, "pf_mm")
                        k.wg(te, "pf_ex", gr + 1)
                        if gr >= 1:
                            k.wg(te, "pf_pev", gr)
                        if iq >= 1:
                            k.wg(te, "pf_ctev", iq)
                        for b in range(8):
                            for kc in range(2):
                                ins = te.transpose(
                                    psT[:, (b * 2 + kc) * 128:
                                        (b * 2 + kc + 1) * 128],
                                    p_sb[:, r * 2048 + b * 256 + kc * 128:
                                         r * 2048 + b * 256 + (kc + 1) * 128],
                                    g["idenb"][:])
                        k.inc(ins, "pf_tp")
                    k.wg(te, "pf_pev", 2 * iq + 2)
                    if iq >= 1:
                        k.wg(te, "pf_y", iq)
                    for h in range(16):
                        r, hl = h // 8, h % 8
                        b = _b_of(hl)
                        for kc in range(2):
                            ins = te.matmul(
                                psC[:, h * 64:(h + 1) * 64],
                                pT_sb[:, r * 2048 + (b * 2 + kc) * 128:
                                      r * 2048 + (b * 2 + kc + 1) * 128],
                                v1[p][:, kc * D + h * 64:kc * D + (h + 1) * 64],
                                start=(kc == 0), stop=(kc == 1))
                    k.inc(ins, "pf_cx")
                    k.wg(te, "pf_cev", iq + 1)
                    for c in range(DC):
                        ins = te.transpose(psT[:, c * 128:(c + 1) * 128],
                                           ctx_sb[:, c * 128:(c + 1) * 128],
                                           g["idenb"][:])
                    k.inc(ins, "pf_ctt")
                    k.wg(te, "pf_ctev", iq + 1)
                    k.wg(te, "pf_cev", iq + 1)
                    for ec in range(DC):
                        for half in range(2):
                            ins = te.matmul(
                                psC[:, half * 512:(half + 1) * 512],
                                ctxT_sb[:, ec * 128:(ec + 1) * 128],
                                wol[:, ec * D + half * 512:
                                    ec * D + (half + 1) * 512],
                                start=(ec == 0), stop=(ec == DC - 1))
                    k.inc(ins, "pf_wo")
                    k.wg(te, "pf_y", iq + 1)
                    k.wg(te, "pf_ex", 2 * iq + 2)
                    te.matmul(psS[0:1, 0:512], g["onescolb"][:, 0:1],
                              y_sb[qs_][:, 0:512], start=True, stop=True)
                    k.inc(te.matmul(psS[0:1, 512:1024], g["onescolb"][:, 0:1],
                                    y_sb[qs_][:, 512:1024], start=True,
                                    stop=True), "pf_sum")

            k.wg(te, "pf_gld", 16)
            k.wg(te, "pf_y", 2 * NBC)
            k.wg(te, "pf_gn", 2 * NBC)
            for c in range(DC):
                ins = te.transpose(psC[:, c * 32:(c + 1) * 32],
                                   gin[:, c * 128:(c + 1) * 128],
                                   g["iden32"][0:32, 0:32])
            k.inc(ins, "pf_g1")
            k.wg(te, "pf_g2", 1)
            for wnm, ofs in (("Wqg", 0), ("Wkg", 256)):
                for ec in range(DC):
                    for dc2 in range(DC):
                        ins = te.matmul(
                            psS[:, ofs + ec * 32:ofs + (ec + 1) * 32],
                            wg[wnm][:, dc2 * D + ec * 128:dc2 * D + (ec + 1) * 128],
                            ginT[:, dc2 * 32:(dc2 + 1) * 32],
                            start=(dc2 == 0), stop=(dc2 == DC - 1))
            k.inc(ins, "pf_g3")
            for dc2 in range(DC):
                for half in range(2):
                    ins = te.matmul(
                        psS[0:32, 512 + half * 512:1024 + half * 512],
                        ginT[:, dc2 * 32:(dc2 + 1) * 32],
                        wg["Wvg"][:, dc2 * D + half * 512:
                                  dc2 * D + (half + 1) * 512],
                        start=(dc2 == 0), stop=(dc2 == DC - 1))
            k.inc(ins, "pf_g4")
            k.wg(te, "pf_g5", 1)
            for h in range(16):
                te.matmul(psC[0:32, h * 32:(h + 1) * 32],
                          g["idenb"][0:32, 0:32], g["gmaskb"][:],
                          start=True, stop=False)
                ins = te.matmul(psC[0:32, h * 32:(h + 1) * 32],
                                qgT[:, h * 32:(h + 1) * 32],
                                kgT[:, h * 32:(h + 1) * 32],
                                start=False, stop=True)
            k.inc(ins, "pf_g6")
            k.wg(te, "pf_g7", 1)
            for h in range(16):
                ins = te.transpose(psT[0:32, h * 32:(h + 1) * 32],
                                   pg[:, h * 32:(h + 1) * 32],
                                   g["idenb"][0:32, 0:32])
            k.inc(ins, "pf_g8")
            k.wg(te, "pf_g9", 1)
            for h in range(16):
                ins = te.matmul(psC[0:32, h * 64:(h + 1) * 64],
                                pgT[:, h * 32:(h + 1) * 32],
                                vg[:, h * 64:(h + 1) * 64],
                                start=True, stop=True)
            k.inc(ins, "pf_g10")
            k.wg(te, "pf_g11", 1)
            for c in range(DC):
                ins = te.transpose(psT[:, 1024 + c * 32:1024 + (c + 1) * 32],
                                   ctxg[:, c * 128:(c + 1) * 128],
                                   g["idenb"][0:32, 0:32])
            k.inc(ins, "pf_g12")
            k.wg(te, "pf_g13", 1)
            for ec in range(DC):
                for half in range(2):
                    ins = te.matmul(
                        psS[0:32, half * 512:(half + 1) * 512],
                        ctxgT[:, ec * 32:(ec + 1) * 32],
                        wg["Wog"][:, ec * D + half * 512:
                                  ec * D + (half + 1) * 512],
                        start=(ec == 0), stop=(ec == DC - 1))
            k.inc(ins, "pf_g14")

        @blk.vector
        def _(v):
            _entry(k, v)
            for bc in range(NBC):
                p = bc % 2
                for qs_ in range(2):
                    iq = 2 * bc + qs_
                    for r in range(2):
                        gr = 2 * iq + r
                        k.wg(v, "pf_mm", gr + 1)
                        if gr >= 2:
                            k.wg(v, "pf_ex", gr - 1)
                        e = k.inc(v.tensor_reduce(stt[:, 2 * r:2 * r + 1],
                                                  psS[:, 0:2048], AX.X, ALU.max),
                                  "pf_e")
                        k.wg(v, "pf_e", e)
                        k.inc(v.tensor_scalar(stt[:, 2 * r + 1:2 * r + 2],
                                              stt[:, 2 * r:2 * r + 1],
                                              -0.125, None, ALU.mult), "pf_mx")
                        k.wg(v, "pf_ex", gr + 1)
                        k.inc(v.reciprocal(stt[:, 4 + 8 * r:12 + 8 * r],
                                           stt[:, 4 + 8 * r:12 + 8 * r]),
                              "pf_sm")
                    k.wg(v, "pf_cx", iq + 1)
                    k.wg(v, "pf_sm", 2 * iq + 2)
                    if iq >= 1:
                        k.wg(v, "pf_ctt", iq)
                    k.inc(v.tensor_tensor(
                        bass.AP(ctx_sb, 0, [[1024, 128], [64, 16], [1, 64]]),
                        bass.AP(psC, 0, [[1024, 128], [64, 16], [1, 64]]),
                        bass.AP(stt, 4, [[24, 128], [1, 16], [0, 64]]),
                        ALU.mult), "pf_cev")
                    k.wg(v, "pf_ctt", iq + 1)
                    if iq >= 1:
                        k.wg(v, "pf_wo", iq)
                    k.inc(v.tensor_copy(ctxT_sb[:], psT[:, 0:1024]), "pf_ctev")
                    k.wg(v, "pf_wo", iq + 1)
                    ldw(k, v, "pf_ld", bc, n=64)
                    stw(k, v, "pf_yst", iq)
                    if iq >= 2:
                        k.wg(v, "pf_sum", iq - 1)
                    k.inc(v.tensor_tensor(y_sb[qs_][:], psC[:, 0:1024],
                                          xn1[p][:, qs_ * D:(qs_ + 1) * D],
                                          ALU.add), "pf_y")
                    k.wg(v, "pf_sum", iq + 1)
                    if qs_ == 0:
                        stw(k, v, "pf_gin", bc + 1)
                        k.inc(v.tensor_scalar(grow[0:1, :], psS[0:1, 0:1024],
                                              1.0 / 256.0, None, ALU.mult),
                              "pf_gn")
                        k.wg(v, "pf_gn", k.cnt["pf_gn"])
                    else:
                        k.inc(v.scalar_tensor_tensor(grow[0:1, :],
                                                     psS[0:1, 0:1024],
                                                     1.0 / 256.0, grow[0:1, :],
                                                     ALU.mult, ALU.add),
                              "pf_gn")

            k.wg(v, "pf_g1", 1)
            k.inc(v.tensor_copy(ginT[:], psC[:, 0:256]), "pf_g2")
            k.wg(v, "pf_g3", 1)
            for ec in range(DC):
                v.tensor_copy(qgT[0:64, (2 * ec) * 32:(2 * ec + 1) * 32],
                              psS[0:64, ec * 32:(ec + 1) * 32])
                v.tensor_copy(qgT[0:64, (2 * ec + 1) * 32:(2 * ec + 2) * 32],
                              psS[64:128, ec * 32:(ec + 1) * 32])
                v.tensor_copy(kgT[0:64, (2 * ec) * 32:(2 * ec + 1) * 32],
                              psS[0:64, 256 + ec * 32:256 + (ec + 1) * 32])
                v.tensor_copy(kgT[0:64, (2 * ec + 1) * 32:(2 * ec + 2) * 32],
                              psS[64:128, 256 + ec * 32:256 + (ec + 1) * 32])
            k.wg(v, "pf_g4", 1)
            k.inc(v.tensor_copy(vg[:], psS[0:32, 512:1536]), "pf_g5")
            k.wg(v, "pf_g6", 1)
            e = k.inc(v.tensor_reduce(gst[:, 0:1], psC[0:32, 0:512], AX.X,
                                      ALU.max), "pf_e")
            k.wg(v, "pf_e", e)
            k.inc(v.tensor_scalar(gst[:, 1:2], gst[:, 0:1], -0.125, None,
                                  ALU.mult), "pf_gmx")
            k.wg(v, "pf_gex", 1)
            e = k.inc(v.tensor_reduce(
                bass.AP(gst, 4, [[24, 32], [1, 16], [1, 1]]),
                bass.AP(pg, 0, [[512, 32], [32, 16], [1, 32]]),
                AX.X, ALU.add), "pf_e")
            k.wg(v, "pf_e", e)
            e = k.inc(v.reciprocal(gst[:, 4:20], gst[:, 4:20]), "pf_e")
            k.wg(v, "pf_e", e)
            for h in range(16):
                ins = v.tensor_scalar(pg[:, h * 32:(h + 1) * 32],
                                      pg[:, h * 32:(h + 1) * 32],
                                      gst[:, 4 + h:5 + h], None, ALU.mult)
            k.inc(ins, "pf_g7")
            k.wg(v, "pf_g8", 1)
            k.inc(v.tensor_copy(pgT[:], psT[0:32, 0:512]), "pf_g9")
            k.wg(v, "pf_g10", 1)
            k.inc(v.tensor_copy(ctxg[:], psC[0:32, 0:1024]), "pf_g11")
            k.wg(v, "pf_g12", 1)
            k.inc(v.tensor_copy(ctxgT[:], psT[:, 1024:1280]), "pf_g13")
            k.wg(v, "pf_g14", 1)
            k.inc(v.tensor_tensor(gc_sb[:], psS[0:32, 0:1024], gin[:], ALU.add),
                  "pf_g15")

        @blk.scalar
        def _(sc):
            _entry(k, sc)
            for bc in range(NBC):
                for qs_ in range(2):
                    iq = 2 * bc + qs_
                    for r in range(2):
                        gr = 2 * iq + r
                        k.wg(sc, "pf_mx", gr + 1)
                        if gr >= 2:
                            k.wg(sc, "pf_tp", gr - 1)
                        if iq >= 1:
                            k.wg(sc, "pf_cev", iq)
                        for b in range(8):
                            hb = (b % 4) * 2 + b // 4
                            ins = sc.activation(
                                p_sb[:, r * 2048 + b * 256:
                                     r * 2048 + (b + 1) * 256],
                                psS[:, b * 256:(b + 1) * 256], AF.Exp,
                                bias=stt[:, 2 * r + 1:2 * r + 2],
                                scale=0.125,
                                accum_out=stt[:, 4 + 8 * r + hb:
                                              5 + 8 * r + hb])
                        k.inc(ins, "pf_ex")
                        k.wg(sc, "pf_tp", gr + 1)
                        if iq >= 1:
                            k.wg(sc, "pf_cx", iq)
                        k.inc(sc.activation(pT_sb[:, r * 2048:(r + 1) * 2048],
                                            psT[:], AF.Identity), "pf_pev")
            k.wg(sc, "pf_gmx", 1)
            k.inc(sc.activation(pg[:], psC[0:32, 0:512], AF.Exp,
                                bias=gst[:, 1:2], scale=0.125), "pf_gex")

        @blk.gpsimd
        def _(gp):
            _entry(k, gp)
            k.inc(gp.dma_start(
                out=wol.ap().rearrange("p (c e) -> p c e", c=DC),
                in_=d["Wol"].rearrange("(c p) e -> p c e", p=128)), "pf_wl", 16)
            for nm in ("Wqg", "Wkg", "Wvg", "Wog"):
                k.inc(gp.dma_start(
                    out=wg[nm].ap().rearrange("p (c e) -> p c e", c=DC),
                    in_=d[nm].rearrange("(c p) e -> p c e", p=128)), "pf_wl", 16)
            for bc in range(NBC):
                for qs_ in range(2):
                    iq = 2 * bc + qs_
                    k.wg(gp, "pf_y", iq + 1)
                    sti(k, gp, "pf_yst", iq, d["y_s"][iq * 128:(iq + 1) * 128, :],
                        y_sb[qs_][:])
                    if qs_ == 1:
                        k.wg(gp, "pf_gn", iq + 1)
                        sti(k, gp, "pf_gin", bc, d["gc_s"][bc:bc + 1, :],
                            grow[0:1, :])
            k.wg(gp, "pf_g15", 1)
            sti(k, gp, "pf_gcst", 0, d["gc_s"][:], gc_sb[:])



def emit_PG1(k):
    nc, d, g = k.nc, k.dram, k.g
    k.declare_stores("pg_st")
    with ExitStack() as es:
        sb = lambda n, s, t: es.enter_context(nc.sbuf_tensor(n, s, t))
        we = sb("pg_we", [128, DC * D], BF16)
        b1 = sb("pg_b1", [128, D], F32)
        xT = [sb("pg_xT0", [128, DC * 128], BF16), sb("pg_xT1", [128, DC * 128], BF16)]
        yt = [sb("pg_y0", [128, D], BF16), sb("pg_y1", [128, D], BF16)]
        x1t = [sb("pg_x10", [128, D], F32), sb("pg_x11", [128, D], F32)]
        grow = [sb("pg_gr0", [1, D], F32), sb("pg_gr1", [1, D], F32)]
        t1 = [sb("pg_t1a", [128, D], F32), sb("pg_t1b", [128, D], F32)]
        g1 = [sb("pg_g1a", [128, D], F32), sb("pg_g1b", [128, D], F32)]
        s1 = [sb("pg_s1a", [128, D], F32), sb("pg_s1b", [128, D], F32)]
        t2 = [sb("pg_t2a", [128, D], F32), sb("pg_t2b", [128, D], F32)]
        outb = [sb("pg_o0", [128, D], F32), sb("pg_o1", [128, D], F32)]
        psW = es.enter_context(nc.psum_tensor("pg_psW", [128, 2048], F32))
        psR = es.enter_context(nc.psum_tensor("pg_psR", [128, 1024], F32))
        blk = es.enter_context(nc.Block())
        xn1T_d = d["xn1T_s"].rearrange("(c p) t -> p c t", p=128)

        @blk.sync
        def _(sync):
            _entry(k, sync)
            k.inc(sync.dma_start(
                out=we.ap().rearrange("p (c e) -> p c e", c=DC),
                in_=d["weT_s"].rearrange("(c p) e -> p c e", p=128)), "pg_wl", 16)
            k.inc(sync.dma_start(out=b1[:], in_=d["bias1Rep"][:]), "pg_wl", 16)
            for i in range(NT):
                p = i % 2
                if i % 2 == 0:
                    ch = i // 2
                    if ch >= 2:
                        k.wg(sync, "pg_rep", ch - 1)
                    ldi(k, sync, "pg_ldr", ch, grow[ch % 2][:],
                        d["gc_s"][ch:ch + 1, :])
                if i >= 2:
                    k.wg(sync, "pg_mm", i - 1)
                ldi(k, sync, "pg_ldx", i,
                    xT[p].ap().rearrange("p (c t) -> p c t", c=DC),
                    xn1T_d[:, :, i * 128:(i + 1) * 128])
                if i >= 2:
                    k.wg(sync, "pg_s1", i - 1)
                ldi(k, sync, "pg_ldy", i, yt[p][:],
                    d["y_s"][i * 128:(i + 1) * 128, :])
                if i >= 2:
                    k.wg(sync, "pg_out", i - 1)
                ldi(k, sync, "pg_ldy", i, x1t[p][:],
                    d["x1_s"][i * 128:(i + 1) * 128, :])
            k.wg(sync, "pg_st0", 16 * (NT // 2))
            k.wg(sync, "pg_st1", 16 * (NT // 2))

        @blk.tensor
        def _(te):
            _entry(k, te)
            k.wg(te, "pg_wl", 32)
            for i in range(NT):
                p = i % 2
                if i % 2 == 0:
                    ch = i // 2
                    ldw(k, te, "pg_ldr", ch)
                    if ch >= 1:
                        k.wg(te, "pg_s1", 2 * ch)
                    te.matmul(psR[:, 0:512], g["onesr32"][0:1, :],
                              grow[ch % 2][0:1, 0:512], start=True, stop=True)
                    k.inc(te.matmul(psR[:, 512:1024], g["onesr32"][0:1, :],
                                    grow[ch % 2][0:1, 512:1024], start=True,
                                    stop=True), "pg_rep")
                ldw(k, te, "pg_ldx", i)
                if i >= 2:
                    k.wg(te, "pg_t1", i - 1)
                for dc2 in range(DC):
                    for half in range(2):
                        ins = te.matmul(
                            psW[:, p * 1024 + half * 512:
                                p * 1024 + (half + 1) * 512],
                            xT[p][:, dc2 * 128:(dc2 + 1) * 128],
                            we[:, dc2 * D + half * 512:dc2 * D + (half + 1) * 512],
                            start=(dc2 == 0), stop=(dc2 == DC - 1))
                k.inc(ins, "pg_mm")

        @blk.vector
        def _(v):
            _entry(k, v)
            for i in range(NT):
                p = i % 2
                ch = i // 2
                k.wg(v, "pg_mm", i + 1)
                if i >= 2:
                    k.wg(v, "pg_g1", i - 1)
                k.inc(v.tensor_tensor(t1[p][:], psW[:, p * 1024:(p + 1) * 1024],
                                      b1[:], ALU.add), "pg_t1")
                k.wg(v, "pg_rep", ch + 1)
                ldw(k, v, "pg_ldy", i, n=32)
                if i >= 2:
                    k.wg(v, "pg_out", i - 1)
                k.inc(v.tensor_tensor(s1[p][:], psR[:, 0:1024], yt[p][:], ALU.add),
                      "pg_s1")

        @blk.scalar
        def _(sc):
            _entry(k, sc)
            for i in range(NT):
                p = i % 2
                k.wg(sc, "pg_t1", i + 1)
                if i >= 2:
                    k.wg(sc, "pg_out", i - 1)
                k.inc(sc.activation(g1[p][:], t1[p][:], AF.Silu), "pg_g1")

        @blk.gpsimd
        def _(gp):
            _entry(k, gp)
            for i in range(NT):
                p = i % 2
                k.wg(gp, "pg_g1", i + 1)
                k.wg(gp, "pg_s1", i + 1)
                if i >= 2:
                    k.wg(gp, "pg_out", i - 1)
                e = k.inc(gp.tensor_tensor(t2[p][:], s1[p][:], g1[p][:],
                                           ALU.mult), "pg_e")
                k.wg(gp, "pg_e", e)
                stw(k, gp, "pg_st", i)
                k.inc(gp.tensor_tensor(outb[p][:], t2[p][:], x1t[p][:], ALU.add),
                      "pg_out")
                k.wg(gp, "pg_out", i + 1)
                sti(k, gp, "pg_st", i, d["y_out"][i * 128:(i + 1) * 128, :],
                    outb[p][:])



EMITTERS = {"P0": emit_P0, "PA": emit_PA, "PC": emit_PC, "PB1": emit_PB1,
            "PB2": emit_PB2, "PD": emit_PD, "PE1": emit_PE1, "PF1": emit_PF1,
            "PG1": emit_PG1}


def _group_io(group):
    later = []
    seen = False
    for ph in PHASE_ORDER:
        if ph in group:
            seen = True
            continue
        if seen or ph not in group:
            later.append(ph)
    later = [ph for ph in PHASE_ORDER if ph not in group and
             PHASE_ORDER.index(ph) > max(PHASE_ORDER.index(p) for p in group)]
    later_cons = set()
    for ph in later:
        later_cons |= PHASE_IO[ph][0]
    cons = set()
    prod = set()
    for ph in group:
        cons |= PHASE_IO[ph][0]
        prod |= PHASE_IO[ph][1]
    ext_in = cons - prod
    ext_out = prod & later_cons
    internal = prod - ext_out
    return ext_in, ext_out, internal


def build_program(group, expose_all=False):
    nc = bass.Bass("TRN2", target_bir_lowering=False)
    dram = {}
    for nm, (shape, dt) in REAL_IN.items():
        dram[nm] = nc.dram_tensor(nm, shape, dt, kind="ExternalInput").ap()
    ext_in, ext_out, internal = _group_io(group)
    if expose_all:
        ext_out = ext_out | internal
        internal = set()
    for nm in sorted(ext_in):
        shape, dt = SCRATCH[nm]
        dram[nm] = nc.dram_tensor(nm, shape, dt, kind="ExternalInput").ap()
    for nm in sorted(ext_out):
        shape, dt = SCRATCH[nm]
        dram[nm] = nc.dram_tensor(nm, shape, dt, kind="ExternalOutput").ap()
    for nm in sorted(internal):
        shape, dt = SCRATCH[nm]
        dram[nm] = nc.dram_tensor(nm, shape, dt, kind="Internal").ap()
    if "PG1" in group:
        dram["y_out"] = nc.dram_tensor("y_out", [T, D], F32,
                                       kind="ExternalOutput").ap()
    k = K(nc, dram)
    emit_PINIT(k)
    for ph in group:
        k.entry = {nm: k.cnt[nm] for nm in
                   list(k.store_sems) + ["cld", "cldg", "cinit"]
                   if k.cnt.get(nm, 0)}
        k.phase_begin()
        EMITTERS[ph](k)
    with nc.Block() as blk:
        @blk.sync
        def _(sync):
            for nm in sorted(k.store_sems):
                sync.wait_ge(k.sem(nm), k.cnt[nm])
    k.es.close()
    return nc, sorted(ext_in), sorted(ext_out | ({"y_out"} if "PG1" in group
                                                 else set()))


import os as _os
if _os.environ.get("KM_UNFUSED"):
    PLAN = [("P0",), ("PA",), ("PC",), ("PB1",), ("PB2",), ("PD",), ("PE1",),
            ("PF1",), ("PG1",)]
else:
    PLAN = [tuple(PHASE_ORDER)]

_CACHE = {}


def kernel(**inputs):
    in_maps = _host_prep(inputs)
    pool = [dict(m) for m in in_maps]
    for group in PLAN:
        key = tuple(group)
        if key not in _CACHE:
            _CACHE[key] = build_program(list(group))
        nc, ext_in, ext_out = _CACHE[key]
        need = list(REAL_IN) + [nm for nm in ext_in if nm in SCRATCH]
        maps = [{nm: pool[c][nm] for nm in need} for c in range(NCORES)]
        res = run_bass_kernel_spmd(nc, maps, core_ids=list(range(NCORES)))
        for c in range(NCORES):
            for nm in ext_out:
                pool[c][nm] = np.asarray(res.results[c][nm])
    out = np.concatenate(
        [np.asarray(pool[c]["y_out"], dtype=np.float32).reshape(B_LOC, S, D)
         for c in range(NCORES)], axis=0)
    return out
